# version 01
# baseline (speedup 1.0000x reference)
import numpy as np
import ml_dtypes
from contextlib import ExitStack

import concourse.bass as bass
import concourse.mybir as mybir
from concourse.bass_utils import run_bass_kernel_spmd

F32 = mybir.dt.float32
BF16 = mybir.dt.bfloat16
I32 = mybir.dt.int32
AF = mybir.ActivationFunctionType
ALU = mybir.AluOpType
BF = ml_dtypes.bfloat16

P = 128
D = 1024
TPC = 4096
SEQ = 8192
B = 4
NCORE = 8
LN_EPS = 1e-5


class Sem:

    def __init__(self, ctx, nc, name):
        self.h = ctx.enter_context(nc.semaphore(name))
        self.n = 0

    def inc(self, inst, k=1):
        inst.then_inc(self.h, k)
        self.n += k
        return inst

    def dma(self, inst):
        return self.inc(inst, 16)

    def wait(self, eng):
        if self.n > 0:
            eng.wait_ge(self.h, self.n)


def fm_ap(dram_ap, d_total, t0, nt):
    return dram_ap[:, t0:t0 + nt].rearrange("(kt p) t -> p kt t", p=P)


def emit_ln(nc, S, xc, kt_n, nt, ones_col, ones_row, p_sum, p_sq, p_mu, p_rstd,
            xsq, murow, qrow, mu2, varrow, rstdrow, eps_ap, mu_b, rstd_b,
            gates=("ld",)):
    blk = S["blk"]
    dtot = kt_n * P

    @blk.scalar
    def _(sc):
        for g in gates:
            S[g].wait(sc)
        for kt in range(kt_n):
            inst = sc.activation(xsq[:, kt, :], xc[:, kt, :], AF.Square)
        S["sc"].inc(inst)

    @blk.tensor
    def _(te):
        for g in gates:
            S[g].wait(te)
        for kt in range(kt_n):
            te.matmul(p_sum[:, :], ones_col[:, :], xc[:, kt, :],
                      start=(kt == 0), stop=(kt == kt_n - 1))
        S["sc"].wait(te)
        for kt in range(kt_n):
            inst = te.matmul(p_sq[:, :], ones_col[:, :], xsq[:, kt, :],
                             start=(kt == 0), stop=(kt == kt_n - 1))
        S["te"].inc(inst)

    @blk.scalar
    def _(sc):
        S["te"].wait(sc)
        sc.mul(murow[:, :], p_sum[:, :], 1.0 / dtot)
        inst = sc.mul(qrow[:, :], p_sq[:, :], 1.0 / dtot)
        S["sc"].inc(inst)

    @blk.vector
    def _(ve):
        S["sc"].wait(ve)
        ve.tensor_mul(mu2[:, :], murow[:, :], murow[:, :])
        inst = ve.tensor_sub(varrow[:, :], qrow[:, :], mu2[:, :])
        S["ve"].inc(inst)

    @blk.scalar
    def _(sc):
        S["ve"].wait(sc)
        sc.activation(varrow[:, :], varrow[:, :], AF.Ln, bias=eps_ap[:1, :])
        inst = sc.activation(rstdrow[:, :], varrow[:, :], AF.Exp, scale=-0.5)
        S["sc"].inc(inst)

    @blk.tensor
    def _(te):
        S["sc"].wait(te)
        te.matmul(p_mu[:, :], ones_row[:, :], murow[:, :], start=True, stop=True)
        inst = te.matmul(p_rstd[:, :], ones_row[:, :], rstdrow[:, :], start=True, stop=True)
        S["te"].inc(inst)

    @blk.vector
    def _(ve):
        S["te"].wait(ve)
        ve.tensor_copy(mu_b[:, :], p_mu[:, :])
        inst = ve.tensor_copy(rstd_b[:, :], p_rstd[:, :])
        S["ve"].inc(inst)


def build_launch_A(nt=512):
    DI = 1536
    HG = 2 * DI
    nch = TPC // nt
    nc = bass.Bass("TRN2", target_bir_lowering=False)
    x = nc.dram_tensor("x", [D, TPC], F32, kind="ExternalInput").ap()
    whgT = nc.dram_tensor("whgT", [D, HG], F32, kind="ExternalInput").ap()
    bhg = nc.dram_tensor("bhg", [P, HG // P], F32, kind="ExternalInput").ap()
    a_out = nc.dram_tensor("a_out", [DI, TPC], F32, kind="ExternalOutput").ap()
    b_out = nc.dram_tensor("b_out", [DI, TPC], F32, kind="ExternalOutput").ap()

    with ExitStack() as ctx:
        E = ctx.enter_context
        w_sb = E(nc.sbuf_tensor("w_sb", [P, 8, HG], BF16))
        bhg_sb = E(nc.sbuf_tensor("bhg_sb", [P, 24], F32))
        bhg_neg = E(nc.sbuf_tensor("bhg_neg", [P, 24], F32))
        bhg_half = E(nc.sbuf_tensor("bhg_half", [P, 24], F32))
        xc = E(nc.sbuf_tensor("xc", [P, 8, nt], F32))
        xsq = E(nc.sbuf_tensor("xsq", [P, 8, nt], F32))
        xnb = E(nc.sbuf_tensor("xnb", [P, 8, nt], BF16))
        ones_col = E(nc.sbuf_tensor("ones_col", [P, 1], F32))
        ones_row = E(nc.sbuf_tensor("ones_row", [1, P], F32))
        eps_ap = E(nc.sbuf_tensor("eps_ap", [1, 1], F32))
        murow = E(nc.sbuf_tensor("murow", [1, nt], F32))
        qrow = E(nc.sbuf_tensor("qrow", [1, nt], F32))
        mu2 = E(nc.sbuf_tensor("mu2", [1, nt], F32))
        varrow = E(nc.sbuf_tensor("varrow", [1, nt], F32))
        rstdrow = E(nc.sbuf_tensor("rstdrow", [1, nt], F32))
        mu_b = E(nc.sbuf_tensor("mu_b", [P, nt], F32))
        rstd_b = E(nc.sbuf_tensor("rstd_b", [P, nt], F32))
        sig_g = E(nc.sbuf_tensor("sig_g", [P, nt], F32))
        sig_h = E(nc.sbuf_tensor("sig_h", [P, nt], F32))
        lin_h = E(nc.sbuf_tensor("lin_h", [P, nt], F32))
        mask_h = E(nc.sbuf_tensor("mask_h", [P, nt], mybir.dt.uint8))
        g_h = E(nc.sbuf_tensor("g_h", [P, nt], F32))
        a_st = E(nc.sbuf_tensor("a_st", [P, nt], F32))
        b_st = E(nc.sbuf_tensor("b_st", [P, nt], F32))

        p_sum = E(nc.psum_tensor("p_sum", [1, nt], F32))
        p_sq = E(nc.psum_tensor("p_sq", [1, nt], F32))
        p_mu = E(nc.psum_tensor("p_mu", [P, nt], F32))
        p_rstd = E(nc.psum_tensor("p_rstd", [P, nt], F32))
        p_h = E(nc.psum_tensor("p_h", [P, nt], F32))
        p_g = E(nc.psum_tensor("p_g", [P, nt], F32))

        blk = E(nc.Block())
        S = {
            "blk": blk,
            "ld": Sem(ctx, nc, "ld"), "st": Sem(ctx, nc, "st"),
            "sc": Sem(ctx, nc, "sc"), "ve": Sem(ctx, nc, "ve"),
            "te": Sem(ctx, nc, "te"),
        }

        @blk.gpsimd
        def _(gp):
            S["ld"].dma(gp.dma_start(
                out=w_sb[:, :, :], in_=whgT.rearrange("(kt p) c -> p kt c", p=P)))

        @blk.sync
        def _(sy):
            S["ld"].dma(sy.dma_start(out=bhg_sb[:, :], in_=bhg))

        @blk.vector
        def _(ve):
            ve.memset(ones_col[:, :], 1.0)
            ve.memset(ones_row[:, :], 1.0)
            ve.memset(eps_ap[:, :], LN_EPS)
            S["ld"].wait(ve)
            ve.tensor_scalar_mul(bhg_neg[:, :], bhg_sb[:, :], -1.0)
            inst = ve.tensor_scalar_add(bhg_half[:, :], bhg_sb[:, :], 0.5)
            S["ve"].inc(inst)

        for ci in range(nch):
            t0 = ci * nt

            @blk.sync
            def _(sy):
                S["st"].wait(sy)
                S["ve"].wait(sy)
                S["sc"].wait(sy)
                S["te"].wait(sy)
                S["ld"].dma(sy.dma_start(out=xc[:, :, :], in_=fm_ap(x, D, t0, nt)))

            emit_ln(nc, S, xc, 8, nt, ones_col, ones_row, p_sum, p_sq, p_mu, p_rstd,
                    xsq, murow, qrow, mu2, varrow, rstdrow, eps_ap, mu_b, rstd_b)

            @blk.vector
            def _(ve):
                S["ve"].wait(ve)
                for kt in range(8):
                    ve.tensor_sub(xsq[:, kt, :], xc[:, kt, :], mu_b[:, :])
                    inst = ve.tensor_mul(xnb[:, kt, :], xsq[:, kt, :], rstd_b[:, :])
                S["ve"].inc(inst)

            for j in range(12):

                @blk.tensor
                def _(te, j=j):
                    S["ve"].wait(te)
                    if j == 0:
                        S["sc"].wait(te)
                    for kt in range(8):
                        te.matmul(p_h[:, :], w_sb[:, kt, j * P:(j + 1) * P],
                                  xnb[:, kt, :], start=(kt == 0), stop=(kt == 7))
                    for kt in range(8):
                        inst = te.matmul(p_g[:, :], w_sb[:, kt, (12 + j) * P:(13 + j) * P],
                                         xnb[:, kt, :], start=(kt == 0), stop=(kt == 7))
                    S["te"].inc(inst)

                @blk.scalar
                def _(sc, j=j):
                    S["te"].wait(sc)
                    S["st"].wait(sc)
                    sc.activation(sig_g[:, :], p_g[:, :], AF.Sigmoid,
                                  bias=bhg_sb[:, 12 + j:13 + j])
                    sc.activation(a_st[:, :], p_g[:, :], AF.Sigmoid,
                                  bias=bhg_neg[:, 12 + j:13 + j], scale=-1.0)
                    inst = sc.activation(sig_h[:, :], p_h[:, :], AF.Sigmoid,
                                         bias=bhg_sb[:, j:j + 1])
                    S["sc"].inc(inst)

                @blk.vector
                def _(ve, j=j):
                    S["sc"].wait(ve)
                    ve.tensor_scalar_add(lin_h[:, :], p_h[:, :], bhg_half[:, j:j + 1])
                    ve.tensor_scalar(mask_h[:, :], p_h[:, :],
                                     bhg_neg[:, j:j + 1], None,
                                     op0=ALU.is_ge)
                    ve.select(g_h[:, :], mask_h[:, :], lin_h[:, :], sig_h[:, :])
                    inst = ve.tensor_mul(b_st[:, :], sig_g[:, :], g_h[:, :])
                    S["ve"].inc(inst)

                @blk.sync
                def _(sy, j=j, t0=t0):
                    S["ve"].wait(sy)
                    S["st"].dma(sy.dma_start(
                        out=a_out[j * P:(j + 1) * P, t0:t0 + nt], in_=a_st[:, :]))
                    S["st"].dma(sy.dma_start(
                        out=b_out[j * P:(j + 1) * P, t0:t0 + nt], in_=b_st[:, :]))

        @blk.sync
        def _(sy):
            S["st"].wait(sy)
    return nc




def build_launch_B():
    CH = 768
    NJ = CH // P
    nc = bass.Bass("TRN2", target_bir_lowering=False)
    a_in = nc.dram_tensor("a_in", [CH, SEQ], F32, kind="ExternalInput").ap()
    b_in = nc.dram_tensor("b_in", [CH, SEQ], F32, kind="ExternalInput").ap()
    h0 = nc.dram_tensor("h0", [P, NJ], F32, kind="ExternalInput").ap()
    h_out = nc.dram_tensor("h_out", [CH, SEQ], F32, kind="ExternalOutput").ap()

    with ExitStack() as ctx:
        E = ctx.enter_context
        s_a = E(nc.sbuf_tensor("s_a", [P, SEQ], F32))
        s_b = E(nc.sbuf_tensor("s_b", [P, SEQ], F32))
        s_h = E(nc.sbuf_tensor("s_h", [P, SEQ], F32))
        s_h0 = E(nc.sbuf_tensor("s_h0", [P, NJ], F32))
        s_sig = E(nc.sbuf_tensor("s_sig", [P, NJ], F32))
        s_lin = E(nc.sbuf_tensor("s_lin", [P, NJ], F32))
        s_msk = E(nc.sbuf_tensor("s_msk", [P, NJ], mybir.dt.uint8))
        s_h0g = E(nc.sbuf_tensor("s_h0g", [P, NJ], F32))
        blk = E(nc.Block())
        S = {"blk": blk, "ld": Sem(ctx, nc, "ld"), "st": Sem(ctx, nc, "st"),
             "sc": Sem(ctx, nc, "sc"), "ve": Sem(ctx, nc, "ve")}

        @blk.sync
        def _(sy):
            S["ld"].dma(sy.dma_start(out=s_h0[:, :], in_=h0))

        @blk.scalar
        def _(sc):
            S["ld"].wait(sc)
            inst = sc.activation(s_sig[:, :], s_h0[:, :], AF.Sigmoid)
            S["sc"].inc(inst)

        @blk.vector
        def _(ve):
            S["sc"].wait(ve)
            ve.tensor_scalar_add(s_lin[:, :], s_h0[:, :], 0.5)
            ve.tensor_scalar(s_msk[:, :], s_h0[:, :], 0.0, None, op0=ALU.is_ge)
            inst = ve.select(s_h0g[:, :], s_msk[:, :], s_lin[:, :], s_sig[:, :])
            S["ve"].inc(inst)

        for j in range(NJ):
            @blk.sync
            def _(sy, j=j):
                S["ve"].wait(sy)
                S["st"].wait(sy)
                S["ld"].dma(sy.dma_start(out=s_a[:, :], in_=a_in[j * P:(j + 1) * P, :]))
                S["ld"].dma(sy.dma_start(out=s_b[:, :], in_=b_in[j * P:(j + 1) * P, :]))

            @blk.vector
            def _(ve, j=j):
                S["ld"].wait(ve)
                inst = ve.tensor_tensor_scan(
                    out=s_h[:, :], data0=s_a[:, :], data1=s_b[:, :],
                    initial=s_h0g[:, j:j + 1], op0=ALU.mult, op1=ALU.add)
                S["ve"].inc(inst)

            @blk.sync
            def _(sy, j=j):
                S["ve"].wait(sy)
                S["st"].dma(sy.dma_start(out=h_out[j * P:(j + 1) * P, :], in_=s_h[:, :]))

        @blk.sync
        def _(sy):
            S["st"].wait(sy)
    return nc


def build_launch_C(nt=512, nch=None, stage=3):
    DI = 1536
    if nch is None:
        nch = TPC // nt
    nc = bass.Bass("TRN2", target_bir_lowering=False)
    h_in = nc.dram_tensor("h_in", [DI, TPC], F32, kind="ExternalInput").ap()
    x = nc.dram_tensor("x", [D, TPC], F32, kind="ExternalInput").ap()
    woutT = nc.dram_tensor("woutT", [DI, D], F32, kind="ExternalInput").ap()
    bout = nc.dram_tensor("bout", [P, 8], F32, kind="ExternalInput").ap()
    lng = nc.dram_tensor("lng", [P, 8], F32, kind="ExternalInput").ap()
    lnb = nc.dram_tensor("lnb", [P, 8], F32, kind="ExternalInput").ap()
    centT = nc.dram_tensor("centT", [D, 8], F32, kind="ExternalInput").ap()
    x2_o = nc.dram_tensor("x2_o", [D, TPC], F32, kind="ExternalOutput").ap()
    feat_o = nc.dram_tensor("feat_o", [D, TPC], F32, kind="ExternalOutput").ap()
    norm_o = nc.dram_tensor("norm_o", [D, TPC], F32, kind="ExternalOutput").ap()
    normbf_o = nc.dram_tensor("normbf_o", [D, TPC], BF16, kind="ExternalOutput").ap()
    sc_o = nc.dram_tensor("sc_o", [8, TPC], F32, kind="ExternalOutput").ap()

    with ExitStack() as ctx:
        E = ctx.enter_context
        w_sb = E(nc.sbuf_tensor("w_sb", [P, 12, D], BF16))
        cent_sb = E(nc.sbuf_tensor("cent_sb", [P, 8, 8], F32))
        bout_sb = E(nc.sbuf_tensor("bout_sb", [P, 8], F32))
        lng_sb = E(nc.sbuf_tensor("lng_sb", [P, 8], F32))
        lnb_sb = E(nc.sbuf_tensor("lnb_sb", [P, 8], F32))
        hc = E(nc.sbuf_tensor("hc", [P, 12, nt], BF16))
        xc = E(nc.sbuf_tensor("xc", [P, 8, nt], F32))
        x2c = E(nc.sbuf_tensor("x2c", [P, 8, nt], F32))
        featc = E(nc.sbuf_tensor("featc", [P, 8, nt], F32))
        normc = E(nc.sbuf_tensor("normc", [P, 8, nt], F32))
        normb = E(nc.sbuf_tensor("normb", [P, 8, nt], BF16))
        xsq = E(nc.sbuf_tensor("xsq", [P, 8, nt], F32))
        ones_col = E(nc.sbuf_tensor("ones_col", [P, 1], F32))
        ones_row = E(nc.sbuf_tensor("ones_row", [1, P], F32))
        eps_ap = E(nc.sbuf_tensor("eps_ap", [1, 1], F32))
        murow = E(nc.sbuf_tensor("murow", [1, nt], F32))
        qrow = E(nc.sbuf_tensor("qrow", [1, nt], F32))
        mu2 = E(nc.sbuf_tensor("mu2", [1, nt], F32))
        varrow = E(nc.sbuf_tensor("varrow", [1, nt], F32))
        rstdrow = E(nc.sbuf_tensor("rstdrow", [1, nt], F32))
        mu_b = E(nc.sbuf_tensor("mu_b", [P, nt], F32))
        rstd_b = E(nc.sbuf_tensor("rstd_b", [P, nt], F32))
        sc_st = E(nc.sbuf_tensor("sc_st", [8, nt], F32))

        p_y = E(nc.psum_tensor("p_y", [P, nt], F32))
        p_sum = E(nc.psum_tensor("p_sum", [1, nt], F32))
        p_sq = E(nc.psum_tensor("p_sq", [1, nt], F32))
        p_mu = E(nc.psum_tensor("p_mu", [P, nt], F32))
        p_rstd = E(nc.psum_tensor("p_rstd", [P, nt], F32))
        p_sc = E(nc.psum_tensor("p_sc", [8, nt], F32))

        blk = E(nc.Block())
        S = {"blk": blk, "ld": Sem(ctx, nc, "ld"), "st": Sem(ctx, nc, "st"),
             "sc": Sem(ctx, nc, "sc"), "ve": Sem(ctx, nc, "ve"),
             "te": Sem(ctx, nc, "te")}

        @blk.gpsimd
        def _(gp):
            S["ld"].dma(gp.dma_start(
                out=w_sb[:, :, :], in_=woutT.rearrange("(ct p) d -> p ct d", p=P)))

        @blk.sync
        def _(sy):
            S["ld"].dma(sy.dma_start(
                out=cent_sb[:, :, :], in_=centT.rearrange("(kt p) e -> p kt e", p=P)))
            S["ld"].dma(sy.dma_start(out=bout_sb[:, :], in_=bout))
            S["ld"].dma(sy.dma_start(out=lng_sb[:, :], in_=lng))
            S["ld"].dma(sy.dma_start(out=lnb_sb[:, :], in_=lnb))

        @blk.vector
        def _(ve):
            ve.memset(ones_col[:, :], 1.0)
            ve.memset(ones_row[:, :], 1.0)
            inst = ve.memset(eps_ap[:, :], LN_EPS)
            S["ve"].inc(inst)

        for ci in range(nch):
            t0 = ci * nt

            @blk.sync
            def _(sy):
                S["st"].wait(sy)
                S["ve"].wait(sy)
                S["sc"].wait(sy)
                S["te"].wait(sy)
                S["ld"].dma(sy.dma_start(out=xc[:, :, :], in_=fm_ap(x, D, t0, nt)))

            @blk.gpsimd
            def _(gp, t0=t0):
                S["st"].wait(gp)
                S["ve"].wait(gp)
                S["ld"].dma(gp.dma_start(
                    out=hc[:, :, :],
                    in_=h_in[:, t0:t0 + nt].rearrange("(ct p) t -> p ct t", p=P)))

            for dt in range(8):
                @blk.tensor
                def _(te, dt=dt):
                    S["ld"].wait(te)
                    S["ve"].wait(te)
                    for ctj in range(12):
                        inst = te.matmul(p_y[:, :], w_sb[:, ctj, dt * P:(dt + 1) * P],
                                         hc[:, ctj, :], start=(ctj == 0), stop=(ctj == 11))
                    S["te"].inc(inst)

                @blk.vector
                def _(ve, dt=dt):
                    S["te"].wait(ve)
                    inst = ve.scalar_tensor_tensor(
                        out=x2c[:, dt, :], in0=p_y[:, :], scalar=bout_sb[:, dt:dt + 1],
                        in1=xc[:, dt, :], op0=ALU.add, op1=ALU.add)
                    S["ve"].inc(inst)

            if stage < 1:
                @blk.sync
                def _(sy, t0=t0):
                    S["ve"].wait(sy)
                    S["st"].dma(sy.dma_start(out=fm_ap(x2_o, D, t0, nt), in_=x2c[:, :, :]))
                continue

            emit_ln(nc, S, x2c, 8, nt, ones_col, ones_row, p_sum, p_sq, p_mu, p_rstd,
                    xsq, murow, qrow, mu2, varrow, rstdrow, eps_ap, mu_b, rstd_b,
                    gates=("ve",))

            @blk.vector
            def _(ve):
                S["ve"].wait(ve)
                for kt in range(8):
                    ve.tensor_sub(xsq[:, kt, :], x2c[:, kt, :], mu_b[:, :])
                    inst = ve.tensor_mul(featc[:, kt, :], xsq[:, kt, :], rstd_b[:, :])
                S["ve"].inc(inst)

            if stage < 2:
                @blk.sync
                def _(sy, t0=t0):
                    S["ve"].wait(sy)
                    S["st"].dma(sy.dma_start(out=fm_ap(x2_o, D, t0, nt), in_=x2c[:, :, :]))
                    S["st"].dma(sy.dma_start(out=fm_ap(feat_o, D, t0, nt), in_=featc[:, :, :]))
                continue

            emit_ln(nc, S, featc, 8, nt, ones_col, ones_row, p_sum, p_sq, p_mu, p_rstd,
                    xsq, murow, qrow, mu2, varrow, rstdrow, eps_ap, mu_b, rstd_b,
                    gates=("ve",))

            @blk.vector
            def _(ve):
                S["ve"].wait(ve)
                for kt in range(8):
                    ve.tensor_sub(xsq[:, kt, :], featc[:, kt, :], mu_b[:, :])
                    ve.tensor_mul(xsq[:, kt, :], xsq[:, kt, :], rstd_b[:, :])
                    ve.tensor_scalar(
                        normc[:, kt, :], xsq[:, kt, :],
                        lng_sb[:, kt:kt + 1], lnb_sb[:, kt:kt + 1],
                        op0=ALU.mult, op1=ALU.add)
                    inst = ve.tensor_copy(normb[:, kt, :], normc[:, kt, :])
                S["ve"].inc(inst)

            @blk.tensor
            def _(te):
                S["ve"].wait(te)
                S["sc"].wait(te)
                for kt in range(8):
                    inst = te.matmul(p_sc[:, :], cent_sb[:, kt, :], normc[:, kt, :],
                                     start=(kt == 0), stop=(kt == 7))
                S["te"].inc(inst)

            @blk.scalar
            def _(sc):
                S["te"].wait(sc)
                inst = sc.copy(sc_st[:, :], p_sc[:, :])
                S["sc"].inc(inst)

            @blk.sync
            def _(sy, t0=t0):
                S["ve"].wait(sy)
                S["sc"].wait(sy)
                S["st"].dma(sy.dma_start(out=fm_ap(x2_o, D, t0, nt), in_=x2c[:, :, :]))
                S["st"].dma(sy.dma_start(out=fm_ap(feat_o, D, t0, nt), in_=featc[:, :, :]))
                S["st"].dma(sy.dma_start(out=fm_ap(norm_o, D, t0, nt), in_=normc[:, :, :]))
                S["st"].dma(sy.dma_start(out=fm_ap(normbf_o, D, t0, nt), in_=normb[:, :, :]))
                S["st"].dma(sy.dma_start(out=sc_o[:, t0:t0 + nt], in_=sc_st[:, :]))

        @blk.sync
        def _(sy):
            S["st"].wait(sy)
    return nc


def build_launch_D(niter=40):
    T = 32768
    FE = T // P
    CAP = 4096.0
    NEG = -3.0e38
    nc = bass.Bass("TRN2", target_bir_lowering=False)
    sc_in = nc.dram_tensor("sc_in", [8, T], F32, kind="ExternalInput").ap()
    triu = nc.dram_tensor("triu", [P, P], F32, kind="ExternalInput").ap()
    iota_f = nc.dram_tensor("iota_f", [P, FE], F32, kind="ExternalInput").ap()
    iota_i = nc.dram_tensor("iota_i", [P, FE], I32, kind="ExternalInput").ap()
    t2s_o = nc.dram_tensor("t2s_o", [P, FE], I32, kind="ExternalOutput").ap()
    s2t_o = nc.dram_tensor("s2t_o", [T, 1], I32, kind="ExternalOutput").ap()
    ident = nc.dram_tensor("ident", [P, P], F32, kind="ExternalInput").ap()
    bmax_d = nc.dram_tensor("bmax_d", [P, 1], F32, kind="Internal").ap()
    bmin_d = nc.dram_tensor("bmin_d", [P, 1], F32, kind="Internal").ap()
    m_out = nc.dram_tensor("m_out", [P, 8 * FE], F32, kind="ExternalOutput").ap()
    dbg_o = nc.dram_tensor("dbg_o", [P, 40], F32, kind="ExternalOutput").ap()

    with ExitStack() as ctx:
        E = ctx.enter_context
        s_sc = E(nc.sbuf_tensor("s_sc", [P, 8, FE], F32))
        s_m = E(nc.sbuf_tensor("s_m", [P, 8, FE], F32))
        s_cur = E(nc.sbuf_tensor("s_cur", [P, FE], F32))
        s_taken = E(nc.sbuf_tensor("s_taken", [P, FE], mybir.dt.uint8))
        s_tkf = E(nc.sbuf_tensor("s_tkf", [P, FE], F32))
        s_id = E(nc.sbuf_tensor("s_id", [P, P], F32))
        s_neg = E(nc.sbuf_tensor("s_neg", [P, FE], F32))
        s_bounds = E(nc.sbuf_tensor("s_bounds", [P, 2], F32))
        junk = E(nc.sbuf_tensor("junk", [P, FE], F32))
        cnt = E(nc.sbuf_tensor("cnt", [P, 1], F32))
        tot_s = E(nc.sbuf_tensor("tot_s", [1, 1], F32))
        lo = E(nc.sbuf_tensor("lo", [P, 1], F32))
        hi = E(nc.sbuf_tensor("hi", [P, 1], F32))
        mid = E(nc.sbuf_tensor("mid", [P, 1], F32))
        lo_n = E(nc.sbuf_tensor("lo_n", [P, 1], F32))
        mid_ptr = E(nc.sbuf_tensor("mid_ptr", [P, 1], F32))
        hi_n = E(nc.sbuf_tensor("hi_n", [P, 1], F32))
        geb = E(nc.sbuf_tensor("geb", [P, 1], mybir.dt.uint8))
        ones_col = E(nc.sbuf_tensor("ones_col", [P, 1], F32))
        ones_row = E(nc.sbuf_tensor("ones_row", [1, P], F32))
        rowred = E(nc.sbuf_tensor("rowred", [P, 2], F32))
        glob = E(nc.sbuf_tensor("glob", [1, 2], F32))
        brow = E(nc.sbuf_tensor("brow", [1, 2 * P], F32))
        glob2 = E(nc.sbuf_tensor("glob2", [1, 1], F32))
        dbg_sb = E(nc.sbuf_tensor("dbg_sb", [P, 40], F32))
        s_triu = E(nc.sbuf_tensor("s_triu", [P, P], F32))
        s_iota = E(nc.sbuf_tensor("s_iota", [P, FE], F32))
        s_iota_i = E(nc.sbuf_tensor("s_iota_i", [P, FE], I32))
        s_one256 = E(nc.sbuf_tensor("s_one256", [P, FE], F32))
        s_cums = E(nc.sbuf_tensor("s_cums", [P, FE], F32))
        s_rowtot = E(nc.sbuf_tensor("s_rowtot", [P, 1], F32))
        s_pre = E(nc.sbuf_tensor("s_pre", [P, 1], F32))
        s_t2s = E(nc.sbuf_tensor("s_t2s", [P, FE], F32))
        s_t2si = E(nc.sbuf_tensor("s_t2si", [P, FE], I32))
        s_tmp = E(nc.sbuf_tensor("s_tmp", [P, FE], F32))
        p_pre = E(nc.psum_tensor("p_pre", [P, 1], F32))
        p_row = E(nc.psum_tensor("p_row", [1, 256], F32))
        p_bc = E(nc.psum_tensor("p_bc", [P, 2], F32))
        p_cntb = E(nc.psum_tensor("p_cntb", [P, 1], F32))

        blk = E(nc.Block())
        S = {"blk": blk, "ld": Sem(ctx, nc, "ld"), "st": Sem(ctx, nc, "st"),
             "ve": Sem(ctx, nc, "ve"), "te": Sem(ctx, nc, "te"),
             "sc": Sem(ctx, nc, "sc"), "vv": Sem(ctx, nc, "vv")}

        @blk.sync
        def _(sy):
            S["ld"].dma(sy.dma_start(
                out=s_sc[:, :, :], in_=sc_in.rearrange("e (p f) -> p e f", p=P)))
            S["ld"].dma(sy.dma_start(out=s_id[:, :], in_=ident))
            S["ld"].dma(sy.dma_start(out=s_triu[:, :], in_=triu))
            S["ld"].dma(sy.dma_start(out=s_iota[:, :], in_=iota_f))
            S["ld"].dma(sy.dma_start(out=s_iota_i[:, :], in_=iota_i))

        @blk.vector
        def _(ve):
            ve.memset(ones_col[:, :], 1.0)
            ve.memset(ones_row[:, :], 1.0)
            ve.memset(s_taken[:, :], 0)
            ve.memset(s_tkf[:, :], 0.0)
            ve.memset(s_neg[:, :], NEG)
            S["ld"].wait(ve)
            flat = s_sc[:, :, :].rearrange("p e f -> p (e f)")
            ve.tensor_reduce(rowred[:, 0:1], flat, mybir.AxisListType.X, ALU.max)
            inst = ve.tensor_reduce(rowred[:, 1:2], flat, mybir.AxisListType.X, ALU.min)
            S["ve"].inc(inst)

        @blk.vector
        def _(ve):
            S["ve"].wait(ve)
            inst = ve.tensor_scalar_mul(rowred[:, 1:2], rowred[:, 1:2], -1.0)
            S["ve"].inc(inst)

        @blk.sync
        def _(sy):
            S["ve"].wait(sy)
            S["st"].dma(sy.dma_start(out=bmax_d, in_=rowred[:, 0:1]))
            S["st"].dma(sy.dma_start(out=bmin_d, in_=rowred[:, 1:2]))

        @blk.sync
        def _(sy):
            S["st"].wait(sy)
            S["ld"].dma(sy.dma_start(out=brow[0:1, 0:P], in_=bmax_d.rearrange("p one -> (p one)")[None, :]))
            S["ld"].dma(sy.dma_start(out=brow[0:1, P:2 * P], in_=bmin_d.rearrange("p one -> (p one)")[None, :]))

        @blk.vector
        def _(ve):
            S["ld"].wait(ve)
            ve.tensor_reduce(glob[0:1, 0:1], brow[0:1, 0:P], mybir.AxisListType.X, ALU.max)
            inst = ve.tensor_reduce(glob2[0:1, 0:1], brow[0:1, P:2 * P], mybir.AxisListType.X, ALU.max)
            S["ve"].inc(inst)

        @blk.tensor
        def _(te):
            S["ve"].wait(te)
            te.matmul(p_bc[:, 0:1], ones_row[:, :], glob[0:1, 0:1], start=True, stop=True)
            inst = te.matmul(p_bc[:, 1:2], ones_row[:, :], glob2[0:1, 0:1], start=True, stop=True)
            S["te"].inc(inst)

        @blk.vector
        def _(ve):
            S["te"].wait(ve)
            ve.tensor_scalar_add(s_bounds[:, 0:1], p_bc[:, 0:1], 1.0)
            ve.tensor_scalar(s_bounds[:, 1:2], p_bc[:, 1:2], -1.0, -1.0,
                             op0=ALU.mult, op1=ALU.add)
            inst = ve.tensor_copy(dbg_sb[:, 32:34], s_bounds[:, :])
            S["ve"].inc(inst)

        for e in range(8):
            @blk.vector
            def _(ve, e=e):
                S["ve"].wait(ve)
                ve.select(s_cur[:, :], s_taken[:, :], s_neg[:, :], s_sc[:, e, :])
                ve.tensor_copy(hi[:, :], s_bounds[:, 0:1])
                inst = ve.tensor_copy(lo[:, :], s_bounds[:, 1:2])
                S["ve"].inc(inst)

            for it in range(niter):
                @blk.vector
                def _(ve):
                    S["ve"].wait(ve)
                    S["te"].wait(ve)
                    S["vv"].inc(ve.tensor_add(mid[:, :], lo[:, :], hi[:, :]))
                    S["vv"].wait(ve)
                    S["vv"].inc(ve.tensor_scalar_mul(mid[:, :], mid[:, :], 0.5))
                    S["vv"].wait(ve)
                    inst = ve.tensor_scalar(junk[:, :], s_cur[:, :], mid[:, :], None,
                                            op0=ALU.is_gt)
                    S["ve"].inc(inst)

                @blk.tensor
                def _(te):
                    S["ve"].wait(te)
                    inst = te.matmul(p_row[:, :], ones_col[:, :], junk[:, :],
                                     start=True, stop=True)
                    S["te"].inc(inst)

                @blk.vector
                def _(ve):
                    S["te"].wait(ve)
                    inst = ve.tensor_reduce(tot_s[:, :], p_row[:, :],
                                            mybir.AxisListType.X, ALU.add)
                    S["ve"].inc(inst)

                @blk.tensor
                def _(te):
                    S["ve"].wait(te)
                    inst = te.matmul(p_cntb[:, :], ones_row[:, :], tot_s[:, :],
                                     start=True, stop=True)
                    S["te"].inc(inst)

                @blk.vector
                def _(ve):
                    S["te"].wait(ve)
                    S["vv"].inc(ve.tensor_scalar(geb[:, :], p_cntb[:, :], CAP, None,
                                                 op0=ALU.is_ge))
                    S["vv"].wait(ve)
                    ve.select(lo_n[:, :], geb[:, :], mid[:, :], lo[:, :])
                    S["vv"].inc(ve.select(hi_n[:, :], geb[:, :], hi[:, :], mid[:, :]))
                    S["vv"].wait(ve)
                    ve.tensor_copy(lo[:, :], lo_n[:, :])
                    inst = ve.tensor_copy(hi[:, :], hi_n[:, :])
                    S["ve"].inc(inst)

            @blk.vector
            def _(ve, e=e):
                S["ve"].wait(ve)
                ve.tensor_copy(dbg_sb[:, 3 * e:3 * e + 1], lo[:, :])
                ve.tensor_copy(dbg_sb[:, 3 * e + 1:3 * e + 2], hi[:, :])
                ve.tensor_copy(dbg_sb[:, 3 * e + 2:3 * e + 3], p_cntb[:, :])
                ve.tensor_scalar(s_m[:, e, :], s_cur[:, :], lo[:, :], None, op0=ALU.is_gt)
                ve.tensor_add(s_tkf[:, :], s_tkf[:, :], s_m[:, e, :])
                inst = ve.tensor_scalar(s_taken[:, :], s_tkf[:, :], 0.5, None, op0=ALU.is_gt)
                S["ve"].inc(inst)

        @blk.vector
        def _(ve):
            S["ve"].wait(ve)
            ve.memset(s_one256[:, :], 1.0)
            inst = ve.memset(s_t2s[:, :], 0.0)
            S["ve"].inc(inst)

        for e in range(8):
            @blk.vector
            def _(ve, e=e):
                S["ve"].wait(ve)
                S["te"].wait(ve)
                S["vv"].inc(ve.tensor_tensor_scan(
                    out=s_cums[:, :], data0=s_one256[:, :], data1=s_m[:, e, :],
                    initial=0.0, op0=ALU.mult, op1=ALU.add))
                S["vv"].wait(ve)
                inst = ve.tensor_copy(s_rowtot[:, :], s_cums[:, FE - 1:FE])
                S["ve"].inc(inst)

            @blk.tensor
            def _(te):
                S["ve"].wait(te)
                inst = te.matmul(p_pre[:, :], s_triu[:, :], s_rowtot[:, :],
                                 start=True, stop=True)
                S["te"].inc(inst)

            @blk.vector
            def _(ve, e=e):
                S["te"].wait(ve)
                S["vv"].inc(ve.tensor_copy(s_pre[:, :], p_pre[:, :]))
                S["vv"].wait(ve)
                ve.tensor_scalar_add(s_tmp[:, :], s_cums[:, :], float(4096 * e - 1))
                S["vv"].inc(ve.tensor_scalar_add(s_tmp[:, :], s_tmp[:, :], s_pre[:, :]))
                S["vv"].wait(ve)
                ve.tensor_mul(s_tmp[:, :], s_tmp[:, :], s_m[:, e, :])
                inst = ve.tensor_add(s_t2s[:, :], s_t2s[:, :], s_tmp[:, :])
                S["ve"].inc(inst)

        @blk.vector
        def _(ve):
            S["ve"].wait(ve)
            inst = ve.tensor_copy(s_t2si[:, :], s_t2s[:, :])
            S["ve"].inc(inst)

        @blk.sync
        def _(sy):
            S["ve"].wait(sy)
            S["st"].dma(sy.dma_start(
                out=m_out.rearrange("p (e f) -> p e f", e=8), in_=s_m[:, :, :]))
            S["st"].dma(sy.dma_start(out=dbg_o, in_=dbg_sb[:, :]))
            S["st"].dma(sy.dma_start(out=t2s_o, in_=s_t2si[:, :]))

        @blk.gpsimd
        def _(gp):
            S["ve"].wait(gp)
            from concourse.bass_core_rust import IndirectOffsetOnAxis
            for f in range(FE):
                S["st"].dma(gp.indirect_dma_start(
                    out=s2t_o,
                    out_offset=IndirectOffsetOnAxis(ap=s_t2si[:, f:f + 1], axis=0),
                    in_=s_iota_i[:, f:f + 1], in_offset=None))

        @blk.sync
        def _(sy):
            sy.wait_ge(S["st"].h, S["st"].n)
    return nc


def _pair_core(c):
    return c // 2, c % 2


def kernel(**inputs):
    raise NotImplementedError("assembled incrementally; see tests")


N_FFT = 16384
R_F = 128


def fft_host_consts():
    k = np.arange(R_F)
    aa, bb = np.meshgrid(k, k, indexing="ij")
    w128 = np.exp(-2j * np.pi * aa * bb / R_F)
    bgrid, kgrid = np.meshgrid(k, k, indexing="ij")
    tw = np.exp(-2j * np.pi * bgrid * kgrid / N_FFT)
    twc = np.conj(tw) / N_FFT
    w128c = np.conj(w128)

    def bf(x):
        return np.ascontiguousarray(x.astype(np.float32)).astype(BF)

    t4 = np.tile(tw, (1, 4))
    tc4 = np.tile(twc, (1, 4))
    return {
        "f_re": bf(w128.real), "f_im": bf(w128.imag), "f_imN": bf(-w128.imag),
        "fc_re": bf(w128c.real), "fc_im": bf(w128c.imag), "fc_imN": bf(-w128c.imag),
        "t_re": bf(t4.real), "t_im": bf(t4.imag),
        "tc_re": bf(tc4.real), "tc_im": bf(tc4.imag),
    }


def emit_fft4(S, te_c, ve_c, xre, xim, C, ps, sb, out_re, out_im, zero_pad=True):
    blk = S["blk"]
    KA = 64 if zero_pad else 128

    @blk.tensor
    def _(te):
        te_c(te)
        for c in range(4):
            sl = slice(128 * c, 128 * (c + 1))
            te.matmul(ps["pA_re"][:, sl], xre[:, c, :], C["f_re"][0:KA, :], start=True, stop=False)
            te.matmul(ps["pA_re"][:, sl], xim[:, c, :], C["f_imN"][0:KA, :], start=False, stop=True)
            te.matmul(ps["pA_im"][:, sl], xre[:, c, :], C["f_im"][0:KA, :], start=True, stop=False)
            inst = te.matmul(ps["pA_im"][:, sl], xim[:, c, :], C["f_re"][0:KA, :], start=False, stop=True)
        S["te"].inc(inst)

    @blk.vector
    def _(ve):
        S["te"].wait(ve)
        ve.tensor_copy(sb["d_re"][:, :], ps["pA_re"][:, :])
        ve.tensor_copy(sb["d_im"][:, :], ps["pA_im"][:, :])
        ve.tensor_mul(sb["e_re"][:, :], sb["d_re"][:, :], C["t_re"][:, :])
        ve.tensor_mul(sb["scr"][:, :], sb["d_im"][:, :], C["t_im"][:, :])
        ve.tensor_sub(sb["e_re"][:, :], sb["e_re"][:, :], sb["scr"][:, :])
        ve.tensor_mul(sb["e_im"][:, :], sb["d_re"][:, :], C["t_im"][:, :])
        ve.tensor_mul(sb["scr"][:, :], sb["d_im"][:, :], C["t_re"][:, :])
        inst = ve.tensor_add(sb["e_im"][:, :], sb["e_im"][:, :], sb["scr"][:, :])
        S["ve"].inc(inst)

    @blk.tensor
    def _(te):
        S["ve"].wait(te)
        te.matmul(ps["pB_re"][:, :], C["f_re"][:, :], sb["e_re"][:, :], start=True, stop=False)
        te.matmul(ps["pB_re"][:, :], C["f_imN"][:, :], sb["e_im"][:, :], start=False, stop=True)
        te.matmul(ps["pB_im"][:, :], C["f_im"][:, :], sb["e_re"][:, :], start=True, stop=False)
        inst = te.matmul(ps["pB_im"][:, :], C["f_re"][:, :], sb["e_im"][:, :], start=False, stop=True)
        S["te"].inc(inst)

    @blk.vector
    def _(ve):
        S["te"].wait(ve)
        ve.tensor_copy(out_re.rearrange("p c k -> p (c k)"), ps["pB_re"][:, :])
        inst = ve.tensor_copy(out_im.rearrange("p c k -> p (c k)"), ps["pB_im"][:, :])
        S["ve"].inc(inst)


def emit_ifft4(S, pr_re, pr_im, C, ps, sb, y_re, y_im):
    blk = S["blk"]

    @blk.tensor
    def _(te):
        S["ve"].wait(te)
        for c in range(4):
            sl = slice(128 * c, 128 * (c + 1))
            te.matmul(ps["pA_re"][:, sl], pr_re[:, c, :], C["fc_re"][:, :], start=True, stop=False)
            te.matmul(ps["pA_re"][:, sl], pr_im[:, c, :], C["fc_imN"][:, :], start=False, stop=True)
            te.matmul(ps["pA_im"][:, sl], pr_re[:, c, :], C["fc_im"][:, :], start=True, stop=False)
            inst = te.matmul(ps["pA_im"][:, sl], pr_im[:, c, :], C["fc_re"][:, :], start=False, stop=True)
        S["te"].inc(inst)

    @blk.vector
    def _(ve):
        S["te"].wait(ve)
        ve.tensor_copy(sb["d_re"][:, :], ps["pA_re"][:, :])
        ve.tensor_copy(sb["d_im"][:, :], ps["pA_im"][:, :])
        ve.tensor_mul(sb["e_re"][:, :], sb["d_re"][:, :], C["tc_re"][:, :])
        ve.tensor_mul(sb["scr"][:, :], sb["d_im"][:, :], C["tc_im"][:, :])
        ve.tensor_sub(sb["e_re"][:, :], sb["e_re"][:, :], sb["scr"][:, :])
        ve.tensor_mul(sb["e_im"][:, :], sb["d_re"][:, :], C["tc_im"][:, :])
        ve.tensor_mul(sb["scr"][:, :], sb["d_im"][:, :], C["tc_re"][:, :])
        inst = ve.tensor_add(sb["e_im"][:, :], sb["e_im"][:, :], sb["scr"][:, :])
        S["ve"].inc(inst)

    @blk.tensor
    def _(te):
        S["ve"].wait(te)
        te.matmul(ps["pB_re"][:, :], C["fc_re"][:, :], sb["e_re"][:, :], start=True, stop=False)
        te.matmul(ps["pB_re"][:, :], C["fc_imN"][:, :], sb["e_im"][:, :], start=False, stop=True)
        te.matmul(ps["pB_im"][:, :], C["fc_im"][:, :], sb["e_re"][:, :], start=True, stop=False)
        inst = te.matmul(ps["pB_im"][:, :], C["fc_re"][:, :], sb["e_im"][:, :], start=False, stop=True)
        S["te"].inc(inst)

    @blk.vector
    def _(ve):
        S["te"].wait(ve)
        ve.tensor_copy(y_re.rearrange("p c k -> p (c k)"), ps["pB_re"][0:64, :])
        inst = ve.tensor_copy(y_im.rearrange("p c k -> p (c k)"), ps["pB_im"][0:64, :])
        S["ve"].inc(inst)


def emit_cmul4(S, a_re, a_im, b_re, b_im, o_re, o_im, scr):
    blk = S["blk"]

    @blk.vector
    def _(ve):
        S["ve"].wait(ve)
        ar = a_re.rearrange("p c k -> p (c k)"); ai = a_im.rearrange("p c k -> p (c k)")
        br = b_re.rearrange("p c k -> p (c k)"); bi = b_im.rearrange("p c k -> p (c k)")
        orr = o_re.rearrange("p c k -> p (c k)"); oi = o_im.rearrange("p c k -> p (c k)")
        ve.tensor_mul(orr, ar, br)
        ve.tensor_mul(scr[:, :], ai, bi)
        ve.tensor_sub(orr, orr, scr[:, :])
        ve.tensor_mul(oi, ar, bi)
        ve.tensor_mul(scr[:, :], ai, br)
        inst = ve.tensor_add(oi, oi, scr[:, :])
        S["ve"].inc(inst)


def build_launch_E(nt=512):
    F = 4096
    nch = TPC // nt
    nc = bass.Bass("TRN2", target_bir_lowering=False)
    normG = nc.dram_tensor("normG", [8 * TPC, D], BF16, kind="ExternalInput").ap()
    s2t = nc.dram_tensor("s2t", [P, 32], I32, kind="ExternalInput").ap()
    w1e = nc.dram_tensor("w1e", [D, F], F32, kind="ExternalInput").ap()
    b1p = nc.dram_tensor("b1p", [P, 32], F32, kind="ExternalInput").ap()
    w2e = nc.dram_tensor("w2e", [F, D], F32, kind="ExternalInput").ap()
    b2p = nc.dram_tensor("b2p", [P, 8], F32, kind="ExternalInput").ap()
    stag = nc.dram_tensor("stag", [TPC, D], BF16, kind="Internal").ap()
    slab = nc.dram_tensor("slab", [D, TPC], BF16, kind="ExternalOutput").ap()

    from concourse.bass_core_rust import IndirectOffsetOnAxis
    with ExitStack() as ctx:
        E = ctx.enter_context
        w1_sb = E(nc.sbuf_tensor("w1_sb", [P, 8, F], BF16))
        w2_sb = E(nc.sbuf_tensor("w2_sb", [P, 32, D], BF16))
        b1_sb = E(nc.sbuf_tensor("b1_sb", [P, 32], F32))
        b2_sb = E(nc.sbuf_tensor("b2_sb", [P, 8], F32))
        idx_sb = E(nc.sbuf_tensor("idx_sb", [P, 32], I32))
        g_sb = E(nc.sbuf_tensor("g_sb", [P, D], BF16))
        nrm_c = E(nc.sbuf_tensor("nrm_c", [P, 8, nt], BF16))
        h_bf = E(nc.sbuf_tensor("h_bf", [P, 32, nt], BF16))
        o_bf = E(nc.sbuf_tensor("o_bf", [P, nt], BF16))
        p_h = E(nc.psum_tensor("p_h", [P, nt], F32))
        p_o = E(nc.psum_tensor("p_o", [P, nt], F32))
        blk = E(nc.Block())
        S = {"blk": blk, "ld": Sem(ctx, nc, "ld"), "st": Sem(ctx, nc, "st"),
             "sc": Sem(ctx, nc, "sc"), "ve": Sem(ctx, nc, "ve"),
             "te": Sem(ctx, nc, "te"), "gi": Sem(ctx, nc, "gi"),
             "tp": Sem(ctx, nc, "tp")}

        @blk.gpsimd
        def _(gp):
            S["ld"].dma(gp.dma_start(
                out=w1_sb[:, :, :], in_=w1e.rearrange("(kt p) f -> p kt f", p=P)))
            S["ld"].dma(gp.dma_start(
                out=w2_sb[:, :, :], in_=w2e.rearrange("(ft p) d -> p ft d", p=P)))

        @blk.sync
        def _(sy):
            S["ld"].dma(sy.dma_start(out=b1_sb[:, :], in_=b1p))
            S["ld"].dma(sy.dma_start(out=b2_sb[:, :], in_=b2p))
            S["ld"].dma(sy.dma_start(out=idx_sb[:, :], in_=s2t))

        for r in range(32):
            @blk.gpsimd
            def _(gp, r=r):
                S["ld"].wait(gp)
                S["st"].wait(gp)
                S["gi"].dma(gp.indirect_dma_start(
                    out=g_sb[:, :], out_offset=None,
                    in_=normG, in_offset=IndirectOffsetOnAxis(ap=idx_sb[:, r:r + 1], axis=0)))

            @blk.sync
            def _(sy, r=r):
                S["gi"].wait(sy)
                S["st"].dma(sy.dma_start(out=stag[r * P:(r + 1) * P, :], in_=g_sb[:, :]))

        for ci in range(nch):
            t0 = ci * nt

            @blk.scalar
            def _(sc, t0=t0):
                S["gi"].wait(sc)
                S["te"].wait(sc)
                S["ve"].wait(sc)
                for dt in range(8):
                    S["tp"].dma(sc.dma_start_transpose(
                        out=nrm_c[:, dt, :], in_=stag[t0:t0 + nt, dt * P:(dt + 1) * P]))

            for ft in range(32):
                @blk.tensor
                def _(te, ft=ft):
                    S["tp"].wait(te)
                    S["sc"].wait(te)
                    for kt in range(8):
                        inst = te.matmul(p_h[:, :], w1_sb[:, kt, ft * P:(ft + 1) * P],
                                         nrm_c[:, kt, :], start=(kt == 0), stop=(kt == 7))
                    S["te"].inc(inst)

                @blk.scalar
                def _(sc, ft=ft):
                    S["te"].wait(sc)
                    inst = sc.activation(h_bf[:, ft, :], p_h[:, :], AF.Gelu,
                                         bias=b1_sb[:, ft:ft + 1])
                    S["sc"].inc(inst)

            for dt in range(8):
                @blk.tensor
                def _(te, dt=dt):
                    S["sc"].wait(te)
                    S["ve"].wait(te)
                    for ft in range(32):
                        inst = te.matmul(p_o[:, :], w2_sb[:, ft, dt * P:(dt + 1) * P],
                                         h_bf[:, ft, :], start=(ft == 0), stop=(ft == 31))
                    S["te"].inc(inst)

                @blk.vector
                def _(ve, dt=dt):
                    S["te"].wait(ve)
                    inst = ve.tensor_scalar_add(o_bf[:, :], p_o[:, :], b2_sb[:, dt:dt + 1])
                    S["ve"].inc(inst)

                @blk.sync
                def _(sy, dt=dt, t0=t0):
                    S["ve"].wait(sy)
                    S["st"].dma(sy.dma_start(
                        out=slab[dt * P:(dt + 1) * P, t0:t0 + nt], in_=o_bf[:, :]))

                @blk.vector
                def _(ve):
                    S["st"].wait(ve)
                    inst = ve.engine_nop()
                    S["ve"].inc(inst)

        @blk.sync
        def _(sy):
            S["st"].wait(sy)
    return nc


def build_launch_G1(nt=256):
    NCH = 37
    nch = TPC // nt
    nc = bass.Bass("TRN2", target_bir_lowering=False)
    x2 = nc.dram_tensor("x2", [D, TPC], F32, kind="ExternalInput").ap()
    feat = nc.dram_tensor("feat", [D, TPC], F32, kind="ExternalInput").ap()
    ffnG = nc.dram_tensor("ffnG", [8 * TPC, D], BF16, kind="ExternalInput").ap()
    t2s = nc.dram_tensor("t2s", [P, 32], I32, kind="ExternalInput").ap()
    wcatT = nc.dram_tensor("wcatT", [D, NCH * P], F32, kind="ExternalInput").ap()
    bcat = nc.dram_tensor("bcat", [P, NCH], F32, kind="ExternalInput").ap()
    ng = nc.dram_tensor("ng", [P, 8], F32, kind="ExternalInput").ap()
    nb = nc.dram_tensor("nb", [P, 8], F32, kind="ExternalInput").ap()
    stag2 = nc.dram_tensor("stag2", [TPC, D], BF16, kind="Internal").ap()
    x3_o = nc.dram_tensor("x3_o", [D, TPC], F32, kind="ExternalOutput").ap()
    z_o = nc.dram_tensor("z_o", [D, TPC], F32, kind="ExternalOutput").ap()
    u_re_o = nc.dram_tensor("u_re_o", [512, TPC], BF16, kind="ExternalOutput").ap()
    u_im_o = nc.dram_tensor("u_im_o", [512, TPC], BF16, kind="ExternalOutput").ap()
    cp_o = nc.dram_tensor("cp_o", [D, TPC], F32, kind="ExternalOutput").ap()
    gv_o = nc.dram_tensor("gv_o", [D, TPC], F32, kind="ExternalOutput").ap()
    ts_o = nc.dram_tensor("ts_o", [1, TPC], F32, kind="ExternalOutput").ap()
    band_o = nc.dram_tensor("band_o", [4, TPC], F32, kind="ExternalOutput").ap()

    from concourse.bass_core_rust import IndirectOffsetOnAxis
    with ExitStack() as ctx:
        E = ctx.enter_context
        w_sb = E(nc.sbuf_tensor("w_sb", [P, 8, NCH * P], BF16))
        bcat_sb = E(nc.sbuf_tensor("bcat_sb", [P, NCH], F32))
        ng_sb = E(nc.sbuf_tensor("ng_sb", [P, 8], F32))
        nb_sb = E(nc.sbuf_tensor("nb_sb", [P, 8], F32))
        idx_sb = E(nc.sbuf_tensor("idx_sb", [P, 32], I32))
        g_sb = E(nc.sbuf_tensor("g_sb", [P, D], BF16))
        x2c = E(nc.sbuf_tensor("x2c", [P, 8, nt], F32))
        featc = E(nc.sbuf_tensor("featc", [P, 8, nt], F32))
        ffn_c = E(nc.sbuf_tensor("ffn_c", [P, 8, nt], BF16))
        x3c = E(nc.sbuf_tensor("x3c", [P, 8, nt], F32))
        zc = E(nc.sbuf_tensor("zc", [P, 8, nt], F32))
        xsq = E(nc.sbuf_tensor("xsq", [P, 8, nt], F32))
        xnb = E(nc.sbuf_tensor("xnb", [P, 8, nt], BF16))
        sig_cg = E(nc.sbuf_tensor("sig_cg", [P, 8, nt], F32))
        sp_st = E(nc.sbuf_tensor("sp_st", [P, 8, nt], F32))
        fg_st = E(nc.sbuf_tensor("fg_st", [P, 4, nt], F32))
        misc_st = E(nc.sbuf_tensor("misc_st", [P, nt], F32))
        cp_st = E(nc.sbuf_tensor("cp_st", [P, nt], F32))
        gv_st = E(nc.sbuf_tensor("gv_st", [P, nt], F32))
        u_st = E(nc.sbuf_tensor("u_st", [P, nt], BF16))
        ones_col = E(nc.sbuf_tensor("ones_col", [P, 1], F32))
        ones_row = E(nc.sbuf_tensor("ones_row", [1, P], F32))
        eps_ap = E(nc.sbuf_tensor("eps_ap", [1, 1], F32))
        murow = E(nc.sbuf_tensor("murow", [1, nt], F32))
        qrow = E(nc.sbuf_tensor("qrow", [1, nt], F32))
        mu2 = E(nc.sbuf_tensor("mu2", [1, nt], F32))
        varrow = E(nc.sbuf_tensor("varrow", [1, nt], F32))
        rstdrow = E(nc.sbuf_tensor("rstdrow", [1, nt], F32))
        mu_b = E(nc.sbuf_tensor("mu_b", [P, nt], F32))
        rstd_b = E(nc.sbuf_tensor("rstd_b", [P, nt], F32))
        p_sum = E(nc.psum_tensor("p_sum", [1, nt], F32))
        p_sq = E(nc.psum_tensor("p_sq", [1, nt], F32))
        p_mu = E(nc.psum_tensor("p_mu", [P, nt], F32))
        p_rstd = E(nc.psum_tensor("p_rstd", [P, nt], F32))
        p_pr = E(nc.psum_tensor("p_pr", [P, nt], F32))
        blk = E(nc.Block())
        S = {"blk": blk, "ld": Sem(ctx, nc, "ld"), "st": Sem(ctx, nc, "st"),
             "sc": Sem(ctx, nc, "sc"), "ve": Sem(ctx, nc, "ve"),
             "te": Sem(ctx, nc, "te"), "gi": Sem(ctx, nc, "gi"),
             "tp": Sem(ctx, nc, "tp")}

        @blk.gpsimd
        def _(gp):
            S["ld"].dma(gp.dma_start(
                out=w_sb[:, :, :], in_=wcatT.rearrange("(kt p) c -> p kt c", p=P)))

        @blk.sync
        def _(sy):
            S["ld"].dma(sy.dma_start(out=bcat_sb[:, :], in_=bcat))
            S["ld"].dma(sy.dma_start(out=ng_sb[:, :], in_=ng))
            S["ld"].dma(sy.dma_start(out=nb_sb[:, :], in_=nb))
            S["ld"].dma(sy.dma_start(out=idx_sb[:, :], in_=t2s))

        @blk.vector
        def _(ve):
            ve.memset(ones_col[:, :], 1.0)
            ve.memset(ones_row[:, :], 1.0)
            inst = ve.memset(eps_ap[:, :], LN_EPS)
            S["ve"].inc(inst)

        for r in range(32):
            @blk.gpsimd
            def _(gp, r=r):
                S["ld"].wait(gp)
                S["st"].wait(gp)
                S["gi"].dma(gp.indirect_dma_start(
                    out=g_sb[:, :], out_offset=None,
                    in_=ffnG, in_offset=IndirectOffsetOnAxis(ap=idx_sb[:, r:r + 1], axis=0)))

            @blk.sync
            def _(sy, r=r):
                S["gi"].wait(sy)
                S["st"].dma(sy.dma_start(out=stag2[r * P:(r + 1) * P, :], in_=g_sb[:, :]))

        for ci in range(nch):
            t0 = ci * nt

            @blk.sync
            def _(sy, t0=t0):
                S["st"].wait(sy)
                S["ve"].wait(sy)
                S["sc"].wait(sy)
                S["te"].wait(sy)
                S["ld"].dma(sy.dma_start(out=x2c[:, :, :], in_=fm_ap(x2, D, t0, nt)))
                S["ld"].dma(sy.dma_start(out=featc[:, :, :], in_=fm_ap(feat, D, t0, nt)))

            @blk.scalar
            def _(sc, t0=t0):
                S["gi"].wait(sc)
                S["ve"].wait(sc)
                for dt in range(8):
                    S["tp"].dma(sc.dma_start_transpose(
                        out=ffn_c[:, dt, :], in_=stag2[t0:t0 + nt, dt * P:(dt + 1) * P]))

            @blk.vector
            def _(ve):
                S["ld"].wait(ve)
                S["tp"].wait(ve)
                S["st"].wait(ve)
                for kt in range(8):
                    ve.tensor_add(x3c[:, kt, :], x2c[:, kt, :], featc[:, kt, :])
                    inst = ve.tensor_add(x3c[:, kt, :], x3c[:, kt, :], ffn_c[:, kt, :])
                S["ve"].inc(inst)

            emit_ln(nc, S, x3c, 8, nt, ones_col, ones_row, p_sum, p_sq, p_mu, p_rstd,
                    xsq, murow, qrow, mu2, varrow, rstdrow, eps_ap, mu_b, rstd_b,
                    gates=("ve",))

            @blk.vector
            def _(ve):
                S["ve"].wait(ve)
                for kt in range(8):
                    ve.tensor_sub(xsq[:, kt, :], x3c[:, kt, :], mu_b[:, :])
                    inst = ve.tensor_mul(zc[:, kt, :], xsq[:, kt, :], rstd_b[:, :])
                S["ve"].inc(inst)

            emit_ln(nc, S, zc, 8, nt, ones_col, ones_row, p_sum, p_sq, p_mu, p_rstd,
                    xsq, murow, qrow, mu2, varrow, rstdrow, eps_ap, mu_b, rstd_b,
                    gates=("ve",))

            @blk.vector
            def _(ve):
                S["ve"].wait(ve)
                for kt in range(8):
                    ve.tensor_sub(xsq[:, kt, :], zc[:, kt, :], mu_b[:, :])
                    ve.tensor_mul(xsq[:, kt, :], xsq[:, kt, :], rstd_b[:, :])
                    inst = ve.tensor_scalar(
                        xnb[:, kt, :], xsq[:, kt, :],
                        ng_sb[:, kt:kt + 1], nb_sb[:, kt:kt + 1],
                        op0=ALU.mult, op1=ALU.add)
                S["ve"].inc(inst)

            order = list(range(8, 16)) + list(range(0, 8)) + list(range(16, NCH))
            for ct in order:
                @blk.tensor
                def _(te, ct=ct):
                    S["ve"].wait(te)
                    S["sc"].wait(te)
                    for kt in range(8):
                        inst = te.matmul(p_pr[:, :], w_sb[:, kt, ct * P:(ct + 1) * P],
                                         xnb[:, kt, :], start=(kt == 0), stop=(kt == 7))
                    S["te"].inc(inst)

                if 8 <= ct < 16:
                    @blk.scalar
                    def _(sc, ct=ct):
                        S["te"].wait(sc)
                        inst = sc.activation(sig_cg[:, ct - 8, :], p_pr[:, :], AF.Sigmoid,
                                             bias=bcat_sb[:, ct:ct + 1])
                        S["sc"].inc(inst)
                elif ct < 8:
                    @blk.vector
                    def _(ve, ct=ct, t0=t0):
                        S["te"].wait(ve)
                        inst = ve.scalar_tensor_tensor(
                            out=cp_st[:, :], in0=p_pr[:, :], scalar=bcat_sb[:, ct:ct + 1],
                            in1=sig_cg[:, ct, :], op0=ALU.add, op1=ALU.mult)
                        S["ve"].inc(inst)

                    @blk.sync
                    def _(sy, ct=ct, t0=t0):
                        S["ve"].wait(sy)
                        S["st"].dma(sy.dma_start(
                            out=cp_o[ct * P:(ct + 1) * P, t0:t0 + nt], in_=cp_st[:, :]))

                    @blk.vector
                    def _(ve):
                        S["st"].wait(ve)
                elif 16 <= ct < 24:
                    @blk.vector
                    def _(ve, ct=ct):
                        S["te"].wait(ve)
                        inst = ve.tensor_copy(sp_st[:, ct - 16, :], p_pr[:, :])
                        S["ve"].inc(inst)
                elif 24 <= ct < 32:
                    @blk.scalar
                    def _(sc, ct=ct):
                        S["te"].wait(sc)
                        inst = sc.activation(gv_st[:, :], p_pr[:, :], AF.Sigmoid,
                                             bias=bcat_sb[:, ct:ct + 1])
                        S["sc"].inc(inst)

                    @blk.sync
                    def _(sy, ct=ct, t0=t0):
                        S["sc"].wait(sy)
                        S["st"].dma(sy.dma_start(
                            out=gv_o[(ct - 24) * P:(ct - 23) * P, t0:t0 + nt], in_=gv_st[:, :]))

                    @blk.scalar
                    def _(sc):
                        S["st"].wait(sc)
                elif 32 <= ct < 36:
                    @blk.scalar
                    def _(sc, ct=ct):
                        S["te"].wait(sc)
                        inst = sc.activation(fg_st[:, ct - 32, :], p_pr[:, :], AF.Sigmoid,
                                             bias=bcat_sb[:, ct:ct + 1])
                        S["sc"].inc(inst)
                else:
                    @blk.scalar
                    def _(sc, ct=ct):
                        S["te"].wait(sc)
                        inst = sc.activation(misc_st[:, :], p_pr[:, :], AF.Sigmoid,
                                             bias=bcat_sb[:, ct:ct + 1])
                        S["sc"].inc(inst)

            for j in range(4):
                @blk.vector
                def _(ve, j=j, t0=t0):
                    S["ve"].wait(ve)
                    S["sc"].wait(ve)
                    S["st"].wait(ve)
                    inst = ve.tensor_mul(u_st[:, :], sp_st[:, j, :], fg_st[:, j, :])
                    S["ve"].inc(inst)

                @blk.sync
                def _(sy, j=j, t0=t0):
                    S["ve"].wait(sy)
                    S["st"].dma(sy.dma_start(
                        out=u_re_o[j * P:(j + 1) * P, t0:t0 + nt], in_=u_st[:, :]))

                @blk.vector
                def _(ve, j=j):
                    S["st"].wait(ve)
                    inst = ve.tensor_mul(u_st[:, :], sp_st[:, 4 + j, :], fg_st[:, j, :])
                    S["ve"].inc(inst)

                @blk.sync
                def _(sy, j=j, t0=t0):
                    S["ve"].wait(sy)
                    S["st"].dma(sy.dma_start(
                        out=u_im_o[j * P:(j + 1) * P, t0:t0 + nt], in_=u_st[:, :]))

            @blk.sync
            def _(sy, t0=t0):
                S["ve"].wait(sy)
                S["sc"].wait(sy)
                S["st"].dma(sy.dma_start(out=fm_ap(x3_o, D, t0, nt), in_=x3c[:, :, :]))
                S["st"].dma(sy.dma_start(out=fm_ap(z_o, D, t0, nt), in_=zc[:, :, :]))
                S["st"].dma(sy.dma_start(out=band_o[:, t0:t0 + nt], in_=misc_st[0:4, :]))
                S["st"].dma(sy.dma_start(out=ts_o[:, t0:t0 + nt], in_=misc_st[4:5, :]))

        @blk.sync
        def _(sy):
            S["st"].wait(sy)
    return nc


def build_launch_F(nt=512):
    CH = 256
    NJ = CH // P
    nch = SEQ // nt
    PI = float(np.pi)
    nc = bass.Bass("TRN2", target_bir_lowering=False)
    u_re = nc.dram_tensor("u_re", [CH, SEQ], BF16, kind="ExternalInput").ap()
    u_im = nc.dram_tensor("u_im", [CH, SEQ], BF16, kind="ExternalInput").ap()
    tsrow = nc.dram_tensor("tsrow", [1, SEQ], F32, kind="ExternalInput").ap()
    bandr = nc.dram_tensor("bandr", [1, 2 * SEQ], F32, kind="ExternalInput").ap()
    ld_loc = nc.dram_tensor("ld_loc", [1, CH], F32, kind="ExternalInput").ap()
    fr_loc = nc.dram_tensor("fr_loc", [1, CH], F32, kind="ExternalInput").ap()
    dt_loc = nc.dram_tensor("dt_loc", [1, CH], F32, kind="ExternalInput").ap()
    fi_roll = nc.dram_tensor("fi_roll", [1, 512], F32, kind="ExternalInput").ap()
    iota1 = nc.dram_tensor("iota1", [1, SEQ], F32, kind="ExternalInput").ap()
    cst = {n: nc.dram_tensor(n, [128, 512 if n.startswith("t") else 128], BF16,
                             kind="ExternalInput").ap()
           for n in ["f_re", "f_im", "f_imN", "fc_re", "fc_im", "fc_imN",
                     "t_re", "t_im", "tc_re", "tc_im"]}
    dec_d0 = nc.dram_tensor("dec_d0", [P, 1], F32, kind="Internal").ap()
    dec_d1 = nc.dram_tensor("dec_d1", [P, 1], F32, kind="Internal").ap()
    k_re_d = nc.dram_tensor("k_re_d", [CH, SEQ], BF16, kind="Internal").ap()
    k_im_d = nc.dram_tensor("k_im_d", [CH, SEQ], BF16, kind="Internal").ap()
    y_re_o = nc.dram_tensor("y_re_o", [CH, SEQ], BF16, kind="ExternalOutput").ap()
    y_im_o = nc.dram_tensor("y_im_o", [CH, SEQ], BF16, kind="ExternalOutput").ap()

    with ExitStack() as ctx:
        E = ctx.enter_context
        C = {n: E(nc.sbuf_tensor("s_" + n, [128, 512 if n.startswith("t") else 128], BF16))
             for n in cst}
        ldr = E(nc.sbuf_tensor("ldr", [1, CH], F32))
        frr = E(nc.sbuf_tensor("frr", [1, CH], F32))
        dtr = E(nc.sbuf_tensor("dtr", [1, CH], F32))
        fir = E(nc.sbuf_tensor("fir", [1, 512], F32))
        fie = E(nc.sbuf_tensor("fie", [1, 512], F32))
        smx = E(nc.sbuf_tensor("smx", [1, 4], F32))
        decr = E(nc.sbuf_tensor("decr", [1, CH], F32))
        omr = E(nc.sbuf_tensor("omr", [1, CH], F32))
        ts_c = E(nc.sbuf_tensor("ts_c", [1, 512], F32))
        bnd_c = E(nc.sbuf_tensor("bnd_c", [1, 512], F32))
        iot_c = E(nc.sbuf_tensor("iot_c", [1, 512], F32))
        dec_col = E(nc.sbuf_tensor("dec_col", [P, NJ], F32))
        amag = E(nc.sbuf_tensor("amag", [P, SEQ], F32))
        rscan = E(nc.sbuf_tensor("rscan", [P, SEQ + 1], F32))
        ones_row = E(nc.sbuf_tensor("ones_row", [1, P], F32))
        ones_cl = E(nc.sbuf_tensor("ones_cl", [P, 1], F32))
        phs = E(nc.sbuf_tensor("phs", [P, nt], F32))
        ki = E(nc.sbuf_tensor("ki", [P, nt], I32))
        kf = E(nc.sbuf_tensor("kf", [P, nt], F32))
        cu8 = E(nc.sbuf_tensor("cu8", [P, nt], mybir.dt.uint8))
        sinv = E(nc.sbuf_tensor("sinv", [P, nt], F32))
        cosv = E(nc.sbuf_tensor("cosv", [P, nt], F32))
        ktmp = E(nc.sbuf_tensor("ktmp", [P, nt], F32))
        kre_st = E(nc.sbuf_tensor("kre_st", [P, nt], BF16))
        kim_st = E(nc.sbuf_tensor("kim_st", [P, nt], BF16))
        xre = E(nc.sbuf_tensor("xre", [64, 4, 128], BF16))
        xim = E(nc.sbuf_tensor("xim", [64, 4, 128], BF16))
        kre = E(nc.sbuf_tensor("kre", [64, 4, 128], BF16))
        kim = E(nc.sbuf_tensor("kim", [64, 4, 128], BF16))
        su_re = E(nc.sbuf_tensor("su_re", [128, 4, 128], BF16))
        su_im = E(nc.sbuf_tensor("su_im", [128, 4, 128], BF16))
        sk_re = E(nc.sbuf_tensor("sk_re", [128, 4, 128], BF16))
        sk_im = E(nc.sbuf_tensor("sk_im", [128, 4, 128], BF16))
        pr_re = E(nc.sbuf_tensor("pr_re", [128, 4, 128], BF16))
        pr_im = E(nc.sbuf_tensor("pr_im", [128, 4, 128], BF16))
        yt_re = E(nc.sbuf_tensor("yt_re", [64, 4, 128], BF16))
        yt_im = E(nc.sbuf_tensor("yt_im", [64, 4, 128], BF16))
        sb = {n: E(nc.sbuf_tensor(n, [128, 512], BF16))
              for n in ["d_re", "d_im", "e_re", "e_im", "scr"]}
        ps = {n: E(nc.psum_tensor(n, [128, 512], F32))
              for n in ["pA_re", "pA_im", "pB_re", "pB_im"]}
        p_b = E(nc.psum_tensor("p_b", [P, nt], F32))
        p_t = E(nc.psum_tensor("p_t", [P, nt], F32))
        p_ph = E(nc.psum_tensor("p_ph", [P, nt], F32))
        blk = E(nc.Block())
        S = {"blk": blk, "ld": Sem(ctx, nc, "ld"), "st": Sem(ctx, nc, "st"),
             "sc": Sem(ctx, nc, "sc"), "ve": Sem(ctx, nc, "ve"),
             "te": Sem(ctx, nc, "te"), "vv": Sem(ctx, nc, "vv")}

        @blk.sync
        def _(sy):
            for n, ap in cst.items():
                S["ld"].dma(sy.dma_start(out=C[n][:, :], in_=ap))
            for t, ap in ((ldr, ld_loc), (frr, fr_loc), (dtr, dt_loc), (fir, fi_roll)):
                S["ld"].dma(sy.dma_start(out=t[:, :], in_=ap))

        @blk.vector
        def _(ve):
            ve.memset(ones_row[:, :], 1.0)
            ve.memset(ones_cl[:, :], 1.0)
            ve.memset(rscan[:, 0:1], 1.0)
            S["ld"].wait(ve)
            inst = ve.tensor_reduce(smx[0:1, 0:1], fir[:, :], mybir.AxisListType.X, ALU.max)
            S["ve"].inc(inst)

        @blk.vector
        def _(ve):
            S["ve"].wait(ve)
            S["vv"].inc(ve.tensor_scalar_mul(smx[0:1, 1:2], smx[0:1, 0:1], -1.0))
            S["vv"].wait(ve)
            inst = ve.tensor_scalar_add(fie[:, :], fir[:, :], smx[0:1, 1:2])
            S["ve"].inc(inst)

        @blk.scalar
        def _(sc):
            S["ve"].wait(sc)
            inst = sc.activation(fie[:, :], fie[:, :], AF.Exp)
            S["sc"].inc(inst)

        @blk.vector
        def _(ve):
            S["sc"].wait(ve)
            S["vv"].inc(ve.tensor_reduce(smx[0:1, 2:3], fie[:, :], mybir.AxisListType.X, ALU.add))
            S["vv"].wait(ve)
            S["vv"].inc(ve.reciprocal(smx[0:1, 3:4], smx[0:1, 2:3]))
            S["vv"].wait(ve)
            S["vv"].inc(ve.tensor_scalar_mul(decr[:, :], fie[0:1, 0:CH], smx[0:1, 3:4]))
            S["vv"].wait(ve)
            inst = ve.tensor_scalar_mul(decr[:, :], decr[:, :], 512.0)
            S["ve"].inc(inst)

        @blk.scalar
        def _(sc):
            S["ve"].wait(sc)
            sc.activation(ldr[:, :], ldr[:, :], AF.Sigmoid)
            inst = sc.activation(dtr[:, :], dtr[:, :], AF.Sigmoid)
            S["sc"].inc(inst)

        @blk.vector
        def _(ve):
            S["sc"].wait(ve)
            ve.tensor_mul(decr[:, :], decr[:, :], ldr[:, :])
            S["vv"].inc(ve.tensor_mul(omr[:, :], frr[:, :], dtr[:, :]))
            S["vv"].wait(ve)
            inst = ve.tensor_scalar_mul(omr[:, :], omr[:, :], 0.2)
            S["ve"].inc(inst)

        @blk.sync
        def _(sy):
            S["ve"].wait(sy)
            S["st"].dma(sy.dma_start(out=dec_d0, in_=decr[0:1, 0:P]))
            S["st"].dma(sy.dma_start(out=dec_d1, in_=decr[0:1, P:2 * P]))

        @blk.sync
        def _(sy):
            S["st"].wait(sy)
            S["ld"].dma(sy.dma_start(out=dec_col[:, 0:1], in_=dec_d0))
            S["ld"].dma(sy.dma_start(out=dec_col[:, 1:2], in_=dec_d1))

        for j in range(NJ):
            @blk.vector
            def _(ve, j=j):
                S["ld"].wait(ve)
                S["te"].wait(ve)
                inst = ve.engine_nop()
                S["ve"].inc(inst)

            for ci in range(nch):
                t0 = ci * nt

                @blk.sync
                def _(sy, j=j, t0=t0):
                    S["te"].wait(sy)
                    S["ld"].dma(sy.dma_start(out=bnd_c[:, 0:nt],
                                             in_=bandr[0:1, j * SEQ + t0:j * SEQ + t0 + nt]))

                @blk.tensor
                def _(te, j=j, t0=t0):
                    S["ve"].wait(te)
                    S["ld"].wait(te)
                    inst = te.matmul(p_b[:, :], ones_row[:, :], bnd_c[0:1, 0:nt],
                                     start=True, stop=True)
                    S["te"].inc(inst)

                @blk.vector
                def _(ve, j=j, t0=t0):
                    S["te"].wait(ve)
                    inst = ve.tensor_single_scalar(amag[:, t0:t0 + nt], p_b[:, :],
                                                   dec_col[:, j:j + 1], op=ALU.mult)
                    S["ve"].inc(inst)

            @blk.vector
            def _(ve, j=j):
                S["ve"].wait(ve)
                inst = ve.tensor_tensor_scan(
                    out=rscan[:, 1:SEQ + 1], data0=amag[:, :],
                    data1=ones_cl[:, 0:1].to_broadcast([P, SEQ]),
                    initial=1.0, op0=ALU.mult, op1=ALU.mult)
                S["ve"].inc(inst)

            for ci in range(nch):
                t0 = ci * nt

                @blk.sync
                def _(sy, t0=t0):
                    S["te"].wait(sy)
                    S["ld"].dma(sy.dma_start(out=iot_c[:, 0:nt], in_=iota1[0:1, t0:t0 + nt]))
                    S["ld"].dma(sy.dma_start(out=ts_c[:, 0:nt], in_=tsrow[0:1, t0:t0 + nt]))

                @blk.tensor
                def _(te, j=j, t0=t0):
                    S["ve"].wait(te)
                    S["sc"].wait(te)
                    S["ld"].wait(te)
                    te.matmul(p_ph[:, :], omr[0:1, j * P:(j + 1) * P], iot_c[0:1, 0:nt],
                              start=True, stop=True)
                    inst = te.matmul(p_t[:, :], ones_row[:, :], ts_c[0:1, 0:nt],
                                     start=True, stop=True)
                    S["te"].inc(inst)

                @blk.vector
                def _(ve, t0=t0):
                    S["te"].wait(ve)
                    S["vv"].inc(ve.tensor_single_scalar(kf[:, :], p_ph[:, :], 1.0 / (2 * PI), op=ALU.mult))
                    S["vv"].wait(ve)
                    S["vv"].inc(ve.tensor_copy(ki[:, :], kf[:, :]))
                    S["vv"].wait(ve)
                    S["vv"].inc(ve.tensor_copy(kf[:, :], ki[:, :]))
                    S["vv"].wait(ve)
                    S["vv"].inc(ve.scalar_tensor_tensor(out=phs[:, :], in0=kf[:, :],
                                                        scalar=-2 * PI, in1=p_ph[:, :],
                                                        op0=ALU.mult, op1=ALU.add))
                    S["vv"].wait(ve)
                    S["vv"].inc(ve.tensor_scalar(cu8[:, :], phs[:, :], PI, None, op0=ALU.is_gt))
                    S["vv"].wait(ve)
                    S["vv"].inc(ve.scalar_tensor_tensor(out=phs[:, :], in0=cu8[:, :],
                                                        scalar=-2 * PI, in1=phs[:, :],
                                                        op0=ALU.mult, op1=ALU.add))
                    S["vv"].wait(ve)
                    S["vv"].inc(ve.tensor_scalar(cu8[:, :], phs[:, :], -PI, None, op0=ALU.is_lt))
                    S["vv"].wait(ve)
                    inst = ve.scalar_tensor_tensor(out=phs[:, :], in0=cu8[:, :],
                                                   scalar=2 * PI, in1=phs[:, :],
                                                   op0=ALU.mult, op1=ALU.add)
                    S["ve"].inc(inst)

                @blk.scalar
                def _(sc):
                    S["ve"].wait(sc)
                    inst = sc.activation(sinv[:, :], phs[:, :], AF.Sin)
                    S["sc"].inc(inst)

                @blk.vector
                def _(ve):
                    S["sc"].wait(ve)
                    S["vv"].inc(ve.tensor_scalar_add(phs[:, :], phs[:, :], PI / 2))
                    S["vv"].wait(ve)
                    S["vv"].inc(ve.tensor_scalar(cu8[:, :], phs[:, :], PI, None, op0=ALU.is_gt))
                    S["vv"].wait(ve)
                    inst = ve.scalar_tensor_tensor(out=phs[:, :], in0=cu8[:, :],
                                                   scalar=-2 * PI, in1=phs[:, :],
                                                   op0=ALU.mult, op1=ALU.add)
                    S["ve"].inc(inst)

                @blk.scalar
                def _(sc):
                    S["ve"].wait(sc)
                    inst = sc.activation(cosv[:, :], phs[:, :], AF.Sin)
                    S["sc"].inc(inst)

                @blk.vector
                def _(ve, t0=t0):
                    S["sc"].wait(ve)
                    S["st"].wait(ve)
                    S["vv"].inc(ve.tensor_mul(ktmp[:, :], rscan[:, t0:t0 + nt], p_t[:, :]))
                    S["vv"].wait(ve)
                    ve.tensor_mul(kre_st[:, :], ktmp[:, :], cosv[:, :])
                    inst = ve.tensor_mul(kim_st[:, :], ktmp[:, :], sinv[:, :])
                    S["ve"].inc(inst)

                @blk.sync
                def _(sy, j=j, t0=t0):
                    S["ve"].wait(sy)
                    S["st"].dma(sy.dma_start(
                        out=k_re_d[j * P:(j + 1) * P, t0:t0 + nt], in_=kre_st[:, :]))
                    S["st"].dma(sy.dma_start(
                        out=k_im_d[j * P:(j + 1) * P, t0:t0 + nt], in_=kim_st[:, :]))

        for g in range(CH // 4):
            c0 = g * 4

            @blk.gpsimd
            def _(gp, c0=c0):
                S["ve"].wait(gp)
                S["st"].wait(gp)
                S["ld"].dma(gp.dma_start(
                    out=xre[:, :, :], in_=u_re[c0:c0 + 4, :].rearrange("c (a b) -> a c b", b=128)))
                S["ld"].dma(gp.dma_start(
                    out=xim[:, :, :], in_=u_im[c0:c0 + 4, :].rearrange("c (a b) -> a c b", b=128)))
                S["ld"].dma(gp.dma_start(
                    out=kre[:, :, :], in_=k_re_d[c0:c0 + 4, :].rearrange("c (a b) -> a c b", b=128)))
                S["ld"].dma(gp.dma_start(
                    out=kim[:, :, :], in_=k_im_d[c0:c0 + 4, :].rearrange("c (a b) -> a c b", b=128)))

            te_gate = (lambda te: (S["ld"].wait(te), S["ve"].wait(te)))
            emit_fft4(S, te_gate, None, xre, xim, C, ps, sb, su_re, su_im)
            emit_fft4(S, te_gate, None, kre, kim, C, ps, sb, sk_re, sk_im)
            emit_cmul4(S, su_re, su_im, sk_re, sk_im, pr_re, pr_im, sb["scr"])
            emit_ifft4(S, pr_re, pr_im, C, ps, sb, yt_re, yt_im)

            @blk.sync
            def _(sy, c0=c0):
                S["ve"].wait(sy)
                S["st"].dma(sy.dma_start(
                    out=y_re_o[c0:c0 + 4, :].rearrange("c (a b) -> a c b", b=128),
                    in_=yt_re[:, :, :]))
                S["st"].dma(sy.dma_start(
                    out=y_im_o[c0:c0 + 4, :].rearrange("c (a b) -> a c b", b=128),
                    in_=yt_im[:, :, :]))

        @blk.sync
        def _(sy):
            S["st"].wait(sy)
    return nc


def build_launch_G2a(nt=512):
    nch = TPC // nt
    nc = bass.Bass("TRN2", target_bir_lowering=False)
    z_in = nc.dram_tensor("z_in", [D, TPC], F32, kind="ExternalInput").ap()
    spec = nc.dram_tensor("spec", [D, TPC], BF16, kind="ExternalInput").ap()
    cp = nc.dram_tensor("cp", [D, TPC], F32, kind="ExternalInput").ap()
    gv = nc.dram_tensor("gv", [D, TPC], F32, kind="ExternalInput").ap()
    wfsT = nc.dram_tensor("wfsT", [D, D], F32, kind="ExternalInput").ap()
    osc = nc.dram_tensor("osc", [1, 1], F32, kind="ExternalInput").ap()
    x4_o = nc.dram_tensor("x4_o", [D, TPC], F32, kind="ExternalOutput").ap()

    with ExitStack() as ctx:
        E = ctx.enter_context
        wfs_sb = E(nc.sbuf_tensor("wfs_sb", [P, 8, D], BF16))
        osc_sb = E(nc.sbuf_tensor("osc_sb", [1, 1], F32))
        osc_col = E(nc.sbuf_tensor("osc_col", [P, 1], F32))
        ones_row = E(nc.sbuf_tensor("ones_row", [1, P], F32))
        specc = E(nc.sbuf_tensor("specc", [P, 8, nt], BF16))
        zc = E(nc.sbuf_tensor("zc", [P, 8, nt], F32))
        cpc = E(nc.sbuf_tensor("cpc", [P, 8, nt], F32))
        gvc = E(nc.sbuf_tensor("gvc", [P, 8, nt], F32))
        x4c = E(nc.sbuf_tensor("x4c", [P, 8, nt], F32))
        scr = E(nc.sbuf_tensor("scr", [P, nt], F32))
        p_y = E(nc.psum_tensor("p_y", [P, nt], F32))
        p_o = E(nc.psum_tensor("p_o", [P, 1], F32))
        blk = E(nc.Block())
        S = {"blk": blk, "ld": Sem(ctx, nc, "ld"), "st": Sem(ctx, nc, "st"),
             "sc": Sem(ctx, nc, "sc"), "ve": Sem(ctx, nc, "ve"),
             "te": Sem(ctx, nc, "te")}

        @blk.gpsimd
        def _(gp):
            S["ld"].dma(gp.dma_start(
                out=wfs_sb[:, :, :], in_=wfsT.rearrange("(kt p) d -> p kt d", p=P)))

        @blk.sync
        def _(sy):
            S["ld"].dma(sy.dma_start(out=osc_sb[:, :], in_=osc))

        @blk.vector
        def _(ve):
            inst = ve.memset(ones_row[:, :], 1.0)
            S["ve"].inc(inst)

        @blk.tensor
        def _(te):
            S["ld"].wait(te)
            S["ve"].wait(te)
            inst = te.matmul(p_o[:, :], ones_row[:, :], osc_sb[:, :], start=True, stop=True)
            S["te"].inc(inst)

        @blk.vector
        def _(ve):
            S["te"].wait(ve)
            inst = ve.tensor_copy(osc_col[:, :], p_o[:, :])
            S["ve"].inc(inst)

        for ci in range(nch):
            t0 = ci * nt

            @blk.sync
            def _(sy, t0=t0):
                S["st"].wait(sy)
                S["ve"].wait(sy)
                S["te"].wait(sy)
                S["ld"].dma(sy.dma_start(out=zc[:, :, :], in_=fm_ap(z_in, D, t0, nt)))
                S["ld"].dma(sy.dma_start(out=cpc[:, :, :], in_=fm_ap(cp, D, t0, nt)))
                S["ld"].dma(sy.dma_start(out=gvc[:, :, :], in_=fm_ap(gv, D, t0, nt)))

            @blk.gpsimd
            def _(gp, t0=t0):
                S["ve"].wait(gp)
                S["te"].wait(gp)
                S["ld"].dma(gp.dma_start(out=specc[:, :, :], in_=fm_ap(spec, D, t0, nt)))

            for dt in range(8):
                @blk.tensor
                def _(te, dt=dt):
                    S["ld"].wait(te)
                    S["ve"].wait(te)
                    for kt in range(8):
                        inst = te.matmul(p_y[:, :], wfs_sb[:, kt, dt * P:(dt + 1) * P],
                                         specc[:, kt, :], start=(kt == 0), stop=(kt == 7))
                    S["te"].inc(inst)

                @blk.vector
                def _(ve, dt=dt):
                    S["te"].wait(ve)
                    ve.tensor_single_scalar(scr[:, :], p_y[:, :], osc_col[:, 0:1], op=ALU.mult)
                    ve.tensor_mul(scr[:, :], scr[:, :], gvc[:, dt, :])
                    ve.tensor_add(scr[:, :], scr[:, :], cpc[:, dt, :])
                    inst = ve.scalar_tensor_tensor(
                        out=x4c[:, dt, :], in0=scr[:, :], scalar=0.5,
                        in1=zc[:, dt, :], op0=ALU.mult, op1=ALU.add)
                    S["ve"].inc(inst)

            @blk.sync
            def _(sy, t0=t0):
                S["ve"].wait(sy)
                S["st"].dma(sy.dma_start(out=fm_ap(x4_o, D, t0, nt), in_=x4c[:, :, :]))

        @blk.sync
        def _(sy):
            S["st"].wait(sy)
    return nc


def build_launch_G2f(nt=256, w_dtype=F32):
    F = 4096
    nch = TPC // nt
    nc = bass.Bass("TRN2", target_bir_lowering=False)
    x3 = nc.dram_tensor("x3", [D, TPC], F32, kind="ExternalInput").ap()
    x4 = nc.dram_tensor("x4", [D, TPC], F32, kind="ExternalInput").ap()
    n2g = nc.dram_tensor("n2g", [P, 8], F32, kind="ExternalInput").ap()
    n2b = nc.dram_tensor("n2b", [P, 8], F32, kind="ExternalInput").ap()
    w1T = nc.dram_tensor("w1T", [D, F], w_dtype, kind="ExternalInput").ap()
    b1p = nc.dram_tensor("b1p", [P, 32], F32, kind="ExternalInput").ap()
    w2T = nc.dram_tensor("w2T", [F, D], w_dtype, kind="ExternalInput").ap()
    b2p = nc.dram_tensor("b2p", [P, 8], F32, kind="ExternalInput").ap()
    x5_o = nc.dram_tensor("x5_o", [D, TPC], F32, kind="ExternalOutput").ap()

    with ExitStack() as ctx:
        E = ctx.enter_context
        w1_sb = E(nc.sbuf_tensor("w1_sb", [P, 8, F], BF16))
        w2_sb = E(nc.sbuf_tensor("w2_sb", [P, 32, D], BF16))
        b1_sb = E(nc.sbuf_tensor("b1_sb", [P, 32], F32))
        b2_sb = E(nc.sbuf_tensor("b2_sb", [P, 8], F32))
        n2g_sb = E(nc.sbuf_tensor("n2g_sb", [P, 8], F32))
        n2b_sb = E(nc.sbuf_tensor("n2b_sb", [P, 8], F32))
        x3c = E(nc.sbuf_tensor("x3c", [P, 8, nt], F32))
        x4c = E(nc.sbuf_tensor("x4c", [P, 8, nt], F32))
        xsq = E(nc.sbuf_tensor("xsq", [P, 8, nt], F32))
        xnb = E(nc.sbuf_tensor("xnb", [P, 8, nt], BF16))
        h_bf = E(nc.sbuf_tensor("h_bf", [P, 32, nt], BF16))
        x5c = E(nc.sbuf_tensor("x5c", [P, 8, nt], F32))
        ones_col = E(nc.sbuf_tensor("ones_col", [P, 1], F32))
        ones_row = E(nc.sbuf_tensor("ones_row", [1, P], F32))
        eps_ap = E(nc.sbuf_tensor("eps_ap", [1, 1], F32))
        murow = E(nc.sbuf_tensor("murow", [1, nt], F32))
        qrow = E(nc.sbuf_tensor("qrow", [1, nt], F32))
        mu2 = E(nc.sbuf_tensor("mu2", [1, nt], F32))
        varrow = E(nc.sbuf_tensor("varrow", [1, nt], F32))
        rstdrow = E(nc.sbuf_tensor("rstdrow", [1, nt], F32))
        mu_b = E(nc.sbuf_tensor("mu_b", [P, nt], F32))
        rstd_b = E(nc.sbuf_tensor("rstd_b", [P, nt], F32))
        p_sum = E(nc.psum_tensor("p_sum", [1, nt], F32))
        p_sq = E(nc.psum_tensor("p_sq", [1, nt], F32))
        p_mu = E(nc.psum_tensor("p_mu", [P, nt], F32))
        p_rstd = E(nc.psum_tensor("p_rstd", [P, nt], F32))
        p_y = E(nc.psum_tensor("p_y", [P, nt], F32))
        blk = E(nc.Block())
        S = {"blk": blk, "ld": Sem(ctx, nc, "ld"), "st": Sem(ctx, nc, "st"),
             "sc": Sem(ctx, nc, "sc"), "ve": Sem(ctx, nc, "ve"),
             "te": Sem(ctx, nc, "te")}

        @blk.gpsimd
        def _(gp):
            S["ld"].dma(gp.dma_start(
                out=w1_sb[:, :, :], in_=w1T.rearrange("(kt p) f -> p kt f", p=P)))
            S["ld"].dma(gp.dma_start(
                out=w2_sb[:, :, :], in_=w2T.rearrange("(ft p) d -> p ft d", p=P)))

        @blk.sync
        def _(sy):
            S["ld"].dma(sy.dma_start(out=b1_sb[:, :], in_=b1p))
            S["ld"].dma(sy.dma_start(out=b2_sb[:, :], in_=b2p))
            S["ld"].dma(sy.dma_start(out=n2g_sb[:, :], in_=n2g))
            S["ld"].dma(sy.dma_start(out=n2b_sb[:, :], in_=n2b))

        @blk.vector
        def _(ve):
            ve.memset(ones_col[:, :], 1.0)
            ve.memset(ones_row[:, :], 1.0)
            inst = ve.memset(eps_ap[:, :], LN_EPS)
            S["ve"].inc(inst)

        for ci in range(nch):
            t0 = ci * nt

            @blk.sync
            def _(sy, t0=t0):
                S["st"].wait(sy)
                S["ve"].wait(sy)
                S["sc"].wait(sy)
                S["te"].wait(sy)
                S["ld"].dma(sy.dma_start(out=x3c[:, :, :], in_=fm_ap(x3, D, t0, nt)))
                S["ld"].dma(sy.dma_start(out=x4c[:, :, :], in_=fm_ap(x4, D, t0, nt)))

            emit_ln(nc, S, x4c, 8, nt, ones_col, ones_row, p_sum, p_sq, p_mu, p_rstd,
                    xsq, murow, qrow, mu2, varrow, rstdrow, eps_ap, mu_b, rstd_b,
                    gates=("ld",))

            @blk.vector
            def _(ve):
                S["ve"].wait(ve)
                for kt in range(8):
                    ve.tensor_sub(xsq[:, kt, :], x4c[:, kt, :], mu_b[:, :])
                    ve.tensor_mul(xsq[:, kt, :], xsq[:, kt, :], rstd_b[:, :])
                    inst = ve.tensor_scalar(
                        xnb[:, kt, :], xsq[:, kt, :],
                        n2g_sb[:, kt:kt + 1], n2b_sb[:, kt:kt + 1],
                        op0=ALU.mult, op1=ALU.add)
                S["ve"].inc(inst)

            for ft in range(32):
                @blk.tensor
                def _(te, ft=ft):
                    S["ve"].wait(te)
                    S["sc"].wait(te)
                    for kt in range(8):
                        inst = te.matmul(p_y[:, :], w1_sb[:, kt, ft * P:(ft + 1) * P],
                                         xnb[:, kt, :], start=(kt == 0), stop=(kt == 7))
                    S["te"].inc(inst)

                @blk.scalar
                def _(sc, ft=ft):
                    S["te"].wait(sc)
                    inst = sc.activation(h_bf[:, ft, :], p_y[:, :], AF.Gelu,
                                         bias=b1_sb[:, ft:ft + 1])
                    S["sc"].inc(inst)

            for dt in range(8):
                @blk.tensor
                def _(te, dt=dt):
                    S["sc"].wait(te)
                    S["ve"].wait(te)
                    for ft in range(32):
                        inst = te.matmul(p_y[:, :], w2_sb[:, ft, dt * P:(dt + 1) * P],
                                         h_bf[:, ft, :], start=(ft == 0), stop=(ft == 31))
                    S["te"].inc(inst)

                @blk.vector
                def _(ve, dt=dt):
                    S["te"].wait(ve)
                    ve.scalar_tensor_tensor(
                        out=x5c[:, dt, :], in0=p_y[:, :], scalar=b2_sb[:, dt:dt + 1],
                        in1=x4c[:, dt, :], op0=ALU.add, op1=ALU.add)
                    inst = ve.tensor_add(x5c[:, dt, :], x5c[:, dt, :], x3c[:, dt, :])
                    S["ve"].inc(inst)

            @blk.sync
            def _(sy, t0=t0):
                S["ve"].wait(sy)
                S["st"].dma(sy.dma_start(out=fm_ap(x5_o, D, t0, nt), in_=x5c[:, :, :]))

        @blk.sync
        def _(sy):
            S["st"].wait(sy)
    return nc


def build_launch_BIN():
    F = 4096
    nc = bass.Bass("TRN2", target_bir_lowering=False)
    fw1T = nc.dram_tensor("fw1T", [D, F], F32, kind="ExternalInput").ap()
    fw2T = nc.dram_tensor("fw2T", [F, D], F32, kind="ExternalInput").ap()
    bw1T_o = nc.dram_tensor("bw1T_o", [D, F], BF16, kind="ExternalOutput").ap()
    bw2T_o = nc.dram_tensor("bw2T_o", [F, D], BF16, kind="ExternalOutput").ap()

    with ExitStack() as ctx:
        E = ctx.enter_context
        wtile = E(nc.sbuf_tensor("wtile", [P, F], F32))
        scr = E(nc.sbuf_tensor("scr", [P, F], F32))
        alpha = E(nc.sbuf_tensor("alpha", [1, F], F32))
        alpha_b = E(nc.sbuf_tensor("alpha_b", [P, F], F32))
        bw_st = E(nc.sbuf_tensor("bw_st", [P, F], BF16))
        ones_col = E(nc.sbuf_tensor("ones_col", [P, 1], F32))
        ones_row = E(nc.sbuf_tensor("ones_row", [1, P], F32))
        p_al = E(nc.psum_tensor("p_al", [1, 512], F32))
        p_alb = E(nc.psum_tensor("p_alb", [P, 512], F32))
        blk = E(nc.Block())
        S = {"blk": blk, "ld": Sem(ctx, nc, "ld"), "st": Sem(ctx, nc, "st"),
             "sc": Sem(ctx, nc, "sc"), "ve": Sem(ctx, nc, "ve"),
             "te": Sem(ctx, nc, "te")}

        @blk.vector
        def _(ve):
            ve.memset(ones_col[:, :], 1.0)
            inst = ve.memset(ones_row[:, :], 1.0)
            S["ve"].inc(inst)

        def binarize(wT_ap, bw_ap, nkt, nfree, dtot):
            for f0 in range(0, nfree, 512):
                for kt in range(nkt):
                    @blk.sync
                    def _(sy, kt=kt, f0=f0):
                        S["sc"].wait(sy)
                        S["te"].wait(sy)
                        S["ld"].dma(sy.dma_start(
                            out=wtile[:, 0:512],
                            in_=wT_ap.rearrange("(kt p) f -> p kt f", p=P)[:, kt, f0:f0 + 512]))

                    @blk.scalar
                    def _(sc, kt=kt):
                        S["ld"].wait(sc)
                        inst = sc.activation(scr[:, 0:512], wtile[:, 0:512], AF.Abs)
                        S["sc"].inc(inst)

                    @blk.tensor
                    def _(te, kt=kt, f0=f0):
                        S["sc"].wait(te)
                        inst = te.matmul(p_al[:, 0:512], ones_col[:, :], scr[:, 0:512],
                                         start=(kt == 0), stop=(kt == nkt - 1))
                        S["te"].inc(inst)

                @blk.scalar
                def _(sc, f0=f0):
                    S["te"].wait(sc)
                    inst = sc.mul(alpha[:, f0:f0 + 512], p_al[:, 0:512], 1.0 / dtot)
                    S["sc"].inc(inst)

            for f0 in range(0, nfree, 512):
                @blk.tensor
                def _(te, f0=f0):
                    S["sc"].wait(te)
                    S["ve"].wait(te)
                    inst = te.matmul(p_alb[:, 0:512], ones_row[:, :], alpha[0:1, f0:f0 + 512],
                                     start=True, stop=True)
                    S["te"].inc(inst)

                @blk.vector
                def _(ve, f0=f0):
                    S["te"].wait(ve)
                    inst = ve.tensor_copy(alpha_b[:, f0:f0 + 512], p_alb[:, 0:512])
                    S["ve"].inc(inst)

            for kt in range(nkt):
                @blk.sync
                def _(sy, kt=kt):
                    S["ve"].wait(sy)
                    S["sc"].wait(sy)
                    S["st"].wait(sy)
                    S["ld"].dma(sy.dma_start(
                        out=wtile[:, 0:nfree],
                        in_=wT_ap.rearrange("(kt p) f -> p kt f", p=P)[:, kt, :]))

                @blk.scalar
                def _(sc, kt=kt):
                    S["ld"].wait(sc)
                    inst = sc.activation(scr[:, 0:nfree], wtile[:, 0:nfree], AF.Sign)
                    S["sc"].inc(inst)

                @blk.vector
                def _(ve, kt=kt):
                    S["sc"].wait(ve)
                    inst = ve.tensor_mul(bw_st[:, 0:nfree], scr[:, 0:nfree],
                                         alpha_b[:, 0:nfree])
                    S["ve"].inc(inst)

                @blk.sync
                def _(sy, kt=kt):
                    S["ve"].wait(sy)
                    S["st"].dma(sy.dma_start(
                        out=bw_ap.rearrange("(kt p) f -> p kt f", p=P)[:, kt, :],
                        in_=bw_st[:, 0:nfree]))

        binarize(fw1T, bw1T_o, 8, 4096, D)
        binarize(fw2T, bw2T_o, 32, D, 4096)

        @blk.sync
        def _(sy):
            S["st"].wait(sy)
    return nc


def _t(a):
    return np.ascontiguousarray(a.T)


def _pp(v, ntile):
    return np.ascontiguousarray(v.reshape(ntile, 128).T)


def kernel(**ins):
    cores = list(range(8))
    x = ins["L0_attn_x"]
    x_fm = [_t(x[b]) for b in range(4)]

    ncA = build_launch_A()
    whgT = _t(ins["L0_attn_W_hg"])
    bhgp = _pp(ins["L0_attn_b_hg"], 24)
    im = []
    for c in cores:
        b, h = c // 2, c % 2
        im.append({"x": np.ascontiguousarray(x_fm[b][:, h * TPC:(h + 1) * TPC]),
                   "whgT": whgT, "bhg": bhgp})
    rA = run_bass_kernel_spmd(ncA, im, core_ids=cores).results

    a_full = [np.concatenate([rA[2 * b]["a_out"], rA[2 * b + 1]["a_out"]], 1) for b in range(4)]
    b_full = [np.concatenate([rA[2 * b]["b_out"], rA[2 * b + 1]["b_out"]], 1) for b in range(4)]

    ncB = build_launch_B()
    h0 = ins["L0_attn_h0"]
    im = []
    for c in cores:
        b, h = c // 2, c % 2
        chs = slice(h * 768, (h + 1) * 768)
        im.append({"a_in": np.ascontiguousarray(a_full[b][chs]),
                   "b_in": np.ascontiguousarray(b_full[b][chs]),
                   "h0": np.ascontiguousarray(h0[chs].reshape(6, 128).T)})
    rB = run_bass_kernel_spmd(ncB, im, core_ids=cores).results
    h_full = [np.concatenate([rB[2 * b]["h_out"], rB[2 * b + 1]["h_out"]], 0) for b in range(4)]

    ncC = build_launch_C()
    woutT = _t(ins["L0_attn_W_out"])
    im = []
    for c in cores:
        b, h = c // 2, c % 2
        ts = slice(h * TPC, (h + 1) * TPC)
        im.append({"h_in": np.ascontiguousarray(h_full[b][:, ts]),
                   "x": np.ascontiguousarray(x_fm[b][:, ts]),
                   "woutT": woutT,
                   "bout": _pp(ins["L0_attn_b_out"], 8),
                   "lng": _pp(ins["L0_ffn_ln_g"], 8),
                   "lnb": _pp(ins["L0_ffn_ln_b"], 8),
                   "centT": _t(ins["L0_ffn_centroids"])})
    rC = run_bass_kernel_spmd(ncC, im, core_ids=cores).results

    scores = np.concatenate([rC[c]["sc_o"] for c in cores], axis=1)

    ncD = build_launch_D()
    pgrid, fgrid = np.meshgrid(np.arange(128), np.arange(256), indexing="ij")
    iota = (pgrid * 256 + fgrid)
    im = [{"sc_in": np.ascontiguousarray(scores),
           "triu": np.ascontiguousarray(np.triu(np.ones((128, 128), np.float32), 1)),
           "iota_f": iota.astype(np.float32),
           "iota_i": iota.astype(np.int32),
           "ident": np.eye(128, dtype=np.float32)}] * 8
    rD = run_bass_kernel_spmd(ncD, im, core_ids=cores).results
    t2s_flat = rD[0]["t2s_o"].reshape(-1)
    s2t_flat = rD[0]["s2t_o"].reshape(-1)

    normG = np.concatenate([_t(rC[c]["normbf_o"]) for c in cores], axis=0)
    ncE = build_launch_E()
    w1 = ins["L0_ffn_w1"]; b1 = ins["L0_ffn_b1"]
    w2 = ins["L0_ffn_w2"]; b2 = ins["L0_ffn_b2"]
    im = []
    for c in cores:
        sl = s2t_flat[c * TPC:(c + 1) * TPC]
        im.append({"normG": normG,
                   "s2t": np.ascontiguousarray(sl.reshape(32, 128).T.astype(np.int32)),
                   "w1e": np.ascontiguousarray(w1[c]),
                   "b1p": _pp(b1[c], 32),
                   "w2e": np.ascontiguousarray(w2[c]),
                   "b2p": _pp(b2[c], 8)})
    rE = run_bass_kernel_spmd(ncE, im, core_ids=cores).results

    ffnG = np.concatenate([_t(rE[c]["slab"]) for c in cores], axis=0)
    ncG1 = build_launch_G1()
    wcatT = np.concatenate([
        _t(ins["L1_attn_Wconcept"]), _t(ins["L1_attn_Wcg"]), _t(ins["L1_attn_Wsp"]),
        _t(ins["L1_attn_Wgate"]), _t(ins["L1_attn_Wfg"]), _t(ins["L1_attn_Wband"]),
        _t(ins["L1_attn_Wtg"]), np.zeros((1024, 123), np.float32)], axis=1)
    bcatv = np.concatenate([
        ins["L1_attn_bconcept"], ins["L1_attn_bcg"], np.zeros(1024, np.float32),
        ins["L1_attn_bgate"], ins["L1_attn_bfg"], ins["L1_attn_bband"],
        ins["L1_attn_btg"], np.zeros(123, np.float32)])
    bcatp = _pp(bcatv, 37)
    im = []
    for c in cores:
        tl = t2s_flat[c * TPC:(c + 1) * TPC]
        im.append({"x2": np.ascontiguousarray(rC[c]["x2_o"]),
                   "feat": np.ascontiguousarray(rC[c]["feat_o"]),
                   "ffnG": ffnG,
                   "t2s": np.ascontiguousarray(tl.reshape(32, 128).T.astype(np.int32)),
                   "wcatT": np.ascontiguousarray(wcatT),
                   "bcat": bcatp,
                   "ng": _pp(ins["L1_attn_norm_g"], 8),
                   "nb": _pp(ins["L1_attn_norm_b"], 8)})
    rG1 = run_bass_kernel_spmd(ncG1, im, core_ids=cores).results

    u_re_full = [np.concatenate([rG1[2 * b]["u_re_o"], rG1[2 * b + 1]["u_re_o"]], 1) for b in range(4)]
    u_im_full = [np.concatenate([rG1[2 * b]["u_im_o"], rG1[2 * b + 1]["u_im_o"]], 1) for b in range(4)]
    ts_full = [np.concatenate([rG1[2 * b]["ts_o"], rG1[2 * b + 1]["ts_o"]], 1) for b in range(4)]
    band_full = [np.concatenate([rG1[2 * b]["band_o"], rG1[2 * b + 1]["band_o"]], 1) for b in range(4)]
    ncF = build_launch_F()
    consts = fft_host_consts()
    ld_ = ins["L1_attn_log_decay"]; fi_ = ins["L1_attn_freq_importance"]
    fr_ = ins["L1_attn_frequencies"]; dt_ = ins["L1_attn_dt"]
    iota1 = np.arange(1, SEQ + 1, dtype=np.float32).reshape(1, SEQ)
    im = []
    for c in cores:
        b, fh = c // 2, c % 2
        fs = slice(256 * fh, 256 * (fh + 1))
        d = {"u_re": np.ascontiguousarray(u_re_full[b][fs]),
             "u_im": np.ascontiguousarray(u_im_full[b][fs]),
             "tsrow": np.ascontiguousarray(ts_full[b]),
             "bandr": np.ascontiguousarray(band_full[b][2 * fh:2 * fh + 2].reshape(1, -1)),
             "ld_loc": np.ascontiguousarray(ld_[fs].reshape(1, 256)),
             "fr_loc": np.ascontiguousarray(fr_[fs].reshape(1, 256)),
             "dt_loc": np.ascontiguousarray(dt_[fs].reshape(1, 256)),
             "fi_roll": np.ascontiguousarray(np.roll(fi_, -256 * fh).reshape(1, 512)),
             "iota1": iota1}
        d.update(consts)
        im.append(d)
    rF = run_bass_kernel_spmd(ncF, im, core_ids=cores).results

    y_re_full = [np.concatenate([rF[2 * b]["y_re_o"], rF[2 * b + 1]["y_re_o"]], 0) for b in range(4)]
    y_im_full = [np.concatenate([rF[2 * b]["y_im_o"], rF[2 * b + 1]["y_im_o"]], 0) for b in range(4)]
    ncG2a = build_launch_G2a()
    im = []
    for c in cores:
        b, h = c // 2, c % 2
        ts = slice(h * TPC, (h + 1) * TPC)
        spec = np.concatenate([y_re_full[b][:, ts], y_im_full[b][:, ts]], axis=0)
        im.append({"z_in": np.ascontiguousarray(rG1[c]["z_o"]),
                   "spec": np.ascontiguousarray(spec),
                   "cp": np.ascontiguousarray(rG1[c]["cp_o"]),
                   "gv": np.ascontiguousarray(rG1[c]["gv_o"]),
                   "wfsT": _t(ins["L1_attn_Wfs"]),
                   "osc": np.ascontiguousarray(ins["L1_attn_output_scale"].reshape(1, 1))})
    rG2a = run_bass_kernel_spmd(ncG2a, im, core_ids=cores).results

    ncG2f = build_launch_G2f()
    im = []
    for c in cores:
        im.append({"x3": np.ascontiguousarray(rG1[c]["x3_o"]),
                   "x4": np.ascontiguousarray(rG2a[c]["x4_o"]),
                   "n2g": _pp(ins["L1_attn_norm2_g"], 8),
                   "n2b": _pp(ins["L1_attn_norm2_b"], 8),
                   "w1T": _t(ins["L1_attn_W1"]),
                   "b1p": _pp(ins["L1_attn_b1"], 32),
                   "w2T": _t(ins["L1_attn_W2"]),
                   "b2p": _pp(ins["L1_attn_b2"], 8)})
    rG2f = run_bass_kernel_spmd(ncG2f, im, core_ids=cores).results

    ncBIN = build_launch_BIN()
    im = [{"fw1T": _t(ins["L1_ffn_w1"]), "fw2T": _t(ins["L1_ffn_w2"])}] * 8
    rBIN = run_bass_kernel_spmd(ncBIN, im, core_ids=cores).results
    bw1T = rBIN[0]["bw1T_o"]
    bw2T = rBIN[0]["bw2T_o"]

    ncFFN = build_launch_G2f(w_dtype=BF16)
    zeros_fm = np.zeros((D, TPC), np.float32)
    ones8 = np.ones((P, 8), np.float32)
    zeros8 = np.zeros((P, 8), np.float32)
    im = []
    for c in cores:
        im.append({"x3": zeros_fm,
                   "x4": np.ascontiguousarray(rG2f[c]["x5_o"]),
                   "n2g": ones8, "n2b": zeros8,
                   "w1T": bw1T,
                   "b1p": _pp(ins["L1_ffn_b1"], 32),
                   "w2T": bw2T,
                   "b2p": _pp(ins["L1_ffn_b2"], 8)})
    rFFN = run_bass_kernel_spmd(ncFFN, im, core_ids=cores).results

    out = np.empty((4, SEQ, D), np.float32)
    for c in cores:
        b, h = c // 2, c % 2
        out[b, h * TPC:(h + 1) * TPC, :] = np.asarray(rFFN[c]["x5_o"], np.float32).T
    return out


# version 02
# speedup vs baseline: 1.1834x; 1.1834x over previous
import numpy as np
import ml_dtypes
from contextlib import ExitStack

import concourse.bass as bass
import concourse.mybir as mybir
from concourse.bass_utils import run_bass_kernel_spmd

F32 = mybir.dt.float32
BF16 = mybir.dt.bfloat16
I32 = mybir.dt.int32
AF = mybir.ActivationFunctionType
ALU = mybir.AluOpType
BF = ml_dtypes.bfloat16

P = 128
D = 1024
TPC = 4096
SEQ = 8192
B = 4
NCORE = 8
LN_EPS = 1e-5


class Sem:

    def __init__(self, ctx, nc, name):
        self.h = ctx.enter_context(nc.semaphore(name))
        self.n = 0

    def inc(self, inst, k=1):
        inst.then_inc(self.h, k)
        self.n += k
        return inst

    def dma(self, inst):
        return self.inc(inst, 16)

    def wait(self, eng, lag=0):
        v = self.n - lag
        if v > 0:
            eng.wait_ge(self.h, v)


def fm_ap(dram_ap, d_total, t0, nt):
    return dram_ap[:, t0:t0 + nt].rearrange("(kt p) t -> p kt t", p=P)


def emit_ln(nc, S, xc, kt_n, nt, ones_col, ones_row, p_sum, p_sq, p_mu, p_rstd,
            xsq, murow, qrow, mu2, varrow, rstdrow, eps_ap, mu_b, rstd_b,
            gates=("ld",)):
    blk = S["blk"]
    dtot = kt_n * P

    @blk.scalar
    def _(sc):
        for g in gates:
            S[g].wait(sc)
        for kt in range(kt_n):
            inst = sc.activation(xsq[:, kt, :], xc[:, kt, :], AF.Square)
        S["sc"].inc(inst)

    @blk.tensor
    def _(te):
        for g in gates:
            S[g].wait(te)
        for kt in range(kt_n):
            te.matmul(p_sum[:, :], ones_col[:, :], xc[:, kt, :],
                      start=(kt == 0), stop=(kt == kt_n - 1))
        S["sc"].wait(te)
        for kt in range(kt_n):
            inst = te.matmul(p_sq[:, :], ones_col[:, :], xsq[:, kt, :],
                             start=(kt == 0), stop=(kt == kt_n - 1))
        S["te"].inc(inst)

    @blk.scalar
    def _(sc):
        S["te"].wait(sc)
        sc.mul(murow[:, :], p_sum[:, :], 1.0 / dtot)
        inst = sc.mul(qrow[:, :], p_sq[:, :], 1.0 / dtot)
        S["sc"].inc(inst)

    @blk.vector
    def _(ve):
        S["sc"].wait(ve)
        ve.tensor_mul(mu2[:, :], murow[:, :], murow[:, :])
        inst = ve.tensor_sub(varrow[:, :], qrow[:, :], mu2[:, :])
        S["ve"].inc(inst)

    @blk.scalar
    def _(sc):
        S["ve"].wait(sc)
        sc.activation(varrow[:, :], varrow[:, :], AF.Ln, bias=eps_ap[:1, :])
        inst = sc.activation(rstdrow[:, :], varrow[:, :], AF.Exp, scale=-0.5)
        S["sc"].inc(inst)

    @blk.tensor
    def _(te):
        S["sc"].wait(te)
        te.matmul(p_mu[:, :], ones_row[:, :], murow[:, :], start=True, stop=True)
        inst = te.matmul(p_rstd[:, :], ones_row[:, :], rstdrow[:, :], start=True, stop=True)
        S["te"].inc(inst)

    @blk.vector
    def _(ve):
        S["te"].wait(ve)
        ve.tensor_copy(mu_b[:, :], p_mu[:, :])
        inst = ve.tensor_copy(rstd_b[:, :], p_rstd[:, :])
        S["ve"].inc(inst)


def build_launch_A(nt=512):
    DI = 1536
    HG = 2 * DI
    nch = TPC // nt
    nc = bass.Bass("TRN2", target_bir_lowering=False)
    x = nc.dram_tensor("x", [D, TPC], F32, kind="ExternalInput").ap()
    whgT = nc.dram_tensor("whgT", [D, HG], F32, kind="ExternalInput").ap()
    bhg = nc.dram_tensor("bhg", [P, HG // P], F32, kind="ExternalInput").ap()
    a_out = nc.dram_tensor("a_out", [DI, TPC], F32, kind="ExternalOutput").ap()
    b_out = nc.dram_tensor("b_out", [DI, TPC], F32, kind="ExternalOutput").ap()

    with ExitStack() as ctx:
        E = ctx.enter_context
        w_sb = E(nc.sbuf_tensor("w_sb", [P, 8, HG], BF16))
        bhg_sb = E(nc.sbuf_tensor("bhg_sb", [P, 24], F32))
        bhg_neg = E(nc.sbuf_tensor("bhg_neg", [P, 24], F32))
        bhg_half = E(nc.sbuf_tensor("bhg_half", [P, 24], F32))
        xc = E(nc.sbuf_tensor("xc", [P, 8, nt], F32))
        xsq = E(nc.sbuf_tensor("xsq", [P, 8, nt], F32))
        xnb = E(nc.sbuf_tensor("xnb", [P, 8, nt], BF16))
        ones_col = E(nc.sbuf_tensor("ones_col", [P, 1], F32))
        ones_row = E(nc.sbuf_tensor("ones_row", [1, P], F32))
        eps_ap = E(nc.sbuf_tensor("eps_ap", [1, 1], F32))
        murow = E(nc.sbuf_tensor("murow", [1, nt], F32))
        qrow = E(nc.sbuf_tensor("qrow", [1, nt], F32))
        mu2 = E(nc.sbuf_tensor("mu2", [1, nt], F32))
        varrow = E(nc.sbuf_tensor("varrow", [1, nt], F32))
        rstdrow = E(nc.sbuf_tensor("rstdrow", [1, nt], F32))
        mu_b = E(nc.sbuf_tensor("mu_b", [P, nt], F32))
        rstd_b = E(nc.sbuf_tensor("rstd_b", [P, nt], F32))
        sig_g = E(nc.sbuf_tensor("sig_g", [P, nt], F32))
        sig_h = E(nc.sbuf_tensor("sig_h", [P, nt], F32))
        lin_h = E(nc.sbuf_tensor("lin_h", [P, nt], F32))
        mask_h = E(nc.sbuf_tensor("mask_h", [P, nt], mybir.dt.uint8))
        g_h = E(nc.sbuf_tensor("g_h", [P, nt], F32))
        a_st = E(nc.sbuf_tensor("a_st", [P, nt], F32))
        b_st = E(nc.sbuf_tensor("b_st", [P, nt], F32))

        p_sum = E(nc.psum_tensor("p_sum", [1, nt], F32))
        p_sq = E(nc.psum_tensor("p_sq", [1, nt], F32))
        p_mu = E(nc.psum_tensor("p_mu", [P, nt], F32))
        p_rstd = E(nc.psum_tensor("p_rstd", [P, nt], F32))
        p_h = E(nc.psum_tensor("p_h", [P, nt], F32))
        p_g = E(nc.psum_tensor("p_g", [P, nt], F32))

        blk = E(nc.Block())
        S = {
            "blk": blk,
            "ld": Sem(ctx, nc, "ld"), "st": Sem(ctx, nc, "st"),
            "sc": Sem(ctx, nc, "sc"), "ve": Sem(ctx, nc, "ve"),
            "te": Sem(ctx, nc, "te"),
        }

        @blk.gpsimd
        def _(gp):
            S["ld"].dma(gp.dma_start(
                out=w_sb[:, :, :], in_=whgT.rearrange("(kt p) c -> p kt c", p=P)))

        @blk.sync
        def _(sy):
            S["ld"].dma(sy.dma_start(out=bhg_sb[:, :], in_=bhg))

        @blk.vector
        def _(ve):
            ve.memset(ones_col[:, :], 1.0)
            ve.memset(ones_row[:, :], 1.0)
            ve.memset(eps_ap[:, :], LN_EPS)
            S["ld"].wait(ve)
            ve.tensor_scalar_mul(bhg_neg[:, :], bhg_sb[:, :], -1.0)
            inst = ve.tensor_scalar_add(bhg_half[:, :], bhg_sb[:, :], 0.5)
            S["ve"].inc(inst)

        for ci in range(nch):
            t0 = ci * nt

            @blk.sync
            def _(sy):
                S["st"].wait(sy)
                S["ve"].wait(sy)
                S["sc"].wait(sy)
                S["te"].wait(sy)
                S["ld"].dma(sy.dma_start(out=xc[:, :, :], in_=fm_ap(x, D, t0, nt)))

            emit_ln(nc, S, xc, 8, nt, ones_col, ones_row, p_sum, p_sq, p_mu, p_rstd,
                    xsq, murow, qrow, mu2, varrow, rstdrow, eps_ap, mu_b, rstd_b)

            @blk.vector
            def _(ve):
                S["ve"].wait(ve)
                for kt in range(8):
                    ve.tensor_sub(xsq[:, kt, :], xc[:, kt, :], mu_b[:, :])
                    inst = ve.tensor_mul(xnb[:, kt, :], xsq[:, kt, :], rstd_b[:, :])
                S["ve"].inc(inst)

            for j in range(12):

                @blk.tensor
                def _(te, j=j):
                    S["ve"].wait(te)
                    if j == 0:
                        S["sc"].wait(te)
                    for kt in range(8):
                        te.matmul(p_h[:, :], w_sb[:, kt, j * P:(j + 1) * P],
                                  xnb[:, kt, :], start=(kt == 0), stop=(kt == 7))
                    for kt in range(8):
                        inst = te.matmul(p_g[:, :], w_sb[:, kt, (12 + j) * P:(13 + j) * P],
                                         xnb[:, kt, :], start=(kt == 0), stop=(kt == 7))
                    S["te"].inc(inst)

                @blk.scalar
                def _(sc, j=j):
                    S["te"].wait(sc)
                    S["st"].wait(sc)
                    sc.activation(sig_g[:, :], p_g[:, :], AF.Sigmoid,
                                  bias=bhg_sb[:, 12 + j:13 + j])
                    sc.activation(a_st[:, :], p_g[:, :], AF.Sigmoid,
                                  bias=bhg_neg[:, 12 + j:13 + j], scale=-1.0)
                    inst = sc.activation(sig_h[:, :], p_h[:, :], AF.Sigmoid,
                                         bias=bhg_sb[:, j:j + 1])
                    S["sc"].inc(inst)

                @blk.vector
                def _(ve, j=j):
                    S["sc"].wait(ve)
                    ve.tensor_scalar_add(lin_h[:, :], p_h[:, :], bhg_half[:, j:j + 1])
                    ve.tensor_scalar(mask_h[:, :], p_h[:, :],
                                     bhg_neg[:, j:j + 1], None,
                                     op0=ALU.is_ge)
                    ve.select(g_h[:, :], mask_h[:, :], lin_h[:, :], sig_h[:, :])
                    inst = ve.tensor_mul(b_st[:, :], sig_g[:, :], g_h[:, :])
                    S["ve"].inc(inst)

                @blk.sync
                def _(sy, j=j, t0=t0):
                    S["ve"].wait(sy)
                    S["st"].dma(sy.dma_start(
                        out=a_out[j * P:(j + 1) * P, t0:t0 + nt], in_=a_st[:, :]))
                    S["st"].dma(sy.dma_start(
                        out=b_out[j * P:(j + 1) * P, t0:t0 + nt], in_=b_st[:, :]))

        @blk.sync
        def _(sy):
            S["st"].wait(sy)
    return nc




def build_launch_B():
    CH = 768
    NJ = CH // P
    nc = bass.Bass("TRN2", target_bir_lowering=False)
    a_in = nc.dram_tensor("a_in", [CH, SEQ], F32, kind="ExternalInput").ap()
    b_in = nc.dram_tensor("b_in", [CH, SEQ], F32, kind="ExternalInput").ap()
    h0 = nc.dram_tensor("h0", [P, NJ], F32, kind="ExternalInput").ap()
    h_out = nc.dram_tensor("h_out", [CH, SEQ], F32, kind="ExternalOutput").ap()

    with ExitStack() as ctx:
        E = ctx.enter_context
        s_a = E(nc.sbuf_tensor("s_a", [P, SEQ], F32))
        s_b = E(nc.sbuf_tensor("s_b", [P, SEQ], F32))
        s_h = E(nc.sbuf_tensor("s_h", [P, SEQ], F32))
        s_h0 = E(nc.sbuf_tensor("s_h0", [P, NJ], F32))
        s_sig = E(nc.sbuf_tensor("s_sig", [P, NJ], F32))
        s_lin = E(nc.sbuf_tensor("s_lin", [P, NJ], F32))
        s_msk = E(nc.sbuf_tensor("s_msk", [P, NJ], mybir.dt.uint8))
        s_h0g = E(nc.sbuf_tensor("s_h0g", [P, NJ], F32))
        blk = E(nc.Block())
        S = {"blk": blk, "ld": Sem(ctx, nc, "ld"), "st": Sem(ctx, nc, "st"),
             "sc": Sem(ctx, nc, "sc"), "ve": Sem(ctx, nc, "ve")}

        @blk.sync
        def _(sy):
            S["ld"].dma(sy.dma_start(out=s_h0[:, :], in_=h0))

        @blk.scalar
        def _(sc):
            S["ld"].wait(sc)
            inst = sc.activation(s_sig[:, :], s_h0[:, :], AF.Sigmoid)
            S["sc"].inc(inst)

        @blk.vector
        def _(ve):
            S["sc"].wait(ve)
            ve.tensor_scalar_add(s_lin[:, :], s_h0[:, :], 0.5)
            ve.tensor_scalar(s_msk[:, :], s_h0[:, :], 0.0, None, op0=ALU.is_ge)
            inst = ve.select(s_h0g[:, :], s_msk[:, :], s_lin[:, :], s_sig[:, :])
            S["ve"].inc(inst)

        for j in range(NJ):
            @blk.sync
            def _(sy, j=j):
                S["ve"].wait(sy)
                S["st"].wait(sy)
                S["ld"].dma(sy.dma_start(out=s_a[:, :], in_=a_in[j * P:(j + 1) * P, :]))
                S["ld"].dma(sy.dma_start(out=s_b[:, :], in_=b_in[j * P:(j + 1) * P, :]))

            @blk.vector
            def _(ve, j=j):
                S["ld"].wait(ve)
                inst = ve.tensor_tensor_scan(
                    out=s_h[:, :], data0=s_a[:, :], data1=s_b[:, :],
                    initial=s_h0g[:, j:j + 1], op0=ALU.mult, op1=ALU.add)
                S["ve"].inc(inst)

            @blk.sync
            def _(sy, j=j):
                S["ve"].wait(sy)
                S["st"].dma(sy.dma_start(out=h_out[j * P:(j + 1) * P, :], in_=s_h[:, :]))

        @blk.sync
        def _(sy):
            S["st"].wait(sy)
    return nc


def build_launch_C(nt=512, nch=None, stage=3):
    DI = 1536
    if nch is None:
        nch = TPC // nt
    nc = bass.Bass("TRN2", target_bir_lowering=False)
    h_in = nc.dram_tensor("h_in", [DI, TPC], F32, kind="ExternalInput").ap()
    x = nc.dram_tensor("x", [D, TPC], F32, kind="ExternalInput").ap()
    woutT = nc.dram_tensor("woutT", [DI, D], F32, kind="ExternalInput").ap()
    bout = nc.dram_tensor("bout", [P, 8], F32, kind="ExternalInput").ap()
    lng = nc.dram_tensor("lng", [P, 8], F32, kind="ExternalInput").ap()
    lnb = nc.dram_tensor("lnb", [P, 8], F32, kind="ExternalInput").ap()
    centT = nc.dram_tensor("centT", [D, 8], F32, kind="ExternalInput").ap()
    x2_o = nc.dram_tensor("x2_o", [D, TPC], F32, kind="ExternalOutput").ap()
    feat_o = nc.dram_tensor("feat_o", [D, TPC], F32, kind="ExternalOutput").ap()
    norm_o = nc.dram_tensor("norm_o", [D, TPC], F32, kind="ExternalOutput").ap()
    normbf_o = nc.dram_tensor("normbf_o", [D, TPC], BF16, kind="ExternalOutput").ap()
    sc_o = nc.dram_tensor("sc_o", [8, TPC], F32, kind="ExternalOutput").ap()

    with ExitStack() as ctx:
        E = ctx.enter_context
        w_sb = E(nc.sbuf_tensor("w_sb", [P, 12, D], BF16))
        cent_sb = E(nc.sbuf_tensor("cent_sb", [P, 8, 8], F32))
        bout_sb = E(nc.sbuf_tensor("bout_sb", [P, 8], F32))
        lng_sb = E(nc.sbuf_tensor("lng_sb", [P, 8], F32))
        lnb_sb = E(nc.sbuf_tensor("lnb_sb", [P, 8], F32))
        hc = E(nc.sbuf_tensor("hc", [P, 12, nt], BF16))
        xc = E(nc.sbuf_tensor("xc", [P, 8, nt], F32))
        x2c = E(nc.sbuf_tensor("x2c", [P, 8, nt], F32))
        featc = E(nc.sbuf_tensor("featc", [P, 8, nt], F32))
        normc = E(nc.sbuf_tensor("normc", [P, 8, nt], F32))
        normb = E(nc.sbuf_tensor("normb", [P, 8, nt], BF16))
        xsq = E(nc.sbuf_tensor("xsq", [P, 8, nt], F32))
        ones_col = E(nc.sbuf_tensor("ones_col", [P, 1], F32))
        ones_row = E(nc.sbuf_tensor("ones_row", [1, P], F32))
        eps_ap = E(nc.sbuf_tensor("eps_ap", [1, 1], F32))
        murow = E(nc.sbuf_tensor("murow", [1, nt], F32))
        qrow = E(nc.sbuf_tensor("qrow", [1, nt], F32))
        mu2 = E(nc.sbuf_tensor("mu2", [1, nt], F32))
        varrow = E(nc.sbuf_tensor("varrow", [1, nt], F32))
        rstdrow = E(nc.sbuf_tensor("rstdrow", [1, nt], F32))
        mu_b = E(nc.sbuf_tensor("mu_b", [P, nt], F32))
        rstd_b = E(nc.sbuf_tensor("rstd_b", [P, nt], F32))
        sc_st = E(nc.sbuf_tensor("sc_st", [8, nt], F32))

        p_y = E(nc.psum_tensor("p_y", [P, nt], F32))
        p_sum = E(nc.psum_tensor("p_sum", [1, nt], F32))
        p_sq = E(nc.psum_tensor("p_sq", [1, nt], F32))
        p_mu = E(nc.psum_tensor("p_mu", [P, nt], F32))
        p_rstd = E(nc.psum_tensor("p_rstd", [P, nt], F32))
        p_sc = E(nc.psum_tensor("p_sc", [8, nt], F32))

        blk = E(nc.Block())
        S = {"blk": blk, "ld": Sem(ctx, nc, "ld"), "st": Sem(ctx, nc, "st"),
             "sc": Sem(ctx, nc, "sc"), "ve": Sem(ctx, nc, "ve"),
             "te": Sem(ctx, nc, "te")}

        @blk.gpsimd
        def _(gp):
            S["ld"].dma(gp.dma_start(
                out=w_sb[:, :, :], in_=woutT.rearrange("(ct p) d -> p ct d", p=P)))

        @blk.sync
        def _(sy):
            S["ld"].dma(sy.dma_start(
                out=cent_sb[:, :, :], in_=centT.rearrange("(kt p) e -> p kt e", p=P)))
            S["ld"].dma(sy.dma_start(out=bout_sb[:, :], in_=bout))
            S["ld"].dma(sy.dma_start(out=lng_sb[:, :], in_=lng))
            S["ld"].dma(sy.dma_start(out=lnb_sb[:, :], in_=lnb))

        @blk.vector
        def _(ve):
            ve.memset(ones_col[:, :], 1.0)
            ve.memset(ones_row[:, :], 1.0)
            inst = ve.memset(eps_ap[:, :], LN_EPS)
            S["ve"].inc(inst)

        for ci in range(nch):
            t0 = ci * nt

            @blk.sync
            def _(sy):
                S["st"].wait(sy)
                S["ve"].wait(sy)
                S["sc"].wait(sy)
                S["te"].wait(sy)
                S["ld"].dma(sy.dma_start(out=xc[:, :, :], in_=fm_ap(x, D, t0, nt)))

            @blk.gpsimd
            def _(gp, t0=t0):
                S["st"].wait(gp)
                S["ve"].wait(gp)
                S["ld"].dma(gp.dma_start(
                    out=hc[:, :, :],
                    in_=h_in[:, t0:t0 + nt].rearrange("(ct p) t -> p ct t", p=P)))

            for dt in range(8):
                @blk.tensor
                def _(te, dt=dt):
                    S["ld"].wait(te)
                    S["ve"].wait(te)
                    for ctj in range(12):
                        inst = te.matmul(p_y[:, :], w_sb[:, ctj, dt * P:(dt + 1) * P],
                                         hc[:, ctj, :], start=(ctj == 0), stop=(ctj == 11))
                    S["te"].inc(inst)

                @blk.vector
                def _(ve, dt=dt):
                    S["te"].wait(ve)
                    inst = ve.scalar_tensor_tensor(
                        out=x2c[:, dt, :], in0=p_y[:, :], scalar=bout_sb[:, dt:dt + 1],
                        in1=xc[:, dt, :], op0=ALU.add, op1=ALU.add)
                    S["ve"].inc(inst)

            if stage < 1:
                @blk.sync
                def _(sy, t0=t0):
                    S["ve"].wait(sy)
                    S["st"].dma(sy.dma_start(out=fm_ap(x2_o, D, t0, nt), in_=x2c[:, :, :]))
                continue

            emit_ln(nc, S, x2c, 8, nt, ones_col, ones_row, p_sum, p_sq, p_mu, p_rstd,
                    xsq, murow, qrow, mu2, varrow, rstdrow, eps_ap, mu_b, rstd_b,
                    gates=("ve",))

            @blk.vector
            def _(ve):
                S["ve"].wait(ve)
                for kt in range(8):
                    ve.tensor_sub(xsq[:, kt, :], x2c[:, kt, :], mu_b[:, :])
                    inst = ve.tensor_mul(featc[:, kt, :], xsq[:, kt, :], rstd_b[:, :])
                S["ve"].inc(inst)

            if stage < 2:
                @blk.sync
                def _(sy, t0=t0):
                    S["ve"].wait(sy)
                    S["st"].dma(sy.dma_start(out=fm_ap(x2_o, D, t0, nt), in_=x2c[:, :, :]))
                    S["st"].dma(sy.dma_start(out=fm_ap(feat_o, D, t0, nt), in_=featc[:, :, :]))
                continue

            emit_ln(nc, S, featc, 8, nt, ones_col, ones_row, p_sum, p_sq, p_mu, p_rstd,
                    xsq, murow, qrow, mu2, varrow, rstdrow, eps_ap, mu_b, rstd_b,
                    gates=("ve",))

            @blk.vector
            def _(ve):
                S["ve"].wait(ve)
                for kt in range(8):
                    ve.tensor_sub(xsq[:, kt, :], featc[:, kt, :], mu_b[:, :])
                    ve.tensor_mul(xsq[:, kt, :], xsq[:, kt, :], rstd_b[:, :])
                    ve.tensor_scalar(
                        normc[:, kt, :], xsq[:, kt, :],
                        lng_sb[:, kt:kt + 1], lnb_sb[:, kt:kt + 1],
                        op0=ALU.mult, op1=ALU.add)
                    inst = ve.tensor_copy(normb[:, kt, :], normc[:, kt, :])
                S["ve"].inc(inst)

            @blk.tensor
            def _(te):
                S["ve"].wait(te)
                S["sc"].wait(te)
                for kt in range(8):
                    inst = te.matmul(p_sc[:, :], cent_sb[:, kt, :], normc[:, kt, :],
                                     start=(kt == 0), stop=(kt == 7))
                S["te"].inc(inst)

            @blk.scalar
            def _(sc):
                S["te"].wait(sc)
                inst = sc.copy(sc_st[:, :], p_sc[:, :])
                S["sc"].inc(inst)

            @blk.sync
            def _(sy, t0=t0):
                S["ve"].wait(sy)
                S["sc"].wait(sy)
                S["st"].dma(sy.dma_start(out=fm_ap(x2_o, D, t0, nt), in_=x2c[:, :, :]))
                S["st"].dma(sy.dma_start(out=fm_ap(feat_o, D, t0, nt), in_=featc[:, :, :]))
                S["st"].dma(sy.dma_start(out=fm_ap(norm_o, D, t0, nt), in_=normc[:, :, :]))
                S["st"].dma(sy.dma_start(out=fm_ap(normbf_o, D, t0, nt), in_=normb[:, :, :]))
                S["st"].dma(sy.dma_start(out=sc_o[:, t0:t0 + nt], in_=sc_st[:, :]))

        @blk.sync
        def _(sy):
            S["st"].wait(sy)
    return nc


def build_launch_D(niter=32):
    T = 32768
    FE = T // P
    CAP = 4096.0
    NEG = -3.0e38
    nc = bass.Bass("TRN2", target_bir_lowering=False)
    sc_in = nc.dram_tensor("sc_in", [8, T], F32, kind="ExternalInput").ap()
    triu = nc.dram_tensor("triu", [P, P], F32, kind="ExternalInput").ap()
    iota_f = nc.dram_tensor("iota_f", [P, FE], F32, kind="ExternalInput").ap()
    iota_i = nc.dram_tensor("iota_i", [P, FE], I32, kind="ExternalInput").ap()
    t2s_o = nc.dram_tensor("t2s_o", [P, FE], I32, kind="ExternalOutput").ap()
    s2t_o = nc.dram_tensor("s2t_o", [T, 1], I32, kind="ExternalOutput").ap()
    ident = nc.dram_tensor("ident", [P, P], F32, kind="ExternalInput").ap()
    bmax_d = nc.dram_tensor("bmax_d", [P, 1], F32, kind="Internal").ap()
    bmin_d = nc.dram_tensor("bmin_d", [P, 1], F32, kind="Internal").ap()
    m_out = nc.dram_tensor("m_out", [P, 8 * FE], F32, kind="ExternalOutput").ap()
    dbg_o = nc.dram_tensor("dbg_o", [P, 40], F32, kind="ExternalOutput").ap()

    with ExitStack() as ctx:
        E = ctx.enter_context
        s_sc = E(nc.sbuf_tensor("s_sc", [P, 8, FE], F32))
        s_m = E(nc.sbuf_tensor("s_m", [P, 8, FE], F32))
        s_cur = E(nc.sbuf_tensor("s_cur", [P, FE], F32))
        s_taken = E(nc.sbuf_tensor("s_taken", [P, FE], mybir.dt.uint8))
        s_tkf = E(nc.sbuf_tensor("s_tkf", [P, FE], F32))
        s_id = E(nc.sbuf_tensor("s_id", [P, P], F32))
        s_neg = E(nc.sbuf_tensor("s_neg", [P, FE], F32))
        s_bounds = E(nc.sbuf_tensor("s_bounds", [P, 2], F32))
        junk = E(nc.sbuf_tensor("junk", [P, FE], F32))
        cnt = E(nc.sbuf_tensor("cnt", [P, 1], F32))
        tot_s = E(nc.sbuf_tensor("tot_s", [1, 1], F32))
        lo = E(nc.sbuf_tensor("lo", [P, 1], F32))
        hi = E(nc.sbuf_tensor("hi", [P, 1], F32))
        mid = E(nc.sbuf_tensor("mid", [P, 1], F32))
        lo_n = E(nc.sbuf_tensor("lo_n", [P, 1], F32))
        mid_ptr = E(nc.sbuf_tensor("mid_ptr", [P, 1], F32))
        hi_n = E(nc.sbuf_tensor("hi_n", [P, 1], F32))
        geb = E(nc.sbuf_tensor("geb", [P, 1], mybir.dt.uint8))
        ones_col = E(nc.sbuf_tensor("ones_col", [P, 1], F32))
        ones_row = E(nc.sbuf_tensor("ones_row", [1, P], F32))
        rowred = E(nc.sbuf_tensor("rowred", [P, 2], F32))
        glob = E(nc.sbuf_tensor("glob", [1, 2], F32))
        brow = E(nc.sbuf_tensor("brow", [1, 2 * P], F32))
        glob2 = E(nc.sbuf_tensor("glob2", [1, 1], F32))
        dbg_sb = E(nc.sbuf_tensor("dbg_sb", [P, 40], F32))
        s_triu = E(nc.sbuf_tensor("s_triu", [P, P], F32))
        s_iota = E(nc.sbuf_tensor("s_iota", [P, FE], F32))
        s_iota_i = E(nc.sbuf_tensor("s_iota_i", [P, FE], I32))
        s_one256 = E(nc.sbuf_tensor("s_one256", [P, FE], F32))
        s_cums = E(nc.sbuf_tensor("s_cums", [P, FE], F32))
        s_rowtot = E(nc.sbuf_tensor("s_rowtot", [P, 1], F32))
        s_pre = E(nc.sbuf_tensor("s_pre", [P, 1], F32))
        s_t2s = E(nc.sbuf_tensor("s_t2s", [P, FE], F32))
        s_t2si = E(nc.sbuf_tensor("s_t2si", [P, FE], I32))
        s_tmp = E(nc.sbuf_tensor("s_tmp", [P, FE], F32))
        p_pre = E(nc.psum_tensor("p_pre", [P, 1], F32))
        p_row = E(nc.psum_tensor("p_row", [1, 256], F32))
        p_bc = E(nc.psum_tensor("p_bc", [P, 2], F32))
        p_cntb = E(nc.psum_tensor("p_cntb", [P, 1], F32))

        blk = E(nc.Block())
        S = {"blk": blk, "ld": Sem(ctx, nc, "ld"), "st": Sem(ctx, nc, "st"),
             "ve": Sem(ctx, nc, "ve"), "te": Sem(ctx, nc, "te"),
             "sc": Sem(ctx, nc, "sc"), "vv": Sem(ctx, nc, "vv")}

        @blk.sync
        def _(sy):
            S["ld"].dma(sy.dma_start(
                out=s_sc[:, :, :], in_=sc_in.rearrange("e (p f) -> p e f", p=P)))
            S["ld"].dma(sy.dma_start(out=s_id[:, :], in_=ident))
            S["ld"].dma(sy.dma_start(out=s_triu[:, :], in_=triu))
            S["ld"].dma(sy.dma_start(out=s_iota[:, :], in_=iota_f))
            S["ld"].dma(sy.dma_start(out=s_iota_i[:, :], in_=iota_i))

        @blk.vector
        def _(ve):
            ve.memset(ones_col[:, :], 1.0)
            ve.memset(ones_row[:, :], 1.0)
            ve.memset(s_taken[:, :], 0)
            ve.memset(s_tkf[:, :], 0.0)
            ve.memset(s_neg[:, :], NEG)
            S["ld"].wait(ve)
            flat = s_sc[:, :, :].rearrange("p e f -> p (e f)")
            ve.tensor_reduce(rowred[:, 0:1], flat, mybir.AxisListType.X, ALU.max)
            inst = ve.tensor_reduce(rowred[:, 1:2], flat, mybir.AxisListType.X, ALU.min)
            S["ve"].inc(inst)

        @blk.vector
        def _(ve):
            S["ve"].wait(ve)
            inst = ve.tensor_scalar_mul(rowred[:, 1:2], rowred[:, 1:2], -1.0)
            S["ve"].inc(inst)

        @blk.sync
        def _(sy):
            S["ve"].wait(sy)
            S["st"].dma(sy.dma_start(out=bmax_d, in_=rowred[:, 0:1]))
            S["st"].dma(sy.dma_start(out=bmin_d, in_=rowred[:, 1:2]))

        @blk.sync
        def _(sy):
            S["st"].wait(sy)
            S["ld"].dma(sy.dma_start(out=brow[0:1, 0:P], in_=bmax_d.rearrange("p one -> (p one)")[None, :]))
            S["ld"].dma(sy.dma_start(out=brow[0:1, P:2 * P], in_=bmin_d.rearrange("p one -> (p one)")[None, :]))

        @blk.vector
        def _(ve):
            S["ld"].wait(ve)
            ve.tensor_reduce(glob[0:1, 0:1], brow[0:1, 0:P], mybir.AxisListType.X, ALU.max)
            inst = ve.tensor_reduce(glob2[0:1, 0:1], brow[0:1, P:2 * P], mybir.AxisListType.X, ALU.max)
            S["ve"].inc(inst)

        @blk.tensor
        def _(te):
            S["ve"].wait(te)
            te.matmul(p_bc[:, 0:1], ones_row[:, :], glob[0:1, 0:1], start=True, stop=True)
            inst = te.matmul(p_bc[:, 1:2], ones_row[:, :], glob2[0:1, 0:1], start=True, stop=True)
            S["te"].inc(inst)

        @blk.vector
        def _(ve):
            S["te"].wait(ve)
            ve.tensor_scalar_add(s_bounds[:, 0:1], p_bc[:, 0:1], 1.0)
            ve.tensor_scalar(s_bounds[:, 1:2], p_bc[:, 1:2], -1.0, -1.0,
                             op0=ALU.mult, op1=ALU.add)
            inst = ve.tensor_copy(dbg_sb[:, 32:34], s_bounds[:, :])
            S["ve"].inc(inst)

        for e in range(8):
            @blk.vector
            def _(ve, e=e):
                S["ve"].wait(ve)
                ve.select(s_cur[:, :], s_taken[:, :], s_neg[:, :], s_sc[:, e, :])
                ve.tensor_copy(hi[:, :], s_bounds[:, 0:1])
                inst = ve.tensor_copy(lo[:, :], s_bounds[:, 1:2])
                S["ve"].inc(inst)

            for it in range(niter):
                @blk.vector
                def _(ve):
                    S["ve"].wait(ve)
                    S["te"].wait(ve)
                    S["vv"].inc(ve.tensor_add(mid[:, :], lo[:, :], hi[:, :]))
                    S["vv"].wait(ve)
                    S["vv"].inc(ve.tensor_scalar_mul(mid[:, :], mid[:, :], 0.5))
                    S["vv"].wait(ve)
                    inst = ve.tensor_scalar(junk[:, :], s_cur[:, :], mid[:, :], None,
                                            op0=ALU.is_gt)
                    S["ve"].inc(inst)

                @blk.tensor
                def _(te):
                    S["ve"].wait(te)
                    inst = te.matmul(p_row[:, :], ones_col[:, :], junk[:, :],
                                     start=True, stop=True)
                    S["te"].inc(inst)

                @blk.vector
                def _(ve):
                    S["te"].wait(ve)
                    inst = ve.tensor_reduce(tot_s[:, :], p_row[:, :],
                                            mybir.AxisListType.X, ALU.add)
                    S["ve"].inc(inst)

                @blk.tensor
                def _(te):
                    S["ve"].wait(te)
                    inst = te.matmul(p_cntb[:, :], ones_row[:, :], tot_s[:, :],
                                     start=True, stop=True)
                    S["te"].inc(inst)

                @blk.vector
                def _(ve):
                    S["te"].wait(ve)
                    S["vv"].inc(ve.tensor_scalar(geb[:, :], p_cntb[:, :], CAP, None,
                                                 op0=ALU.is_ge))
                    S["vv"].wait(ve)
                    ve.select(lo_n[:, :], geb[:, :], mid[:, :], lo[:, :])
                    S["vv"].inc(ve.select(hi_n[:, :], geb[:, :], hi[:, :], mid[:, :]))
                    S["vv"].wait(ve)
                    ve.tensor_copy(lo[:, :], lo_n[:, :])
                    inst = ve.tensor_copy(hi[:, :], hi_n[:, :])
                    S["ve"].inc(inst)

            @blk.vector
            def _(ve, e=e):
                S["ve"].wait(ve)
                ve.tensor_copy(dbg_sb[:, 3 * e:3 * e + 1], lo[:, :])
                ve.tensor_copy(dbg_sb[:, 3 * e + 1:3 * e + 2], hi[:, :])
                ve.tensor_copy(dbg_sb[:, 3 * e + 2:3 * e + 3], p_cntb[:, :])
                ve.tensor_scalar(s_m[:, e, :], s_cur[:, :], lo[:, :], None, op0=ALU.is_gt)
                ve.tensor_add(s_tkf[:, :], s_tkf[:, :], s_m[:, e, :])
                inst = ve.tensor_scalar(s_taken[:, :], s_tkf[:, :], 0.5, None, op0=ALU.is_gt)
                S["ve"].inc(inst)

        @blk.vector
        def _(ve):
            S["ve"].wait(ve)
            ve.memset(s_one256[:, :], 1.0)
            inst = ve.memset(s_t2s[:, :], 0.0)
            S["ve"].inc(inst)

        for e in range(8):
            @blk.vector
            def _(ve, e=e):
                S["ve"].wait(ve)
                S["te"].wait(ve)
                S["vv"].inc(ve.tensor_tensor_scan(
                    out=s_cums[:, :], data0=s_one256[:, :], data1=s_m[:, e, :],
                    initial=0.0, op0=ALU.mult, op1=ALU.add))
                S["vv"].wait(ve)
                inst = ve.tensor_copy(s_rowtot[:, :], s_cums[:, FE - 1:FE])
                S["ve"].inc(inst)

            @blk.tensor
            def _(te):
                S["ve"].wait(te)
                inst = te.matmul(p_pre[:, :], s_triu[:, :], s_rowtot[:, :],
                                 start=True, stop=True)
                S["te"].inc(inst)

            @blk.vector
            def _(ve, e=e):
                S["te"].wait(ve)
                S["vv"].inc(ve.tensor_copy(s_pre[:, :], p_pre[:, :]))
                S["vv"].wait(ve)
                ve.tensor_scalar_add(s_tmp[:, :], s_cums[:, :], float(4096 * e - 1))
                S["vv"].inc(ve.tensor_scalar_add(s_tmp[:, :], s_tmp[:, :], s_pre[:, :]))
                S["vv"].wait(ve)
                ve.tensor_mul(s_tmp[:, :], s_tmp[:, :], s_m[:, e, :])
                inst = ve.tensor_add(s_t2s[:, :], s_t2s[:, :], s_tmp[:, :])
                S["ve"].inc(inst)

        @blk.vector
        def _(ve):
            S["ve"].wait(ve)
            inst = ve.tensor_copy(s_t2si[:, :], s_t2s[:, :])
            S["ve"].inc(inst)

        @blk.sync
        def _(sy):
            S["ve"].wait(sy)
            S["st"].dma(sy.dma_start(
                out=m_out.rearrange("p (e f) -> p e f", e=8), in_=s_m[:, :, :]))
            S["st"].dma(sy.dma_start(out=dbg_o, in_=dbg_sb[:, :]))
            S["st"].dma(sy.dma_start(out=t2s_o, in_=s_t2si[:, :]))

        @blk.gpsimd
        def _(gp):
            S["ve"].wait(gp)
            from concourse.bass_core_rust import IndirectOffsetOnAxis
            for f in range(FE):
                S["st"].dma(gp.indirect_dma_start(
                    out=s2t_o,
                    out_offset=IndirectOffsetOnAxis(ap=s_t2si[:, f:f + 1], axis=0),
                    in_=s_iota_i[:, f:f + 1], in_offset=None))

        @blk.sync
        def _(sy):
            sy.wait_ge(S["st"].h, S["st"].n)
    return nc


def _pair_core(c):
    return c // 2, c % 2


def kernel(**inputs):
    raise NotImplementedError("assembled incrementally; see tests")


N_FFT = 16384
R_F = 128


def fft_host_consts():
    k = np.arange(R_F)
    aa, bb = np.meshgrid(k, k, indexing="ij")
    w128 = np.exp(-2j * np.pi * aa * bb / R_F)
    bgrid, kgrid = np.meshgrid(k, k, indexing="ij")
    tw = np.exp(-2j * np.pi * bgrid * kgrid / N_FFT)
    twc = np.conj(tw) / N_FFT
    w128c = np.conj(w128)

    def bf(x):
        return np.ascontiguousarray(x.astype(np.float32)).astype(BF)

    t4 = np.tile(tw, (1, 4))
    tc4 = np.tile(twc, (1, 4))
    return {
        "f_re": bf(w128.real), "f_im": bf(w128.imag), "f_imN": bf(-w128.imag),
        "fc_re": bf(w128c.real), "fc_im": bf(w128c.imag), "fc_imN": bf(-w128c.imag),
        "t_re": bf(t4.real), "t_im": bf(t4.imag),
        "tc_re": bf(tc4.real), "tc_im": bf(tc4.imag),
    }


def emit_fft4(S, te_c, ve_c, xre, xim, C, ps, sb, out_re, out_im, zero_pad=True):
    blk = S["blk"]
    KA = 64 if zero_pad else 128

    @blk.tensor
    def _(te):
        te_c(te)
        for c in range(4):
            sl = slice(128 * c, 128 * (c + 1))
            te.matmul(ps["pA_re"][:, sl], xre[:, c, :], C["f_re"][0:KA, :], start=True, stop=False)
            te.matmul(ps["pA_re"][:, sl], xim[:, c, :], C["f_imN"][0:KA, :], start=False, stop=True)
            te.matmul(ps["pA_im"][:, sl], xre[:, c, :], C["f_im"][0:KA, :], start=True, stop=False)
            inst = te.matmul(ps["pA_im"][:, sl], xim[:, c, :], C["f_re"][0:KA, :], start=False, stop=True)
        S["te"].inc(inst)

    @blk.vector
    def _(ve):
        S["te"].wait(ve)
        ve.tensor_copy(sb["d_re"][:, :], ps["pA_re"][:, :])
        ve.tensor_copy(sb["d_im"][:, :], ps["pA_im"][:, :])
        ve.tensor_mul(sb["e_re"][:, :], sb["d_re"][:, :], C["t_re"][:, :])
        ve.tensor_mul(sb["scr"][:, :], sb["d_im"][:, :], C["t_im"][:, :])
        ve.tensor_sub(sb["e_re"][:, :], sb["e_re"][:, :], sb["scr"][:, :])
        ve.tensor_mul(sb["e_im"][:, :], sb["d_re"][:, :], C["t_im"][:, :])
        ve.tensor_mul(sb["scr"][:, :], sb["d_im"][:, :], C["t_re"][:, :])
        inst = ve.tensor_add(sb["e_im"][:, :], sb["e_im"][:, :], sb["scr"][:, :])
        S["ve"].inc(inst)

    @blk.tensor
    def _(te):
        S["ve"].wait(te)
        te.matmul(ps["pB_re"][:, :], C["f_re"][:, :], sb["e_re"][:, :], start=True, stop=False)
        te.matmul(ps["pB_re"][:, :], C["f_imN"][:, :], sb["e_im"][:, :], start=False, stop=True)
        te.matmul(ps["pB_im"][:, :], C["f_im"][:, :], sb["e_re"][:, :], start=True, stop=False)
        inst = te.matmul(ps["pB_im"][:, :], C["f_re"][:, :], sb["e_im"][:, :], start=False, stop=True)
        S["te"].inc(inst)

    @blk.vector
    def _(ve):
        S["te"].wait(ve)
        ve.tensor_copy(out_re.rearrange("p c k -> p (c k)"), ps["pB_re"][:, :])
        inst = ve.tensor_copy(out_im.rearrange("p c k -> p (c k)"), ps["pB_im"][:, :])
        S["ve"].inc(inst)


def emit_ifft4(S, pr_re, pr_im, C, ps, sb, y_re, y_im):
    blk = S["blk"]

    @blk.tensor
    def _(te):
        S["ve"].wait(te)
        for c in range(4):
            sl = slice(128 * c, 128 * (c + 1))
            te.matmul(ps["pA_re"][:, sl], pr_re[:, c, :], C["fc_re"][:, :], start=True, stop=False)
            te.matmul(ps["pA_re"][:, sl], pr_im[:, c, :], C["fc_imN"][:, :], start=False, stop=True)
            te.matmul(ps["pA_im"][:, sl], pr_re[:, c, :], C["fc_im"][:, :], start=True, stop=False)
            inst = te.matmul(ps["pA_im"][:, sl], pr_im[:, c, :], C["fc_re"][:, :], start=False, stop=True)
        S["te"].inc(inst)

    @blk.vector
    def _(ve):
        S["te"].wait(ve)
        ve.tensor_copy(sb["d_re"][:, :], ps["pA_re"][:, :])
        ve.tensor_copy(sb["d_im"][:, :], ps["pA_im"][:, :])
        ve.tensor_mul(sb["e_re"][:, :], sb["d_re"][:, :], C["tc_re"][:, :])
        ve.tensor_mul(sb["scr"][:, :], sb["d_im"][:, :], C["tc_im"][:, :])
        ve.tensor_sub(sb["e_re"][:, :], sb["e_re"][:, :], sb["scr"][:, :])
        ve.tensor_mul(sb["e_im"][:, :], sb["d_re"][:, :], C["tc_im"][:, :])
        ve.tensor_mul(sb["scr"][:, :], sb["d_im"][:, :], C["tc_re"][:, :])
        inst = ve.tensor_add(sb["e_im"][:, :], sb["e_im"][:, :], sb["scr"][:, :])
        S["ve"].inc(inst)

    @blk.tensor
    def _(te):
        S["ve"].wait(te)
        te.matmul(ps["pB_re"][:, :], C["fc_re"][:, :], sb["e_re"][:, :], start=True, stop=False)
        te.matmul(ps["pB_re"][:, :], C["fc_imN"][:, :], sb["e_im"][:, :], start=False, stop=True)
        te.matmul(ps["pB_im"][:, :], C["fc_im"][:, :], sb["e_re"][:, :], start=True, stop=False)
        inst = te.matmul(ps["pB_im"][:, :], C["fc_re"][:, :], sb["e_im"][:, :], start=False, stop=True)
        S["te"].inc(inst)

    @blk.vector
    def _(ve):
        S["te"].wait(ve)
        ve.tensor_copy(y_re.rearrange("p c k -> p (c k)"), ps["pB_re"][0:64, :])
        inst = ve.tensor_copy(y_im.rearrange("p c k -> p (c k)"), ps["pB_im"][0:64, :])
        S["ve"].inc(inst)


def emit_cmul4(S, a_re, a_im, b_re, b_im, o_re, o_im, scr):
    blk = S["blk"]

    @blk.vector
    def _(ve):
        S["ve"].wait(ve)
        ar = a_re.rearrange("p c k -> p (c k)"); ai = a_im.rearrange("p c k -> p (c k)")
        br = b_re.rearrange("p c k -> p (c k)"); bi = b_im.rearrange("p c k -> p (c k)")
        orr = o_re.rearrange("p c k -> p (c k)"); oi = o_im.rearrange("p c k -> p (c k)")
        ve.tensor_mul(orr, ar, br)
        ve.tensor_mul(scr[:, :], ai, bi)
        ve.tensor_sub(orr, orr, scr[:, :])
        ve.tensor_mul(oi, ar, bi)
        ve.tensor_mul(scr[:, :], ai, br)
        inst = ve.tensor_add(oi, oi, scr[:, :])
        S["ve"].inc(inst)


def build_launch_E(nt=512):
    F = 4096
    nch = TPC // nt
    nc = bass.Bass("TRN2", target_bir_lowering=False)
    normG = nc.dram_tensor("normG", [8 * TPC, D], BF16, kind="ExternalInput").ap()
    s2t = nc.dram_tensor("s2t", [P, 32], I32, kind="ExternalInput").ap()
    w1e = nc.dram_tensor("w1e", [D, F], F32, kind="ExternalInput").ap()
    b1p = nc.dram_tensor("b1p", [P, 32], F32, kind="ExternalInput").ap()
    w2e = nc.dram_tensor("w2e", [F, D], F32, kind="ExternalInput").ap()
    b2p = nc.dram_tensor("b2p", [P, 8], F32, kind="ExternalInput").ap()
    stag = nc.dram_tensor("stag", [TPC, D], BF16, kind="Internal").ap()
    slab = nc.dram_tensor("slab", [D, TPC], BF16, kind="ExternalOutput").ap()

    from concourse.bass_core_rust import IndirectOffsetOnAxis
    with ExitStack() as ctx:
        E = ctx.enter_context
        w1_sb = E(nc.sbuf_tensor("w1_sb", [P, 8, F], BF16))
        w2_sb = E(nc.sbuf_tensor("w2_sb", [P, 32, D], BF16))
        b1_sb = E(nc.sbuf_tensor("b1_sb", [P, 32], F32))
        b2_sb = E(nc.sbuf_tensor("b2_sb", [P, 8], F32))
        idx_sb = E(nc.sbuf_tensor("idx_sb", [P, 32], I32))
        g_sb = E(nc.sbuf_tensor("g_sb", [P, D], BF16))
        nrm_c = E(nc.sbuf_tensor("nrm_c", [P, 8, nt], BF16))
        h_bf = E(nc.sbuf_tensor("h_bf", [P, 32, nt], BF16))
        o_bf2 = [E(nc.sbuf_tensor("o_bf0", [P, nt], BF16)),
                 E(nc.sbuf_tensor("o_bf1", [P, nt], BF16))]
        p_h2 = [E(nc.psum_tensor("p_h0", [P, nt], F32)),
                E(nc.psum_tensor("p_h1", [P, nt], F32))]
        p_o2 = [E(nc.psum_tensor("p_o0", [P, nt], F32)),
                E(nc.psum_tensor("p_o1", [P, nt], F32))]
        blk = E(nc.Block())
        S = {"blk": blk, "ld": Sem(ctx, nc, "ld"), "st": Sem(ctx, nc, "st"),
             "sc": Sem(ctx, nc, "sc"), "ve": Sem(ctx, nc, "ve"),
             "te": Sem(ctx, nc, "te"), "gi": Sem(ctx, nc, "gi"),
             "tp": Sem(ctx, nc, "tp")}

        @blk.gpsimd
        def _(gp):
            S["ld"].dma(gp.dma_start(
                out=w1_sb[:, :, :], in_=w1e.rearrange("(kt p) f -> p kt f", p=P)))
            S["ld"].dma(gp.dma_start(
                out=w2_sb[:, :, :], in_=w2e.rearrange("(ft p) d -> p ft d", p=P)))

        @blk.sync
        def _(sy):
            S["ld"].dma(sy.dma_start(out=b1_sb[:, :], in_=b1p))
            S["ld"].dma(sy.dma_start(out=b2_sb[:, :], in_=b2p))
            S["ld"].dma(sy.dma_start(out=idx_sb[:, :], in_=s2t))

        for r in range(32):
            @blk.gpsimd
            def _(gp, r=r):
                S["ld"].wait(gp)
                S["st"].wait(gp)
                S["gi"].dma(gp.indirect_dma_start(
                    out=g_sb[:, :], out_offset=None,
                    in_=normG, in_offset=IndirectOffsetOnAxis(ap=idx_sb[:, r:r + 1], axis=0)))

            @blk.sync
            def _(sy, r=r):
                S["gi"].wait(sy)
                S["st"].dma(sy.dma_start(out=stag[r * P:(r + 1) * P, :], in_=g_sb[:, :]))

        for ci in range(nch):
            t0 = ci * nt

            @blk.scalar
            def _(sc, t0=t0):
                S["gi"].wait(sc)
                S["te"].wait(sc)
                S["ve"].wait(sc)
                for dt in range(8):
                    S["tp"].dma(sc.dma_start_transpose(
                        out=nrm_c[:, dt, :], in_=stag[t0:t0 + nt, dt * P:(dt + 1) * P]))

            for ft in range(32):
                @blk.tensor
                def _(te, ft=ft):
                    S["tp"].wait(te)
                    S["sc"].wait(te, lag=1)
                    for kt in range(8):
                        inst = te.matmul(p_h2[ft % 2][:, :], w1_sb[:, kt, ft * P:(ft + 1) * P],
                                         nrm_c[:, kt, :], start=(kt == 0), stop=(kt == 7))
                    S["te"].inc(inst)

                @blk.scalar
                def _(sc, ft=ft):
                    S["te"].wait(sc)
                    inst = sc.activation(h_bf[:, ft, :], p_h2[ft % 2][:, :], AF.Gelu,
                                         bias=b1_sb[:, ft:ft + 1])
                    S["sc"].inc(inst)

            for dt in range(8):
                @blk.tensor
                def _(te, dt=dt):
                    S["sc"].wait(te)
                    S["ve"].wait(te, lag=1)
                    for ft in range(32):
                        inst = te.matmul(p_o2[dt % 2][:, :], w2_sb[:, ft, dt * P:(dt + 1) * P],
                                         h_bf[:, ft, :], start=(ft == 0), stop=(ft == 31))
                    S["te"].inc(inst)

                @blk.vector
                def _(ve, dt=dt):
                    S["te"].wait(ve)
                    inst = ve.tensor_scalar_add(o_bf2[dt % 2][:, :], p_o2[dt % 2][:, :],
                                                b2_sb[:, dt:dt + 1])
                    S["ve"].inc(inst)

                @blk.sync
                def _(sy, dt=dt, t0=t0):
                    S["ve"].wait(sy)
                    S["st"].dma(sy.dma_start(
                        out=slab[dt * P:(dt + 1) * P, t0:t0 + nt], in_=o_bf2[dt % 2][:, :]))

                @blk.vector
                def _(ve):
                    S["st"].wait(ve, lag=16)
                    inst = ve.engine_nop()
                    S["ve"].inc(inst)

        @blk.sync
        def _(sy):
            S["st"].wait(sy)
    return nc


def build_launch_G1(nt=256):
    NCH = 37
    nch = TPC // nt
    nc = bass.Bass("TRN2", target_bir_lowering=False)
    x2 = nc.dram_tensor("x2", [D, TPC], F32, kind="ExternalInput").ap()
    feat = nc.dram_tensor("feat", [D, TPC], F32, kind="ExternalInput").ap()
    ffnG = nc.dram_tensor("ffnG", [8 * TPC, D], BF16, kind="ExternalInput").ap()
    t2s = nc.dram_tensor("t2s", [P, 32], I32, kind="ExternalInput").ap()
    wcatT = nc.dram_tensor("wcatT", [D, NCH * P], F32, kind="ExternalInput").ap()
    bcat = nc.dram_tensor("bcat", [P, NCH], F32, kind="ExternalInput").ap()
    ng = nc.dram_tensor("ng", [P, 8], F32, kind="ExternalInput").ap()
    nb = nc.dram_tensor("nb", [P, 8], F32, kind="ExternalInput").ap()
    stag2 = nc.dram_tensor("stag2", [TPC, D], BF16, kind="Internal").ap()
    x3_o = nc.dram_tensor("x3_o", [D, TPC], F32, kind="ExternalOutput").ap()
    z_o = nc.dram_tensor("z_o", [D, TPC], F32, kind="ExternalOutput").ap()
    u_re_o = nc.dram_tensor("u_re_o", [512, TPC], BF16, kind="ExternalOutput").ap()
    u_im_o = nc.dram_tensor("u_im_o", [512, TPC], BF16, kind="ExternalOutput").ap()
    cp_o = nc.dram_tensor("cp_o", [D, TPC], F32, kind="ExternalOutput").ap()
    gv_o = nc.dram_tensor("gv_o", [D, TPC], F32, kind="ExternalOutput").ap()
    ts_o = nc.dram_tensor("ts_o", [1, TPC], F32, kind="ExternalOutput").ap()
    band_o = nc.dram_tensor("band_o", [4, TPC], F32, kind="ExternalOutput").ap()

    from concourse.bass_core_rust import IndirectOffsetOnAxis
    with ExitStack() as ctx:
        E = ctx.enter_context
        w_sb = E(nc.sbuf_tensor("w_sb", [P, 8, NCH * P], BF16))
        bcat_sb = E(nc.sbuf_tensor("bcat_sb", [P, NCH], F32))
        ng_sb = E(nc.sbuf_tensor("ng_sb", [P, 8], F32))
        nb_sb = E(nc.sbuf_tensor("nb_sb", [P, 8], F32))
        idx_sb = E(nc.sbuf_tensor("idx_sb", [P, 32], I32))
        g_sb = E(nc.sbuf_tensor("g_sb", [P, D], BF16))
        x2c = E(nc.sbuf_tensor("x2c", [P, 8, nt], F32))
        featc = E(nc.sbuf_tensor("featc", [P, 8, nt], F32))
        ffn_c = E(nc.sbuf_tensor("ffn_c", [P, 8, nt], BF16))
        x3c = E(nc.sbuf_tensor("x3c", [P, 8, nt], F32))
        zc = E(nc.sbuf_tensor("zc", [P, 8, nt], F32))
        xsq = E(nc.sbuf_tensor("xsq", [P, 8, nt], F32))
        xnb = E(nc.sbuf_tensor("xnb", [P, 8, nt], BF16))
        sig_cg = E(nc.sbuf_tensor("sig_cg", [P, 8, nt], F32))
        sp_st = E(nc.sbuf_tensor("sp_st", [P, 8, nt], F32))
        fg_st = E(nc.sbuf_tensor("fg_st", [P, 4, nt], F32))
        misc_st = E(nc.sbuf_tensor("misc_st", [P, nt], F32))
        cp_st = E(nc.sbuf_tensor("cp_st", [P, nt], F32))
        gv_st = E(nc.sbuf_tensor("gv_st", [P, nt], F32))
        u_st = E(nc.sbuf_tensor("u_st", [P, nt], BF16))
        ones_col = E(nc.sbuf_tensor("ones_col", [P, 1], F32))
        ones_row = E(nc.sbuf_tensor("ones_row", [1, P], F32))
        eps_ap = E(nc.sbuf_tensor("eps_ap", [1, 1], F32))
        murow = E(nc.sbuf_tensor("murow", [1, nt], F32))
        qrow = E(nc.sbuf_tensor("qrow", [1, nt], F32))
        mu2 = E(nc.sbuf_tensor("mu2", [1, nt], F32))
        varrow = E(nc.sbuf_tensor("varrow", [1, nt], F32))
        rstdrow = E(nc.sbuf_tensor("rstdrow", [1, nt], F32))
        mu_b = E(nc.sbuf_tensor("mu_b", [P, nt], F32))
        rstd_b = E(nc.sbuf_tensor("rstd_b", [P, nt], F32))
        p_sum = E(nc.psum_tensor("p_sum", [1, nt], F32))
        p_sq = E(nc.psum_tensor("p_sq", [1, nt], F32))
        p_mu = E(nc.psum_tensor("p_mu", [P, nt], F32))
        p_rstd = E(nc.psum_tensor("p_rstd", [P, nt], F32))
        p_pr = E(nc.psum_tensor("p_pr", [P, nt], F32))
        blk = E(nc.Block())
        S = {"blk": blk, "ld": Sem(ctx, nc, "ld"), "st": Sem(ctx, nc, "st"),
             "sc": Sem(ctx, nc, "sc"), "ve": Sem(ctx, nc, "ve"),
             "te": Sem(ctx, nc, "te"), "gi": Sem(ctx, nc, "gi"),
             "tp": Sem(ctx, nc, "tp")}

        @blk.gpsimd
        def _(gp):
            S["ld"].dma(gp.dma_start(
                out=w_sb[:, :, :], in_=wcatT.rearrange("(kt p) c -> p kt c", p=P)))

        @blk.sync
        def _(sy):
            S["ld"].dma(sy.dma_start(out=bcat_sb[:, :], in_=bcat))
            S["ld"].dma(sy.dma_start(out=ng_sb[:, :], in_=ng))
            S["ld"].dma(sy.dma_start(out=nb_sb[:, :], in_=nb))
            S["ld"].dma(sy.dma_start(out=idx_sb[:, :], in_=t2s))

        @blk.vector
        def _(ve):
            ve.memset(ones_col[:, :], 1.0)
            ve.memset(ones_row[:, :], 1.0)
            inst = ve.memset(eps_ap[:, :], LN_EPS)
            S["ve"].inc(inst)

        for r in range(32):
            @blk.gpsimd
            def _(gp, r=r):
                S["ld"].wait(gp)
                S["st"].wait(gp)
                S["gi"].dma(gp.indirect_dma_start(
                    out=g_sb[:, :], out_offset=None,
                    in_=ffnG, in_offset=IndirectOffsetOnAxis(ap=idx_sb[:, r:r + 1], axis=0)))

            @blk.sync
            def _(sy, r=r):
                S["gi"].wait(sy)
                S["st"].dma(sy.dma_start(out=stag2[r * P:(r + 1) * P, :], in_=g_sb[:, :]))

        for ci in range(nch):
            t0 = ci * nt

            @blk.sync
            def _(sy, t0=t0):
                S["st"].wait(sy)
                S["ve"].wait(sy)
                S["sc"].wait(sy)
                S["te"].wait(sy)
                S["ld"].dma(sy.dma_start(out=x2c[:, :, :], in_=fm_ap(x2, D, t0, nt)))
                S["ld"].dma(sy.dma_start(out=featc[:, :, :], in_=fm_ap(feat, D, t0, nt)))

            @blk.scalar
            def _(sc, t0=t0):
                S["gi"].wait(sc)
                S["ve"].wait(sc)
                for dt in range(8):
                    S["tp"].dma(sc.dma_start_transpose(
                        out=ffn_c[:, dt, :], in_=stag2[t0:t0 + nt, dt * P:(dt + 1) * P]))

            @blk.vector
            def _(ve):
                S["ld"].wait(ve)
                S["tp"].wait(ve)
                S["st"].wait(ve)
                for kt in range(8):
                    ve.tensor_add(x3c[:, kt, :], x2c[:, kt, :], featc[:, kt, :])
                    inst = ve.tensor_add(x3c[:, kt, :], x3c[:, kt, :], ffn_c[:, kt, :])
                S["ve"].inc(inst)

            emit_ln(nc, S, x3c, 8, nt, ones_col, ones_row, p_sum, p_sq, p_mu, p_rstd,
                    xsq, murow, qrow, mu2, varrow, rstdrow, eps_ap, mu_b, rstd_b,
                    gates=("ve",))

            @blk.vector
            def _(ve):
                S["ve"].wait(ve)
                for kt in range(8):
                    ve.tensor_sub(xsq[:, kt, :], x3c[:, kt, :], mu_b[:, :])
                    inst = ve.tensor_mul(zc[:, kt, :], xsq[:, kt, :], rstd_b[:, :])
                S["ve"].inc(inst)

            emit_ln(nc, S, zc, 8, nt, ones_col, ones_row, p_sum, p_sq, p_mu, p_rstd,
                    xsq, murow, qrow, mu2, varrow, rstdrow, eps_ap, mu_b, rstd_b,
                    gates=("ve",))

            @blk.vector
            def _(ve):
                S["ve"].wait(ve)
                for kt in range(8):
                    ve.tensor_sub(xsq[:, kt, :], zc[:, kt, :], mu_b[:, :])
                    ve.tensor_mul(xsq[:, kt, :], xsq[:, kt, :], rstd_b[:, :])
                    inst = ve.tensor_scalar(
                        xnb[:, kt, :], xsq[:, kt, :],
                        ng_sb[:, kt:kt + 1], nb_sb[:, kt:kt + 1],
                        op0=ALU.mult, op1=ALU.add)
                S["ve"].inc(inst)

            order = list(range(8, 16)) + list(range(0, 8)) + list(range(16, NCH))
            for ct in order:
                @blk.tensor
                def _(te, ct=ct):
                    S["ve"].wait(te)
                    S["sc"].wait(te)
                    for kt in range(8):
                        inst = te.matmul(p_pr[:, :], w_sb[:, kt, ct * P:(ct + 1) * P],
                                         xnb[:, kt, :], start=(kt == 0), stop=(kt == 7))
                    S["te"].inc(inst)

                if 8 <= ct < 16:
                    @blk.scalar
                    def _(sc, ct=ct):
                        S["te"].wait(sc)
                        inst = sc.activation(sig_cg[:, ct - 8, :], p_pr[:, :], AF.Sigmoid,
                                             bias=bcat_sb[:, ct:ct + 1])
                        S["sc"].inc(inst)
                elif ct < 8:
                    @blk.vector
                    def _(ve, ct=ct, t0=t0):
                        S["te"].wait(ve)
                        inst = ve.scalar_tensor_tensor(
                            out=cp_st[:, :], in0=p_pr[:, :], scalar=bcat_sb[:, ct:ct + 1],
                            in1=sig_cg[:, ct, :], op0=ALU.add, op1=ALU.mult)
                        S["ve"].inc(inst)

                    @blk.sync
                    def _(sy, ct=ct, t0=t0):
                        S["ve"].wait(sy)
                        S["st"].dma(sy.dma_start(
                            out=cp_o[ct * P:(ct + 1) * P, t0:t0 + nt], in_=cp_st[:, :]))

                    @blk.vector
                    def _(ve):
                        S["st"].wait(ve)
                elif 16 <= ct < 24:
                    @blk.vector
                    def _(ve, ct=ct):
                        S["te"].wait(ve)
                        inst = ve.tensor_copy(sp_st[:, ct - 16, :], p_pr[:, :])
                        S["ve"].inc(inst)
                elif 24 <= ct < 32:
                    @blk.scalar
                    def _(sc, ct=ct):
                        S["te"].wait(sc)
                        inst = sc.activation(gv_st[:, :], p_pr[:, :], AF.Sigmoid,
                                             bias=bcat_sb[:, ct:ct + 1])
                        S["sc"].inc(inst)

                    @blk.sync
                    def _(sy, ct=ct, t0=t0):
                        S["sc"].wait(sy)
                        S["st"].dma(sy.dma_start(
                            out=gv_o[(ct - 24) * P:(ct - 23) * P, t0:t0 + nt], in_=gv_st[:, :]))

                    @blk.scalar
                    def _(sc):
                        S["st"].wait(sc)
                elif 32 <= ct < 36:
                    @blk.scalar
                    def _(sc, ct=ct):
                        S["te"].wait(sc)
                        inst = sc.activation(fg_st[:, ct - 32, :], p_pr[:, :], AF.Sigmoid,
                                             bias=bcat_sb[:, ct:ct + 1])
                        S["sc"].inc(inst)
                else:
                    @blk.scalar
                    def _(sc, ct=ct):
                        S["te"].wait(sc)
                        inst = sc.activation(misc_st[:, :], p_pr[:, :], AF.Sigmoid,
                                             bias=bcat_sb[:, ct:ct + 1])
                        S["sc"].inc(inst)

            for j in range(4):
                @blk.vector
                def _(ve, j=j, t0=t0):
                    S["ve"].wait(ve)
                    S["sc"].wait(ve)
                    S["st"].wait(ve)
                    inst = ve.tensor_mul(u_st[:, :], sp_st[:, j, :], fg_st[:, j, :])
                    S["ve"].inc(inst)

                @blk.sync
                def _(sy, j=j, t0=t0):
                    S["ve"].wait(sy)
                    S["st"].dma(sy.dma_start(
                        out=u_re_o[j * P:(j + 1) * P, t0:t0 + nt], in_=u_st[:, :]))

                @blk.vector
                def _(ve, j=j):
                    S["st"].wait(ve)
                    inst = ve.tensor_mul(u_st[:, :], sp_st[:, 4 + j, :], fg_st[:, j, :])
                    S["ve"].inc(inst)

                @blk.sync
                def _(sy, j=j, t0=t0):
                    S["ve"].wait(sy)
                    S["st"].dma(sy.dma_start(
                        out=u_im_o[j * P:(j + 1) * P, t0:t0 + nt], in_=u_st[:, :]))

            @blk.sync
            def _(sy, t0=t0):
                S["ve"].wait(sy)
                S["sc"].wait(sy)
                S["st"].dma(sy.dma_start(out=fm_ap(x3_o, D, t0, nt), in_=x3c[:, :, :]))
                S["st"].dma(sy.dma_start(out=fm_ap(z_o, D, t0, nt), in_=zc[:, :, :]))
                S["st"].dma(sy.dma_start(out=band_o[:, t0:t0 + nt], in_=misc_st[0:4, :]))
                S["st"].dma(sy.dma_start(out=ts_o[:, t0:t0 + nt], in_=misc_st[4:5, :]))

        @blk.sync
        def _(sy):
            S["st"].wait(sy)
    return nc


def build_launch_F(nt=512):
    CH = 256
    NJ = CH // P
    nch = SEQ // nt
    PI = float(np.pi)
    nc = bass.Bass("TRN2", target_bir_lowering=False)
    u_re = nc.dram_tensor("u_re", [CH, SEQ], BF16, kind="ExternalInput").ap()
    u_im = nc.dram_tensor("u_im", [CH, SEQ], BF16, kind="ExternalInput").ap()
    tsrow = nc.dram_tensor("tsrow", [1, SEQ], F32, kind="ExternalInput").ap()
    bandr = nc.dram_tensor("bandr", [1, 2 * SEQ], F32, kind="ExternalInput").ap()
    ld_loc = nc.dram_tensor("ld_loc", [1, CH], F32, kind="ExternalInput").ap()
    fr_loc = nc.dram_tensor("fr_loc", [1, CH], F32, kind="ExternalInput").ap()
    dt_loc = nc.dram_tensor("dt_loc", [1, CH], F32, kind="ExternalInput").ap()
    fi_roll = nc.dram_tensor("fi_roll", [1, 512], F32, kind="ExternalInput").ap()
    iota1 = nc.dram_tensor("iota1", [1, SEQ], F32, kind="ExternalInput").ap()
    cst = {n: nc.dram_tensor(n, [128, 512 if n.startswith("t") else 128], BF16,
                             kind="ExternalInput").ap()
           for n in ["f_re", "f_im", "f_imN", "fc_re", "fc_im", "fc_imN",
                     "t_re", "t_im", "tc_re", "tc_im"]}
    dec_d0 = nc.dram_tensor("dec_d0", [P, 1], F32, kind="Internal").ap()
    dec_d1 = nc.dram_tensor("dec_d1", [P, 1], F32, kind="Internal").ap()
    k_re_d = nc.dram_tensor("k_re_d", [CH, SEQ], BF16, kind="Internal").ap()
    k_im_d = nc.dram_tensor("k_im_d", [CH, SEQ], BF16, kind="Internal").ap()
    y_re_o = nc.dram_tensor("y_re_o", [CH, SEQ], BF16, kind="ExternalOutput").ap()
    y_im_o = nc.dram_tensor("y_im_o", [CH, SEQ], BF16, kind="ExternalOutput").ap()

    with ExitStack() as ctx:
        E = ctx.enter_context
        C = {n: E(nc.sbuf_tensor("s_" + n, [128, 512 if n.startswith("t") else 128], BF16))
             for n in cst}
        ldr = E(nc.sbuf_tensor("ldr", [1, CH], F32))
        frr = E(nc.sbuf_tensor("frr", [1, CH], F32))
        dtr = E(nc.sbuf_tensor("dtr", [1, CH], F32))
        fir = E(nc.sbuf_tensor("fir", [1, 512], F32))
        fie = E(nc.sbuf_tensor("fie", [1, 512], F32))
        smx = E(nc.sbuf_tensor("smx", [1, 4], F32))
        decr = E(nc.sbuf_tensor("decr", [1, CH], F32))
        omr = E(nc.sbuf_tensor("omr", [1, CH], F32))
        ts_c = E(nc.sbuf_tensor("ts_c", [1, 512], F32))
        bnd_c = E(nc.sbuf_tensor("bnd_c", [1, 512], F32))
        iot_c = E(nc.sbuf_tensor("iot_c", [1, 512], F32))
        dec_col = E(nc.sbuf_tensor("dec_col", [P, NJ], F32))
        amag = E(nc.sbuf_tensor("amag", [P, SEQ], F32))
        rscan = E(nc.sbuf_tensor("rscan", [P, SEQ + 1], F32))
        ones_row = E(nc.sbuf_tensor("ones_row", [1, P], F32))
        ones_cl = E(nc.sbuf_tensor("ones_cl", [P, 1], F32))
        phs = E(nc.sbuf_tensor("phs", [P, nt], F32))
        ki = E(nc.sbuf_tensor("ki", [P, nt], I32))
        kf = E(nc.sbuf_tensor("kf", [P, nt], F32))
        cu8 = E(nc.sbuf_tensor("cu8", [P, nt], mybir.dt.uint8))
        sinv = E(nc.sbuf_tensor("sinv", [P, nt], F32))
        cosv = E(nc.sbuf_tensor("cosv", [P, nt], F32))
        ktmp = E(nc.sbuf_tensor("ktmp", [P, nt], F32))
        kre_st = E(nc.sbuf_tensor("kre_st", [P, nt], BF16))
        kim_st = E(nc.sbuf_tensor("kim_st", [P, nt], BF16))
        xre = E(nc.sbuf_tensor("xre", [64, 4, 128], BF16))
        xim = E(nc.sbuf_tensor("xim", [64, 4, 128], BF16))
        kre = E(nc.sbuf_tensor("kre", [64, 4, 128], BF16))
        kim = E(nc.sbuf_tensor("kim", [64, 4, 128], BF16))
        su_re = E(nc.sbuf_tensor("su_re", [128, 4, 128], BF16))
        su_im = E(nc.sbuf_tensor("su_im", [128, 4, 128], BF16))
        sk_re = E(nc.sbuf_tensor("sk_re", [128, 4, 128], BF16))
        sk_im = E(nc.sbuf_tensor("sk_im", [128, 4, 128], BF16))
        pr_re = E(nc.sbuf_tensor("pr_re", [128, 4, 128], BF16))
        pr_im = E(nc.sbuf_tensor("pr_im", [128, 4, 128], BF16))
        yt_re = E(nc.sbuf_tensor("yt_re", [64, 4, 128], BF16))
        yt_im = E(nc.sbuf_tensor("yt_im", [64, 4, 128], BF16))
        sb = {n: E(nc.sbuf_tensor(n, [128, 512], BF16))
              for n in ["d_re", "d_im", "e_re", "e_im", "scr"]}
        ps = {n: E(nc.psum_tensor(n, [128, 512], F32))
              for n in ["pA_re", "pA_im", "pB_re", "pB_im"]}
        p_b = E(nc.psum_tensor("p_b", [P, nt], F32))
        p_t = E(nc.psum_tensor("p_t", [P, nt], F32))
        p_ph = E(nc.psum_tensor("p_ph", [P, nt], F32))
        blk = E(nc.Block())
        S = {"blk": blk, "ld": Sem(ctx, nc, "ld"), "st": Sem(ctx, nc, "st"),
             "sc": Sem(ctx, nc, "sc"), "ve": Sem(ctx, nc, "ve"),
             "te": Sem(ctx, nc, "te"), "vv": Sem(ctx, nc, "vv")}

        @blk.sync
        def _(sy):
            for n, ap in cst.items():
                S["ld"].dma(sy.dma_start(out=C[n][:, :], in_=ap))
            for t, ap in ((ldr, ld_loc), (frr, fr_loc), (dtr, dt_loc), (fir, fi_roll)):
                S["ld"].dma(sy.dma_start(out=t[:, :], in_=ap))

        @blk.vector
        def _(ve):
            ve.memset(ones_row[:, :], 1.0)
            ve.memset(ones_cl[:, :], 1.0)
            ve.memset(rscan[:, 0:1], 1.0)
            S["ld"].wait(ve)
            inst = ve.tensor_reduce(smx[0:1, 0:1], fir[:, :], mybir.AxisListType.X, ALU.max)
            S["ve"].inc(inst)

        @blk.vector
        def _(ve):
            S["ve"].wait(ve)
            S["vv"].inc(ve.tensor_scalar_mul(smx[0:1, 1:2], smx[0:1, 0:1], -1.0))
            S["vv"].wait(ve)
            inst = ve.tensor_scalar_add(fie[:, :], fir[:, :], smx[0:1, 1:2])
            S["ve"].inc(inst)

        @blk.scalar
        def _(sc):
            S["ve"].wait(sc)
            inst = sc.activation(fie[:, :], fie[:, :], AF.Exp)
            S["sc"].inc(inst)

        @blk.vector
        def _(ve):
            S["sc"].wait(ve)
            S["vv"].inc(ve.tensor_reduce(smx[0:1, 2:3], fie[:, :], mybir.AxisListType.X, ALU.add))
            S["vv"].wait(ve)
            S["vv"].inc(ve.reciprocal(smx[0:1, 3:4], smx[0:1, 2:3]))
            S["vv"].wait(ve)
            S["vv"].inc(ve.tensor_scalar_mul(decr[:, :], fie[0:1, 0:CH], smx[0:1, 3:4]))
            S["vv"].wait(ve)
            inst = ve.tensor_scalar_mul(decr[:, :], decr[:, :], 512.0)
            S["ve"].inc(inst)

        @blk.scalar
        def _(sc):
            S["ve"].wait(sc)
            sc.activation(ldr[:, :], ldr[:, :], AF.Sigmoid)
            inst = sc.activation(dtr[:, :], dtr[:, :], AF.Sigmoid)
            S["sc"].inc(inst)

        @blk.vector
        def _(ve):
            S["sc"].wait(ve)
            ve.tensor_mul(decr[:, :], decr[:, :], ldr[:, :])
            S["vv"].inc(ve.tensor_mul(omr[:, :], frr[:, :], dtr[:, :]))
            S["vv"].wait(ve)
            inst = ve.tensor_scalar_mul(omr[:, :], omr[:, :], 0.2)
            S["ve"].inc(inst)

        @blk.sync
        def _(sy):
            S["ve"].wait(sy)
            S["st"].dma(sy.dma_start(out=dec_d0, in_=decr[0:1, 0:P]))
            S["st"].dma(sy.dma_start(out=dec_d1, in_=decr[0:1, P:2 * P]))

        @blk.sync
        def _(sy):
            S["st"].wait(sy)
            S["ld"].dma(sy.dma_start(out=dec_col[:, 0:1], in_=dec_d0))
            S["ld"].dma(sy.dma_start(out=dec_col[:, 1:2], in_=dec_d1))

        for j in range(NJ):
            @blk.vector
            def _(ve, j=j):
                S["ld"].wait(ve)
                S["te"].wait(ve)
                inst = ve.engine_nop()
                S["ve"].inc(inst)

            for ci in range(nch):
                t0 = ci * nt

                @blk.sync
                def _(sy, j=j, t0=t0):
                    S["te"].wait(sy)
                    S["ld"].dma(sy.dma_start(out=bnd_c[:, 0:nt],
                                             in_=bandr[0:1, j * SEQ + t0:j * SEQ + t0 + nt]))

                @blk.tensor
                def _(te, j=j, t0=t0):
                    S["ve"].wait(te)
                    S["ld"].wait(te)
                    inst = te.matmul(p_b[:, :], ones_row[:, :], bnd_c[0:1, 0:nt],
                                     start=True, stop=True)
                    S["te"].inc(inst)

                @blk.vector
                def _(ve, j=j, t0=t0):
                    S["te"].wait(ve)
                    inst = ve.tensor_single_scalar(amag[:, t0:t0 + nt], p_b[:, :],
                                                   dec_col[:, j:j + 1], op=ALU.mult)
                    S["ve"].inc(inst)

            @blk.vector
            def _(ve, j=j):
                S["ve"].wait(ve)
                inst = ve.tensor_tensor_scan(
                    out=rscan[:, 1:SEQ + 1], data0=amag[:, :],
                    data1=ones_cl[:, 0:1].to_broadcast([P, SEQ]),
                    initial=1.0, op0=ALU.mult, op1=ALU.mult)
                S["ve"].inc(inst)

            for ci in range(nch):
                t0 = ci * nt

                @blk.sync
                def _(sy, t0=t0):
                    S["te"].wait(sy)
                    S["ld"].dma(sy.dma_start(out=iot_c[:, 0:nt], in_=iota1[0:1, t0:t0 + nt]))
                    S["ld"].dma(sy.dma_start(out=ts_c[:, 0:nt], in_=tsrow[0:1, t0:t0 + nt]))

                @blk.tensor
                def _(te, j=j, t0=t0):
                    S["ve"].wait(te)
                    S["sc"].wait(te)
                    S["ld"].wait(te)
                    te.matmul(p_ph[:, :], omr[0:1, j * P:(j + 1) * P], iot_c[0:1, 0:nt],
                              start=True, stop=True)
                    inst = te.matmul(p_t[:, :], ones_row[:, :], ts_c[0:1, 0:nt],
                                     start=True, stop=True)
                    S["te"].inc(inst)

                @blk.vector
                def _(ve, t0=t0):
                    S["te"].wait(ve)
                    S["vv"].inc(ve.tensor_single_scalar(kf[:, :], p_ph[:, :], 1.0 / (2 * PI), op=ALU.mult))
                    S["vv"].wait(ve)
                    S["vv"].inc(ve.tensor_copy(ki[:, :], kf[:, :]))
                    S["vv"].wait(ve)
                    S["vv"].inc(ve.tensor_copy(kf[:, :], ki[:, :]))
                    S["vv"].wait(ve)
                    S["vv"].inc(ve.scalar_tensor_tensor(out=phs[:, :], in0=kf[:, :],
                                                        scalar=-2 * PI, in1=p_ph[:, :],
                                                        op0=ALU.mult, op1=ALU.add))
                    S["vv"].wait(ve)
                    S["vv"].inc(ve.tensor_scalar(cu8[:, :], phs[:, :], PI, None, op0=ALU.is_gt))
                    S["vv"].wait(ve)
                    S["vv"].inc(ve.scalar_tensor_tensor(out=phs[:, :], in0=cu8[:, :],
                                                        scalar=-2 * PI, in1=phs[:, :],
                                                        op0=ALU.mult, op1=ALU.add))
                    S["vv"].wait(ve)
                    S["vv"].inc(ve.tensor_scalar(cu8[:, :], phs[:, :], -PI, None, op0=ALU.is_lt))
                    S["vv"].wait(ve)
                    inst = ve.scalar_tensor_tensor(out=phs[:, :], in0=cu8[:, :],
                                                   scalar=2 * PI, in1=phs[:, :],
                                                   op0=ALU.mult, op1=ALU.add)
                    S["ve"].inc(inst)

                @blk.scalar
                def _(sc):
                    S["ve"].wait(sc)
                    inst = sc.activation(sinv[:, :], phs[:, :], AF.Sin)
                    S["sc"].inc(inst)

                @blk.vector
                def _(ve):
                    S["sc"].wait(ve)
                    S["vv"].inc(ve.tensor_scalar_add(phs[:, :], phs[:, :], PI / 2))
                    S["vv"].wait(ve)
                    S["vv"].inc(ve.tensor_scalar(cu8[:, :], phs[:, :], PI, None, op0=ALU.is_gt))
                    S["vv"].wait(ve)
                    inst = ve.scalar_tensor_tensor(out=phs[:, :], in0=cu8[:, :],
                                                   scalar=-2 * PI, in1=phs[:, :],
                                                   op0=ALU.mult, op1=ALU.add)
                    S["ve"].inc(inst)

                @blk.scalar
                def _(sc):
                    S["ve"].wait(sc)
                    inst = sc.activation(cosv[:, :], phs[:, :], AF.Sin)
                    S["sc"].inc(inst)

                @blk.vector
                def _(ve, t0=t0):
                    S["sc"].wait(ve)
                    S["st"].wait(ve)
                    S["vv"].inc(ve.tensor_mul(ktmp[:, :], rscan[:, t0:t0 + nt], p_t[:, :]))
                    S["vv"].wait(ve)
                    ve.tensor_mul(kre_st[:, :], ktmp[:, :], cosv[:, :])
                    inst = ve.tensor_mul(kim_st[:, :], ktmp[:, :], sinv[:, :])
                    S["ve"].inc(inst)

                @blk.sync
                def _(sy, j=j, t0=t0):
                    S["ve"].wait(sy)
                    S["st"].dma(sy.dma_start(
                        out=k_re_d[j * P:(j + 1) * P, t0:t0 + nt], in_=kre_st[:, :]))
                    S["st"].dma(sy.dma_start(
                        out=k_im_d[j * P:(j + 1) * P, t0:t0 + nt], in_=kim_st[:, :]))

        for g in range(CH // 4):
            c0 = g * 4

            @blk.gpsimd
            def _(gp, c0=c0):
                S["ve"].wait(gp)
                S["st"].wait(gp)
                S["ld"].dma(gp.dma_start(
                    out=xre[:, :, :], in_=u_re[c0:c0 + 4, :].rearrange("c (a b) -> a c b", b=128)))
                S["ld"].dma(gp.dma_start(
                    out=xim[:, :, :], in_=u_im[c0:c0 + 4, :].rearrange("c (a b) -> a c b", b=128)))
                S["ld"].dma(gp.dma_start(
                    out=kre[:, :, :], in_=k_re_d[c0:c0 + 4, :].rearrange("c (a b) -> a c b", b=128)))
                S["ld"].dma(gp.dma_start(
                    out=kim[:, :, :], in_=k_im_d[c0:c0 + 4, :].rearrange("c (a b) -> a c b", b=128)))

            te_gate = (lambda te: (S["ld"].wait(te), S["ve"].wait(te)))
            emit_fft4(S, te_gate, None, xre, xim, C, ps, sb, su_re, su_im)
            emit_fft4(S, te_gate, None, kre, kim, C, ps, sb, sk_re, sk_im)
            emit_cmul4(S, su_re, su_im, sk_re, sk_im, pr_re, pr_im, sb["scr"])
            emit_ifft4(S, pr_re, pr_im, C, ps, sb, yt_re, yt_im)

            @blk.sync
            def _(sy, c0=c0):
                S["ve"].wait(sy)
                S["st"].dma(sy.dma_start(
                    out=y_re_o[c0:c0 + 4, :].rearrange("c (a b) -> a c b", b=128),
                    in_=yt_re[:, :, :]))
                S["st"].dma(sy.dma_start(
                    out=y_im_o[c0:c0 + 4, :].rearrange("c (a b) -> a c b", b=128),
                    in_=yt_im[:, :, :]))

        @blk.sync
        def _(sy):
            S["st"].wait(sy)
    return nc


def build_launch_G2a(nt=512):
    nch = TPC // nt
    nc = bass.Bass("TRN2", target_bir_lowering=False)
    z_in = nc.dram_tensor("z_in", [D, TPC], F32, kind="ExternalInput").ap()
    spec = nc.dram_tensor("spec", [D, TPC], BF16, kind="ExternalInput").ap()
    cp = nc.dram_tensor("cp", [D, TPC], F32, kind="ExternalInput").ap()
    gv = nc.dram_tensor("gv", [D, TPC], F32, kind="ExternalInput").ap()
    wfsT = nc.dram_tensor("wfsT", [D, D], F32, kind="ExternalInput").ap()
    osc = nc.dram_tensor("osc", [1, 1], F32, kind="ExternalInput").ap()
    x4_o = nc.dram_tensor("x4_o", [D, TPC], F32, kind="ExternalOutput").ap()

    with ExitStack() as ctx:
        E = ctx.enter_context
        wfs_sb = E(nc.sbuf_tensor("wfs_sb", [P, 8, D], BF16))
        osc_sb = E(nc.sbuf_tensor("osc_sb", [1, 1], F32))
        osc_col = E(nc.sbuf_tensor("osc_col", [P, 1], F32))
        ones_row = E(nc.sbuf_tensor("ones_row", [1, P], F32))
        specc = E(nc.sbuf_tensor("specc", [P, 8, nt], BF16))
        zc = E(nc.sbuf_tensor("zc", [P, 8, nt], F32))
        cpc = E(nc.sbuf_tensor("cpc", [P, 8, nt], F32))
        gvc = E(nc.sbuf_tensor("gvc", [P, 8, nt], F32))
        x4c = E(nc.sbuf_tensor("x4c", [P, 8, nt], F32))
        scr = E(nc.sbuf_tensor("scr", [P, nt], F32))
        p_y = E(nc.psum_tensor("p_y", [P, nt], F32))
        p_o = E(nc.psum_tensor("p_o", [P, 1], F32))
        blk = E(nc.Block())
        S = {"blk": blk, "ld": Sem(ctx, nc, "ld"), "st": Sem(ctx, nc, "st"),
             "sc": Sem(ctx, nc, "sc"), "ve": Sem(ctx, nc, "ve"),
             "te": Sem(ctx, nc, "te")}

        @blk.gpsimd
        def _(gp):
            S["ld"].dma(gp.dma_start(
                out=wfs_sb[:, :, :], in_=wfsT.rearrange("(kt p) d -> p kt d", p=P)))

        @blk.sync
        def _(sy):
            S["ld"].dma(sy.dma_start(out=osc_sb[:, :], in_=osc))

        @blk.vector
        def _(ve):
            inst = ve.memset(ones_row[:, :], 1.0)
            S["ve"].inc(inst)

        @blk.tensor
        def _(te):
            S["ld"].wait(te)
            S["ve"].wait(te)
            inst = te.matmul(p_o[:, :], ones_row[:, :], osc_sb[:, :], start=True, stop=True)
            S["te"].inc(inst)

        @blk.vector
        def _(ve):
            S["te"].wait(ve)
            inst = ve.tensor_copy(osc_col[:, :], p_o[:, :])
            S["ve"].inc(inst)

        for ci in range(nch):
            t0 = ci * nt

            @blk.sync
            def _(sy, t0=t0):
                S["st"].wait(sy)
                S["ve"].wait(sy)
                S["te"].wait(sy)
                S["ld"].dma(sy.dma_start(out=zc[:, :, :], in_=fm_ap(z_in, D, t0, nt)))
                S["ld"].dma(sy.dma_start(out=cpc[:, :, :], in_=fm_ap(cp, D, t0, nt)))
                S["ld"].dma(sy.dma_start(out=gvc[:, :, :], in_=fm_ap(gv, D, t0, nt)))

            @blk.gpsimd
            def _(gp, t0=t0):
                S["ve"].wait(gp)
                S["te"].wait(gp)
                S["ld"].dma(gp.dma_start(out=specc[:, :, :], in_=fm_ap(spec, D, t0, nt)))

            for dt in range(8):
                @blk.tensor
                def _(te, dt=dt):
                    S["ld"].wait(te)
                    S["ve"].wait(te)
                    for kt in range(8):
                        inst = te.matmul(p_y[:, :], wfs_sb[:, kt, dt * P:(dt + 1) * P],
                                         specc[:, kt, :], start=(kt == 0), stop=(kt == 7))
                    S["te"].inc(inst)

                @blk.vector
                def _(ve, dt=dt):
                    S["te"].wait(ve)
                    ve.tensor_single_scalar(scr[:, :], p_y[:, :], osc_col[:, 0:1], op=ALU.mult)
                    ve.tensor_mul(scr[:, :], scr[:, :], gvc[:, dt, :])
                    ve.tensor_add(scr[:, :], scr[:, :], cpc[:, dt, :])
                    inst = ve.scalar_tensor_tensor(
                        out=x4c[:, dt, :], in0=scr[:, :], scalar=0.5,
                        in1=zc[:, dt, :], op0=ALU.mult, op1=ALU.add)
                    S["ve"].inc(inst)

            @blk.sync
            def _(sy, t0=t0):
                S["ve"].wait(sy)
                S["st"].dma(sy.dma_start(out=fm_ap(x4_o, D, t0, nt), in_=x4c[:, :, :]))

        @blk.sync
        def _(sy):
            S["st"].wait(sy)
    return nc


def build_launch_G2f(nt=256, w_dtype=F32):
    F = 4096
    nch = TPC // nt
    nc = bass.Bass("TRN2", target_bir_lowering=False)
    x3 = nc.dram_tensor("x3", [D, TPC], F32, kind="ExternalInput").ap()
    x4 = nc.dram_tensor("x4", [D, TPC], F32, kind="ExternalInput").ap()
    n2g = nc.dram_tensor("n2g", [P, 8], F32, kind="ExternalInput").ap()
    n2b = nc.dram_tensor("n2b", [P, 8], F32, kind="ExternalInput").ap()
    w1T = nc.dram_tensor("w1T", [D, F], w_dtype, kind="ExternalInput").ap()
    b1p = nc.dram_tensor("b1p", [P, 32], F32, kind="ExternalInput").ap()
    w2T = nc.dram_tensor("w2T", [F, D], w_dtype, kind="ExternalInput").ap()
    b2p = nc.dram_tensor("b2p", [P, 8], F32, kind="ExternalInput").ap()
    x5_o = nc.dram_tensor("x5_o", [D, TPC], F32, kind="ExternalOutput").ap()

    with ExitStack() as ctx:
        E = ctx.enter_context
        w1_sb = E(nc.sbuf_tensor("w1_sb", [P, 8, F], BF16))
        w2_sb = E(nc.sbuf_tensor("w2_sb", [P, 32, D], BF16))
        b1_sb = E(nc.sbuf_tensor("b1_sb", [P, 32], F32))
        b2_sb = E(nc.sbuf_tensor("b2_sb", [P, 8], F32))
        n2g_sb = E(nc.sbuf_tensor("n2g_sb", [P, 8], F32))
        n2b_sb = E(nc.sbuf_tensor("n2b_sb", [P, 8], F32))
        x3c = E(nc.sbuf_tensor("x3c", [P, 8, nt], F32))
        x4c = E(nc.sbuf_tensor("x4c", [P, 8, nt], F32))
        xsq = E(nc.sbuf_tensor("xsq", [P, 8, nt], F32))
        xnb = E(nc.sbuf_tensor("xnb", [P, 8, nt], BF16))
        h_bf = E(nc.sbuf_tensor("h_bf", [P, 32, nt], BF16))
        x5c = E(nc.sbuf_tensor("x5c", [P, 8, nt], F32))
        ones_col = E(nc.sbuf_tensor("ones_col", [P, 1], F32))
        ones_row = E(nc.sbuf_tensor("ones_row", [1, P], F32))
        eps_ap = E(nc.sbuf_tensor("eps_ap", [1, 1], F32))
        murow = E(nc.sbuf_tensor("murow", [1, nt], F32))
        qrow = E(nc.sbuf_tensor("qrow", [1, nt], F32))
        mu2 = E(nc.sbuf_tensor("mu2", [1, nt], F32))
        varrow = E(nc.sbuf_tensor("varrow", [1, nt], F32))
        rstdrow = E(nc.sbuf_tensor("rstdrow", [1, nt], F32))
        mu_b = E(nc.sbuf_tensor("mu_b", [P, nt], F32))
        rstd_b = E(nc.sbuf_tensor("rstd_b", [P, nt], F32))
        p_sum = E(nc.psum_tensor("p_sum", [1, nt], F32))
        p_sq = E(nc.psum_tensor("p_sq", [1, nt], F32))
        p_mu = E(nc.psum_tensor("p_mu", [P, nt], F32))
        p_rstd = E(nc.psum_tensor("p_rstd", [P, nt], F32))
        p_h2 = [E(nc.psum_tensor("p_h0", [P, nt], F32)),
                E(nc.psum_tensor("p_h1", [P, nt], F32))]
        p_o2 = [E(nc.psum_tensor("p_o0", [P, nt], F32)),
                E(nc.psum_tensor("p_o1", [P, nt], F32))]
        blk = E(nc.Block())
        S = {"blk": blk, "ld": Sem(ctx, nc, "ld"), "st": Sem(ctx, nc, "st"),
             "sc": Sem(ctx, nc, "sc"), "ve": Sem(ctx, nc, "ve"),
             "te": Sem(ctx, nc, "te")}

        @blk.gpsimd
        def _(gp):
            S["ld"].dma(gp.dma_start(
                out=w1_sb[:, :, :], in_=w1T.rearrange("(kt p) f -> p kt f", p=P)))
            S["ld"].dma(gp.dma_start(
                out=w2_sb[:, :, :], in_=w2T.rearrange("(ft p) d -> p ft d", p=P)))

        @blk.sync
        def _(sy):
            S["ld"].dma(sy.dma_start(out=b1_sb[:, :], in_=b1p))
            S["ld"].dma(sy.dma_start(out=b2_sb[:, :], in_=b2p))
            S["ld"].dma(sy.dma_start(out=n2g_sb[:, :], in_=n2g))
            S["ld"].dma(sy.dma_start(out=n2b_sb[:, :], in_=n2b))

        @blk.vector
        def _(ve):
            ve.memset(ones_col[:, :], 1.0)
            ve.memset(ones_row[:, :], 1.0)
            inst = ve.memset(eps_ap[:, :], LN_EPS)
            S["ve"].inc(inst)

        for ci in range(nch):
            t0 = ci * nt

            @blk.sync
            def _(sy, t0=t0):
                S["st"].wait(sy)
                S["ve"].wait(sy)
                S["sc"].wait(sy)
                S["te"].wait(sy)
                S["ld"].dma(sy.dma_start(out=x3c[:, :, :], in_=fm_ap(x3, D, t0, nt)))
                S["ld"].dma(sy.dma_start(out=x4c[:, :, :], in_=fm_ap(x4, D, t0, nt)))

            emit_ln(nc, S, x4c, 8, nt, ones_col, ones_row, p_sum, p_sq, p_mu, p_rstd,
                    xsq, murow, qrow, mu2, varrow, rstdrow, eps_ap, mu_b, rstd_b,
                    gates=("ld",))

            @blk.vector
            def _(ve):
                S["ve"].wait(ve)
                for kt in range(8):
                    ve.tensor_sub(xsq[:, kt, :], x4c[:, kt, :], mu_b[:, :])
                    ve.tensor_mul(xsq[:, kt, :], xsq[:, kt, :], rstd_b[:, :])
                    inst = ve.tensor_scalar(
                        xnb[:, kt, :], xsq[:, kt, :],
                        n2g_sb[:, kt:kt + 1], n2b_sb[:, kt:kt + 1],
                        op0=ALU.mult, op1=ALU.add)
                S["ve"].inc(inst)

            for ft in range(32):
                @blk.tensor
                def _(te, ft=ft):
                    S["ve"].wait(te)
                    S["sc"].wait(te, lag=1)
                    for kt in range(8):
                        inst = te.matmul(p_h2[ft % 2][:, :], w1_sb[:, kt, ft * P:(ft + 1) * P],
                                         xnb[:, kt, :], start=(kt == 0), stop=(kt == 7))
                    S["te"].inc(inst)

                @blk.scalar
                def _(sc, ft=ft):
                    S["te"].wait(sc)
                    inst = sc.activation(h_bf[:, ft, :], p_h2[ft % 2][:, :], AF.Gelu,
                                         bias=b1_sb[:, ft:ft + 1])
                    S["sc"].inc(inst)

            for dt in range(8):
                @blk.tensor
                def _(te, dt=dt):
                    S["sc"].wait(te)
                    S["ve"].wait(te, lag=1)
                    for ft in range(32):
                        inst = te.matmul(p_o2[dt % 2][:, :], w2_sb[:, ft, dt * P:(dt + 1) * P],
                                         h_bf[:, ft, :], start=(ft == 0), stop=(ft == 31))
                    S["te"].inc(inst)

                @blk.vector
                def _(ve, dt=dt):
                    S["te"].wait(ve)
                    ve.scalar_tensor_tensor(
                        out=x5c[:, dt, :], in0=p_o2[dt % 2][:, :], scalar=b2_sb[:, dt:dt + 1],
                        in1=x4c[:, dt, :], op0=ALU.add, op1=ALU.add)
                    inst = ve.tensor_add(x5c[:, dt, :], x5c[:, dt, :], x3c[:, dt, :])
                    S["ve"].inc(inst)

            @blk.sync
            def _(sy, t0=t0):
                S["ve"].wait(sy)
                S["st"].dma(sy.dma_start(out=fm_ap(x5_o, D, t0, nt), in_=x5c[:, :, :]))

        @blk.sync
        def _(sy):
            S["st"].wait(sy)
    return nc


def build_launch_BIN():
    F = 4096
    nc = bass.Bass("TRN2", target_bir_lowering=False)
    fw1T = nc.dram_tensor("fw1T", [D, F], F32, kind="ExternalInput").ap()
    fw2T = nc.dram_tensor("fw2T", [F, D], F32, kind="ExternalInput").ap()
    bw1T_o = nc.dram_tensor("bw1T_o", [D, F], BF16, kind="ExternalOutput").ap()
    bw2T_o = nc.dram_tensor("bw2T_o", [F, D], BF16, kind="ExternalOutput").ap()

    with ExitStack() as ctx:
        E = ctx.enter_context
        wtile = E(nc.sbuf_tensor("wtile", [P, F], F32))
        scr = E(nc.sbuf_tensor("scr", [P, F], F32))
        alpha = E(nc.sbuf_tensor("alpha", [1, F], F32))
        alpha_b = E(nc.sbuf_tensor("alpha_b", [P, F], F32))
        bw_st = E(nc.sbuf_tensor("bw_st", [P, F], BF16))
        ones_col = E(nc.sbuf_tensor("ones_col", [P, 1], F32))
        ones_row = E(nc.sbuf_tensor("ones_row", [1, P], F32))
        p_al = E(nc.psum_tensor("p_al", [1, 512], F32))
        p_alb = E(nc.psum_tensor("p_alb", [P, 512], F32))
        blk = E(nc.Block())
        S = {"blk": blk, "ld": Sem(ctx, nc, "ld"), "st": Sem(ctx, nc, "st"),
             "sc": Sem(ctx, nc, "sc"), "ve": Sem(ctx, nc, "ve"),
             "te": Sem(ctx, nc, "te")}

        @blk.vector
        def _(ve):
            ve.memset(ones_col[:, :], 1.0)
            inst = ve.memset(ones_row[:, :], 1.0)
            S["ve"].inc(inst)

        def binarize(wT_ap, bw_ap, nkt, nfree, dtot):
            for f0 in range(0, nfree, 512):
                for kt in range(nkt):
                    @blk.sync
                    def _(sy, kt=kt, f0=f0):
                        S["sc"].wait(sy)
                        S["te"].wait(sy)
                        S["ld"].dma(sy.dma_start(
                            out=wtile[:, 0:512],
                            in_=wT_ap.rearrange("(kt p) f -> p kt f", p=P)[:, kt, f0:f0 + 512]))

                    @blk.scalar
                    def _(sc, kt=kt):
                        S["ld"].wait(sc)
                        inst = sc.activation(scr[:, 0:512], wtile[:, 0:512], AF.Abs)
                        S["sc"].inc(inst)

                    @blk.tensor
                    def _(te, kt=kt, f0=f0):
                        S["sc"].wait(te)
                        inst = te.matmul(p_al[:, 0:512], ones_col[:, :], scr[:, 0:512],
                                         start=(kt == 0), stop=(kt == nkt - 1))
                        S["te"].inc(inst)

                @blk.scalar
                def _(sc, f0=f0):
                    S["te"].wait(sc)
                    inst = sc.mul(alpha[:, f0:f0 + 512], p_al[:, 0:512], 1.0 / dtot)
                    S["sc"].inc(inst)

            for f0 in range(0, nfree, 512):
                @blk.tensor
                def _(te, f0=f0):
                    S["sc"].wait(te)
                    S["ve"].wait(te)
                    inst = te.matmul(p_alb[:, 0:512], ones_row[:, :], alpha[0:1, f0:f0 + 512],
                                     start=True, stop=True)
                    S["te"].inc(inst)

                @blk.vector
                def _(ve, f0=f0):
                    S["te"].wait(ve)
                    inst = ve.tensor_copy(alpha_b[:, f0:f0 + 512], p_alb[:, 0:512])
                    S["ve"].inc(inst)

            for kt in range(nkt):
                @blk.sync
                def _(sy, kt=kt):
                    S["ve"].wait(sy)
                    S["sc"].wait(sy)
                    S["st"].wait(sy)
                    S["ld"].dma(sy.dma_start(
                        out=wtile[:, 0:nfree],
                        in_=wT_ap.rearrange("(kt p) f -> p kt f", p=P)[:, kt, :]))

                @blk.scalar
                def _(sc, kt=kt):
                    S["ld"].wait(sc)
                    inst = sc.activation(scr[:, 0:nfree], wtile[:, 0:nfree], AF.Sign)
                    S["sc"].inc(inst)

                @blk.vector
                def _(ve, kt=kt):
                    S["sc"].wait(ve)
                    inst = ve.tensor_mul(bw_st[:, 0:nfree], scr[:, 0:nfree],
                                         alpha_b[:, 0:nfree])
                    S["ve"].inc(inst)

                @blk.sync
                def _(sy, kt=kt):
                    S["ve"].wait(sy)
                    S["st"].dma(sy.dma_start(
                        out=bw_ap.rearrange("(kt p) f -> p kt f", p=P)[:, kt, :],
                        in_=bw_st[:, 0:nfree]))

        binarize(fw1T, bw1T_o, 8, 4096, D)
        binarize(fw2T, bw2T_o, 32, D, 4096)

        @blk.sync
        def _(sy):
            S["st"].wait(sy)
    return nc


def _t(a):
    return np.ascontiguousarray(a.T)


def _pp(v, ntile):
    return np.ascontiguousarray(v.reshape(ntile, 128).T)


def kernel(**ins):
    cores = list(range(8))
    x = ins["L0_attn_x"]
    x_fm = [_t(x[b]) for b in range(4)]

    ncA = build_launch_A()
    whgT = _t(ins["L0_attn_W_hg"])
    bhgp = _pp(ins["L0_attn_b_hg"], 24)
    im = []
    for c in cores:
        b, h = c // 2, c % 2
        im.append({"x": np.ascontiguousarray(x_fm[b][:, h * TPC:(h + 1) * TPC]),
                   "whgT": whgT, "bhg": bhgp})
    rA = run_bass_kernel_spmd(ncA, im, core_ids=cores).results

    a_full = [np.concatenate([rA[2 * b]["a_out"], rA[2 * b + 1]["a_out"]], 1) for b in range(4)]
    b_full = [np.concatenate([rA[2 * b]["b_out"], rA[2 * b + 1]["b_out"]], 1) for b in range(4)]

    ncB = build_launch_B()
    h0 = ins["L0_attn_h0"]
    im = []
    for c in cores:
        b, h = c // 2, c % 2
        chs = slice(h * 768, (h + 1) * 768)
        im.append({"a_in": np.ascontiguousarray(a_full[b][chs]),
                   "b_in": np.ascontiguousarray(b_full[b][chs]),
                   "h0": np.ascontiguousarray(h0[chs].reshape(6, 128).T)})
    rB = run_bass_kernel_spmd(ncB, im, core_ids=cores).results
    h_full = [np.concatenate([rB[2 * b]["h_out"], rB[2 * b + 1]["h_out"]], 0) for b in range(4)]

    ncC = build_launch_C()
    woutT = _t(ins["L0_attn_W_out"])
    im = []
    for c in cores:
        b, h = c // 2, c % 2
        ts = slice(h * TPC, (h + 1) * TPC)
        im.append({"h_in": np.ascontiguousarray(h_full[b][:, ts]),
                   "x": np.ascontiguousarray(x_fm[b][:, ts]),
                   "woutT": woutT,
                   "bout": _pp(ins["L0_attn_b_out"], 8),
                   "lng": _pp(ins["L0_ffn_ln_g"], 8),
                   "lnb": _pp(ins["L0_ffn_ln_b"], 8),
                   "centT": _t(ins["L0_ffn_centroids"])})
    rC = run_bass_kernel_spmd(ncC, im, core_ids=cores).results

    scores = np.concatenate([rC[c]["sc_o"] for c in cores], axis=1)

    ncD = build_launch_D()
    pgrid, fgrid = np.meshgrid(np.arange(128), np.arange(256), indexing="ij")
    iota = (pgrid * 256 + fgrid)
    im = [{"sc_in": np.ascontiguousarray(scores),
           "triu": np.ascontiguousarray(np.triu(np.ones((128, 128), np.float32), 1)),
           "iota_f": iota.astype(np.float32),
           "iota_i": iota.astype(np.int32),
           "ident": np.eye(128, dtype=np.float32)}] * 8
    rD = run_bass_kernel_spmd(ncD, im, core_ids=cores).results
    t2s_flat = rD[0]["t2s_o"].reshape(-1)
    s2t_flat = rD[0]["s2t_o"].reshape(-1)

    normG = np.concatenate([_t(rC[c]["normbf_o"]) for c in cores], axis=0)
    ncE = build_launch_E()
    w1 = ins["L0_ffn_w1"]; b1 = ins["L0_ffn_b1"]
    w2 = ins["L0_ffn_w2"]; b2 = ins["L0_ffn_b2"]
    im = []
    for c in cores:
        sl = s2t_flat[c * TPC:(c + 1) * TPC]
        im.append({"normG": normG,
                   "s2t": np.ascontiguousarray(sl.reshape(32, 128).T.astype(np.int32)),
                   "w1e": np.ascontiguousarray(w1[c]),
                   "b1p": _pp(b1[c], 32),
                   "w2e": np.ascontiguousarray(w2[c]),
                   "b2p": _pp(b2[c], 8)})
    rE = run_bass_kernel_spmd(ncE, im, core_ids=cores).results

    ffnG = np.concatenate([_t(rE[c]["slab"]) for c in cores], axis=0)
    ncG1 = build_launch_G1()
    wcatT = np.concatenate([
        _t(ins["L1_attn_Wconcept"]), _t(ins["L1_attn_Wcg"]), _t(ins["L1_attn_Wsp"]),
        _t(ins["L1_attn_Wgate"]), _t(ins["L1_attn_Wfg"]), _t(ins["L1_attn_Wband"]),
        _t(ins["L1_attn_Wtg"]), np.zeros((1024, 123), np.float32)], axis=1)
    bcatv = np.concatenate([
        ins["L1_attn_bconcept"], ins["L1_attn_bcg"], np.zeros(1024, np.float32),
        ins["L1_attn_bgate"], ins["L1_attn_bfg"], ins["L1_attn_bband"],
        ins["L1_attn_btg"], np.zeros(123, np.float32)])
    bcatp = _pp(bcatv, 37)
    im = []
    for c in cores:
        tl = t2s_flat[c * TPC:(c + 1) * TPC]
        im.append({"x2": np.ascontiguousarray(rC[c]["x2_o"]),
                   "feat": np.ascontiguousarray(rC[c]["feat_o"]),
                   "ffnG": ffnG,
                   "t2s": np.ascontiguousarray(tl.reshape(32, 128).T.astype(np.int32)),
                   "wcatT": np.ascontiguousarray(wcatT),
                   "bcat": bcatp,
                   "ng": _pp(ins["L1_attn_norm_g"], 8),
                   "nb": _pp(ins["L1_attn_norm_b"], 8)})
    rG1 = run_bass_kernel_spmd(ncG1, im, core_ids=cores).results

    u_re_full = [np.concatenate([rG1[2 * b]["u_re_o"], rG1[2 * b + 1]["u_re_o"]], 1) for b in range(4)]
    u_im_full = [np.concatenate([rG1[2 * b]["u_im_o"], rG1[2 * b + 1]["u_im_o"]], 1) for b in range(4)]
    ts_full = [np.concatenate([rG1[2 * b]["ts_o"], rG1[2 * b + 1]["ts_o"]], 1) for b in range(4)]
    band_full = [np.concatenate([rG1[2 * b]["band_o"], rG1[2 * b + 1]["band_o"]], 1) for b in range(4)]
    ncF = build_launch_F()
    consts = fft_host_consts()
    ld_ = ins["L1_attn_log_decay"]; fi_ = ins["L1_attn_freq_importance"]
    fr_ = ins["L1_attn_frequencies"]; dt_ = ins["L1_attn_dt"]
    iota1 = np.arange(1, SEQ + 1, dtype=np.float32).reshape(1, SEQ)
    im = []
    for c in cores:
        b, fh = c // 2, c % 2
        fs = slice(256 * fh, 256 * (fh + 1))
        d = {"u_re": np.ascontiguousarray(u_re_full[b][fs]),
             "u_im": np.ascontiguousarray(u_im_full[b][fs]),
             "tsrow": np.ascontiguousarray(ts_full[b]),
             "bandr": np.ascontiguousarray(band_full[b][2 * fh:2 * fh + 2].reshape(1, -1)),
             "ld_loc": np.ascontiguousarray(ld_[fs].reshape(1, 256)),
             "fr_loc": np.ascontiguousarray(fr_[fs].reshape(1, 256)),
             "dt_loc": np.ascontiguousarray(dt_[fs].reshape(1, 256)),
             "fi_roll": np.ascontiguousarray(np.roll(fi_, -256 * fh).reshape(1, 512)),
             "iota1": iota1}
        d.update(consts)
        im.append(d)
    rF = run_bass_kernel_spmd(ncF, im, core_ids=cores).results

    y_re_full = [np.concatenate([rF[2 * b]["y_re_o"], rF[2 * b + 1]["y_re_o"]], 0) for b in range(4)]
    y_im_full = [np.concatenate([rF[2 * b]["y_im_o"], rF[2 * b + 1]["y_im_o"]], 0) for b in range(4)]
    ncG2a = build_launch_G2a()
    im = []
    for c in cores:
        b, h = c // 2, c % 2
        ts = slice(h * TPC, (h + 1) * TPC)
        spec = np.concatenate([y_re_full[b][:, ts], y_im_full[b][:, ts]], axis=0)
        im.append({"z_in": np.ascontiguousarray(rG1[c]["z_o"]),
                   "spec": np.ascontiguousarray(spec),
                   "cp": np.ascontiguousarray(rG1[c]["cp_o"]),
                   "gv": np.ascontiguousarray(rG1[c]["gv_o"]),
                   "wfsT": _t(ins["L1_attn_Wfs"]),
                   "osc": np.ascontiguousarray(ins["L1_attn_output_scale"].reshape(1, 1))})
    rG2a = run_bass_kernel_spmd(ncG2a, im, core_ids=cores).results

    ncG2f = build_launch_G2f()
    im = []
    for c in cores:
        im.append({"x3": np.ascontiguousarray(rG1[c]["x3_o"]),
                   "x4": np.ascontiguousarray(rG2a[c]["x4_o"]),
                   "n2g": _pp(ins["L1_attn_norm2_g"], 8),
                   "n2b": _pp(ins["L1_attn_norm2_b"], 8),
                   "w1T": _t(ins["L1_attn_W1"]),
                   "b1p": _pp(ins["L1_attn_b1"], 32),
                   "w2T": _t(ins["L1_attn_W2"]),
                   "b2p": _pp(ins["L1_attn_b2"], 8)})
    rG2f = run_bass_kernel_spmd(ncG2f, im, core_ids=cores).results

    ncBIN = build_launch_BIN()
    im = [{"fw1T": _t(ins["L1_ffn_w1"]), "fw2T": _t(ins["L1_ffn_w2"])}] * 8
    rBIN = run_bass_kernel_spmd(ncBIN, im, core_ids=cores).results
    bw1T = rBIN[0]["bw1T_o"]
    bw2T = rBIN[0]["bw2T_o"]

    ncFFN = build_launch_G2f(w_dtype=BF16)
    zeros_fm = np.zeros((D, TPC), np.float32)
    ones8 = np.ones((P, 8), np.float32)
    zeros8 = np.zeros((P, 8), np.float32)
    im = []
    for c in cores:
        im.append({"x3": zeros_fm,
                   "x4": np.ascontiguousarray(rG2f[c]["x5_o"]),
                   "n2g": ones8, "n2b": zeros8,
                   "w1T": bw1T,
                   "b1p": _pp(ins["L1_ffn_b1"], 32),
                   "w2T": bw2T,
                   "b2p": _pp(ins["L1_ffn_b2"], 8)})
    rFFN = run_bass_kernel_spmd(ncFFN, im, core_ids=cores).results

    out = np.empty((4, SEQ, D), np.float32)
    for c in cores:
        b, h = c // 2, c % 2
        out[b, h * TPC:(h + 1) * TPC, :] = np.asarray(rFFN[c]["x5_o"], np.float32).T
    return out


# version 03
# speedup vs baseline: 1.2641x; 1.0682x over previous
import numpy as np
import ml_dtypes
from contextlib import ExitStack

import concourse.bass as bass
import concourse.mybir as mybir
from concourse.bass_utils import run_bass_kernel_spmd

F32 = mybir.dt.float32
BF16 = mybir.dt.bfloat16
I32 = mybir.dt.int32
AF = mybir.ActivationFunctionType
ALU = mybir.AluOpType
BF = ml_dtypes.bfloat16

P = 128
D = 1024
TPC = 4096
SEQ = 8192
B = 4
NCORE = 8
LN_EPS = 1e-5


class Sem:

    def __init__(self, ctx, nc, name):
        self.h = ctx.enter_context(nc.semaphore(name))
        self.n = 0

    def inc(self, inst, k=1):
        inst.then_inc(self.h, k)
        self.n += k
        return inst

    def dma(self, inst):
        return self.inc(inst, 16)

    def wait(self, eng, lag=0):
        v = self.n - lag
        if v > 0:
            eng.wait_ge(self.h, v)

    def wait_at(self, eng, v):
        if v > 0:
            eng.wait_ge(self.h, v)


def fm_ap(dram_ap, d_total, t0, nt):
    return dram_ap[:, t0:t0 + nt].rearrange("(kt p) t -> p kt t", p=P)


def emit_ln(nc, S, xc, kt_n, nt, ones_col, ones_row, p_sum, p_sq, p_mu, p_rstd,
            xsq, murow, qrow, mu2, varrow, rstdrow, eps_ap, mu_b, rstd_b,
            gates=("ld",)):
    blk = S["blk"]
    dtot = kt_n * P

    @blk.scalar
    def _(sc):
        for g in gates:
            S[g].wait(sc)
        for kt in range(kt_n):
            inst = sc.activation(xsq[:, kt, :], xc[:, kt, :], AF.Square)
        S["sc"].inc(inst)

    @blk.tensor
    def _(te):
        for g in gates:
            S[g].wait(te)
        for kt in range(kt_n):
            te.matmul(p_sum[:, :], ones_col[:, :], xc[:, kt, :],
                      start=(kt == 0), stop=(kt == kt_n - 1))
        S["sc"].wait(te)
        for kt in range(kt_n):
            inst = te.matmul(p_sq[:, :], ones_col[:, :], xsq[:, kt, :],
                             start=(kt == 0), stop=(kt == kt_n - 1))
        S["te"].inc(inst)

    @blk.scalar
    def _(sc):
        S["te"].wait(sc)
        sc.mul(murow[:, :], p_sum[:, :], 1.0 / dtot)
        inst = sc.mul(qrow[:, :], p_sq[:, :], 1.0 / dtot)
        S["sc"].inc(inst)

    @blk.vector
    def _(ve):
        S["sc"].wait(ve)
        ve.tensor_mul(mu2[:, :], murow[:, :], murow[:, :])
        inst = ve.tensor_sub(varrow[:, :], qrow[:, :], mu2[:, :])
        S["ve"].inc(inst)

    @blk.scalar
    def _(sc):
        S["ve"].wait(sc)
        sc.activation(varrow[:, :], varrow[:, :], AF.Ln, bias=eps_ap[:1, :])
        inst = sc.activation(rstdrow[:, :], varrow[:, :], AF.Exp, scale=-0.5)
        S["sc"].inc(inst)

    @blk.tensor
    def _(te):
        S["sc"].wait(te)
        te.matmul(p_mu[:, :], ones_row[:, :], murow[:, :], start=True, stop=True)
        inst = te.matmul(p_rstd[:, :], ones_row[:, :], rstdrow[:, :], start=True, stop=True)
        S["te"].inc(inst)

    @blk.vector
    def _(ve):
        S["te"].wait(ve)
        ve.tensor_copy(mu_b[:, :], p_mu[:, :])
        inst = ve.tensor_copy(rstd_b[:, :], p_rstd[:, :])
        S["ve"].inc(inst)


def build_launch_A(nt=512):
    DI = 1536
    HG = 2 * DI
    nch = TPC // nt
    nc = bass.Bass("TRN2", target_bir_lowering=False)
    x = nc.dram_tensor("x", [D, TPC], F32, kind="ExternalInput").ap()
    whgT = nc.dram_tensor("whgT", [D, HG], F32, kind="ExternalInput").ap()
    bhg = nc.dram_tensor("bhg", [P, HG // P], F32, kind="ExternalInput").ap()
    a_out = nc.dram_tensor("a_out", [DI, TPC], F32, kind="ExternalOutput").ap()
    b_out = nc.dram_tensor("b_out", [DI, TPC], F32, kind="ExternalOutput").ap()

    with ExitStack() as ctx:
        E = ctx.enter_context
        w_sb = E(nc.sbuf_tensor("w_sb", [P, 8, HG], BF16))
        bhg_sb = E(nc.sbuf_tensor("bhg_sb", [P, 24], F32))
        bhg_neg = E(nc.sbuf_tensor("bhg_neg", [P, 24], F32))
        bhg_half = E(nc.sbuf_tensor("bhg_half", [P, 24], F32))
        xc = E(nc.sbuf_tensor("xc", [P, 8, nt], F32))
        xsq = E(nc.sbuf_tensor("xsq", [P, 8, nt], F32))
        xnb = E(nc.sbuf_tensor("xnb", [P, 8, nt], BF16))
        ones_col = E(nc.sbuf_tensor("ones_col", [P, 1], F32))
        ones_row = E(nc.sbuf_tensor("ones_row", [1, P], F32))
        eps_ap = E(nc.sbuf_tensor("eps_ap", [1, 1], F32))
        murow = E(nc.sbuf_tensor("murow", [1, nt], F32))
        qrow = E(nc.sbuf_tensor("qrow", [1, nt], F32))
        mu2 = E(nc.sbuf_tensor("mu2", [1, nt], F32))
        varrow = E(nc.sbuf_tensor("varrow", [1, nt], F32))
        rstdrow = E(nc.sbuf_tensor("rstdrow", [1, nt], F32))
        mu_b = E(nc.sbuf_tensor("mu_b", [P, nt], F32))
        rstd_b = E(nc.sbuf_tensor("rstd_b", [P, nt], F32))
        sig_g = E(nc.sbuf_tensor("sig_g", [P, nt], F32))
        sig_h = E(nc.sbuf_tensor("sig_h", [P, nt], F32))
        lin_h = E(nc.sbuf_tensor("lin_h", [P, nt], F32))
        mask_h = E(nc.sbuf_tensor("mask_h", [P, nt], mybir.dt.uint8))
        g_h = E(nc.sbuf_tensor("g_h", [P, nt], F32))
        a_st2 = [E(nc.sbuf_tensor("a_st0", [P, nt], F32)),
                 E(nc.sbuf_tensor("a_st1", [P, nt], F32))]
        b_st2 = [E(nc.sbuf_tensor("b_st0", [P, nt], F32)),
                 E(nc.sbuf_tensor("b_st1", [P, nt], F32))]

        p_sum = E(nc.psum_tensor("p_sum", [1, nt], F32))
        p_sq = E(nc.psum_tensor("p_sq", [1, nt], F32))
        p_mu = E(nc.psum_tensor("p_mu", [P, nt], F32))
        p_rstd = E(nc.psum_tensor("p_rstd", [P, nt], F32))
        p_h2 = [E(nc.psum_tensor("p_h0", [P, nt], F32)),
                E(nc.psum_tensor("p_h1", [P, nt], F32))]
        p_g2 = [E(nc.psum_tensor("p_g0", [P, nt], F32)),
                E(nc.psum_tensor("p_g1", [P, nt], F32))]

        blk = E(nc.Block())
        S = {
            "blk": blk,
            "ld": Sem(ctx, nc, "ld"), "st": Sem(ctx, nc, "st"),
            "sc": Sem(ctx, nc, "sc"), "ve": Sem(ctx, nc, "ve"),
            "te": Sem(ctx, nc, "te"),
        }

        bank_guard = [(), ()]
        st_guard = [0, 0]

        @blk.gpsimd
        def _(gp):
            S["ld"].dma(gp.dma_start(
                out=w_sb[:, :, :], in_=whgT.rearrange("(kt p) c -> p kt c", p=P)))

        @blk.sync
        def _(sy):
            S["ld"].dma(sy.dma_start(out=bhg_sb[:, :], in_=bhg))

        @blk.vector
        def _(ve):
            ve.memset(ones_col[:, :], 1.0)
            ve.memset(ones_row[:, :], 1.0)
            ve.memset(eps_ap[:, :], LN_EPS)
            S["ld"].wait(ve)
            ve.tensor_scalar_mul(bhg_neg[:, :], bhg_sb[:, :], -1.0)
            inst = ve.tensor_scalar_add(bhg_half[:, :], bhg_sb[:, :], 0.5)
            S["ve"].inc(inst)

        for ci in range(nch):
            t0 = ci * nt

            @blk.sync
            def _(sy):
                S["st"].wait(sy)
                S["ve"].wait(sy)
                S["sc"].wait(sy)
                S["te"].wait(sy)
                S["ld"].dma(sy.dma_start(out=xc[:, :, :], in_=fm_ap(x, D, t0, nt)))

            emit_ln(nc, S, xc, 8, nt, ones_col, ones_row, p_sum, p_sq, p_mu, p_rstd,
                    xsq, murow, qrow, mu2, varrow, rstdrow, eps_ap, mu_b, rstd_b)

            @blk.vector
            def _(ve):
                S["ve"].wait(ve)
                for kt in range(8):
                    ve.tensor_sub(xsq[:, kt, :], xc[:, kt, :], mu_b[:, :])
                    inst = ve.tensor_mul(xnb[:, kt, :], xsq[:, kt, :], rstd_b[:, :])
                S["ve"].inc(inst)

            xnb_ready = S["ve"].n
            for j in range(12):

                @blk.tensor
                def _(te, j=j):
                    S["ve"].wait_at(te, xnb_ready)
                    for g0 in bank_guard[j % 2]:
                        S[g0[0]].wait_at(te, g0[1])
                    for kt in range(8):
                        te.matmul(p_h2[j % 2][:, :], w_sb[:, kt, j * P:(j + 1) * P],
                                  xnb[:, kt, :], start=(kt == 0), stop=(kt == 7))
                    for kt in range(8):
                        inst = te.matmul(p_g2[j % 2][:, :], w_sb[:, kt, (12 + j) * P:(13 + j) * P],
                                         xnb[:, kt, :], start=(kt == 0), stop=(kt == 7))
                    S["te"].inc(inst)

                @blk.scalar
                def _(sc, j=j):
                    S["te"].wait(sc)
                    S["ve"].wait(sc)
                    S["st"].wait_at(sc, st_guard[j % 2])
                    sc.activation(sig_g[:, :], p_g2[j % 2][:, :], AF.Sigmoid,
                                  bias=bhg_sb[:, 12 + j:13 + j])
                    sc.activation(a_st2[j % 2][:, :], p_g2[j % 2][:, :], AF.Sigmoid,
                                  bias=bhg_neg[:, 12 + j:13 + j], scale=-1.0)
                    inst = sc.activation(sig_h[:, :], p_h2[j % 2][:, :], AF.Sigmoid,
                                         bias=bhg_sb[:, j:j + 1])
                    S["sc"].inc(inst)

                @blk.vector
                def _(ve, j=j):
                    S["sc"].wait(ve)
                    ve.tensor_scalar_add(lin_h[:, :], p_h2[j % 2][:, :], bhg_half[:, j:j + 1])
                    ve.tensor_scalar(mask_h[:, :], p_h2[j % 2][:, :],
                                     bhg_neg[:, j:j + 1], None,
                                     op0=ALU.is_ge)
                    ve.select(g_h[:, :], mask_h[:, :], lin_h[:, :], sig_h[:, :])
                    inst = ve.tensor_mul(b_st2[j % 2][:, :], sig_g[:, :], g_h[:, :])
                    S["ve"].inc(inst)

                bank_guard[j % 2] = (("sc", S["sc"].n), ("ve", S["ve"].n))

                @blk.sync
                def _(sy, j=j, t0=t0):
                    S["ve"].wait(sy)
                    S["st"].dma(sy.dma_start(
                        out=a_out[j * P:(j + 1) * P, t0:t0 + nt], in_=a_st2[j % 2][:, :]))
                    S["st"].dma(sy.dma_start(
                        out=b_out[j * P:(j + 1) * P, t0:t0 + nt], in_=b_st2[j % 2][:, :]))

                st_guard[j % 2] = S["st"].n

        @blk.sync
        def _(sy):
            S["st"].wait(sy)
    return nc




def build_launch_B():
    CH = 768
    NJ = CH // P
    nc = bass.Bass("TRN2", target_bir_lowering=False)
    a_in = nc.dram_tensor("a_in", [CH, SEQ], F32, kind="ExternalInput").ap()
    b_in = nc.dram_tensor("b_in", [CH, SEQ], F32, kind="ExternalInput").ap()
    h0 = nc.dram_tensor("h0", [P, NJ], F32, kind="ExternalInput").ap()
    h_out = nc.dram_tensor("h_out", [CH, SEQ], F32, kind="ExternalOutput").ap()

    with ExitStack() as ctx:
        E = ctx.enter_context
        s_a = E(nc.sbuf_tensor("s_a", [P, SEQ], F32))
        s_b = E(nc.sbuf_tensor("s_b", [P, SEQ], F32))
        s_h = E(nc.sbuf_tensor("s_h", [P, SEQ], F32))
        s_h0 = E(nc.sbuf_tensor("s_h0", [P, NJ], F32))
        s_sig = E(nc.sbuf_tensor("s_sig", [P, NJ], F32))
        s_lin = E(nc.sbuf_tensor("s_lin", [P, NJ], F32))
        s_msk = E(nc.sbuf_tensor("s_msk", [P, NJ], mybir.dt.uint8))
        s_h0g = E(nc.sbuf_tensor("s_h0g", [P, NJ], F32))
        blk = E(nc.Block())
        S = {"blk": blk, "ld": Sem(ctx, nc, "ld"), "st": Sem(ctx, nc, "st"),
             "sc": Sem(ctx, nc, "sc"), "ve": Sem(ctx, nc, "ve")}

        @blk.sync
        def _(sy):
            S["ld"].dma(sy.dma_start(out=s_h0[:, :], in_=h0))

        @blk.scalar
        def _(sc):
            S["ld"].wait(sc)
            inst = sc.activation(s_sig[:, :], s_h0[:, :], AF.Sigmoid)
            S["sc"].inc(inst)

        @blk.vector
        def _(ve):
            S["sc"].wait(ve)
            ve.tensor_scalar_add(s_lin[:, :], s_h0[:, :], 0.5)
            ve.tensor_scalar(s_msk[:, :], s_h0[:, :], 0.0, None, op0=ALU.is_ge)
            inst = ve.select(s_h0g[:, :], s_msk[:, :], s_lin[:, :], s_sig[:, :])
            S["ve"].inc(inst)

        for j in range(NJ):
            @blk.sync
            def _(sy, j=j):
                S["ve"].wait(sy)
                S["st"].wait(sy)
                S["ld"].dma(sy.dma_start(out=s_a[:, :], in_=a_in[j * P:(j + 1) * P, :]))
                S["ld"].dma(sy.dma_start(out=s_b[:, :], in_=b_in[j * P:(j + 1) * P, :]))

            @blk.vector
            def _(ve, j=j):
                S["ld"].wait(ve)
                inst = ve.tensor_tensor_scan(
                    out=s_h[:, :], data0=s_a[:, :], data1=s_b[:, :],
                    initial=s_h0g[:, j:j + 1], op0=ALU.mult, op1=ALU.add)
                S["ve"].inc(inst)

            @blk.sync
            def _(sy, j=j):
                S["ve"].wait(sy)
                S["st"].dma(sy.dma_start(out=h_out[j * P:(j + 1) * P, :], in_=s_h[:, :]))

        @blk.sync
        def _(sy):
            S["st"].wait(sy)
    return nc


def build_launch_C(nt=512, nch=None, stage=3):
    DI = 1536
    if nch is None:
        nch = TPC // nt
    nc = bass.Bass("TRN2", target_bir_lowering=False)
    h_in = nc.dram_tensor("h_in", [DI, TPC], F32, kind="ExternalInput").ap()
    x = nc.dram_tensor("x", [D, TPC], F32, kind="ExternalInput").ap()
    woutT = nc.dram_tensor("woutT", [DI, D], F32, kind="ExternalInput").ap()
    bout = nc.dram_tensor("bout", [P, 8], F32, kind="ExternalInput").ap()
    lng = nc.dram_tensor("lng", [P, 8], F32, kind="ExternalInput").ap()
    lnb = nc.dram_tensor("lnb", [P, 8], F32, kind="ExternalInput").ap()
    centT = nc.dram_tensor("centT", [D, 8], F32, kind="ExternalInput").ap()
    x2_o = nc.dram_tensor("x2_o", [D, TPC], F32, kind="ExternalOutput").ap()
    feat_o = nc.dram_tensor("feat_o", [D, TPC], F32, kind="ExternalOutput").ap()
    norm_o = nc.dram_tensor("norm_o", [D, TPC], F32, kind="ExternalOutput").ap()
    normbf_o = nc.dram_tensor("normbf_o", [D, TPC], BF16, kind="ExternalOutput").ap()
    sc_o = nc.dram_tensor("sc_o", [8, TPC], F32, kind="ExternalOutput").ap()

    with ExitStack() as ctx:
        E = ctx.enter_context
        w_sb = E(nc.sbuf_tensor("w_sb", [P, 12, D], BF16))
        cent_sb = E(nc.sbuf_tensor("cent_sb", [P, 8, 8], F32))
        bout_sb = E(nc.sbuf_tensor("bout_sb", [P, 8], F32))
        lng_sb = E(nc.sbuf_tensor("lng_sb", [P, 8], F32))
        lnb_sb = E(nc.sbuf_tensor("lnb_sb", [P, 8], F32))
        hc = E(nc.sbuf_tensor("hc", [P, 12, nt], BF16))
        xc = E(nc.sbuf_tensor("xc", [P, 8, nt], F32))
        x2c = E(nc.sbuf_tensor("x2c", [P, 8, nt], F32))
        featc = E(nc.sbuf_tensor("featc", [P, 8, nt], F32))
        normc = E(nc.sbuf_tensor("normc", [P, 8, nt], F32))
        normb = E(nc.sbuf_tensor("normb", [P, 8, nt], BF16))
        xsq = E(nc.sbuf_tensor("xsq", [P, 8, nt], F32))
        ones_col = E(nc.sbuf_tensor("ones_col", [P, 1], F32))
        ones_row = E(nc.sbuf_tensor("ones_row", [1, P], F32))
        eps_ap = E(nc.sbuf_tensor("eps_ap", [1, 1], F32))
        murow = E(nc.sbuf_tensor("murow", [1, nt], F32))
        qrow = E(nc.sbuf_tensor("qrow", [1, nt], F32))
        mu2 = E(nc.sbuf_tensor("mu2", [1, nt], F32))
        varrow = E(nc.sbuf_tensor("varrow", [1, nt], F32))
        rstdrow = E(nc.sbuf_tensor("rstdrow", [1, nt], F32))
        mu_b = E(nc.sbuf_tensor("mu_b", [P, nt], F32))
        rstd_b = E(nc.sbuf_tensor("rstd_b", [P, nt], F32))
        sc_st = E(nc.sbuf_tensor("sc_st", [8, nt], F32))

        p_y2 = [E(nc.psum_tensor("p_y0", [P, nt], F32)),
                E(nc.psum_tensor("p_y1", [P, nt], F32))]
        p_sum = E(nc.psum_tensor("p_sum", [1, nt], F32))
        p_sq = E(nc.psum_tensor("p_sq", [1, nt], F32))
        p_mu = E(nc.psum_tensor("p_mu", [P, nt], F32))
        p_rstd = E(nc.psum_tensor("p_rstd", [P, nt], F32))
        p_sc = E(nc.psum_tensor("p_sc", [8, nt], F32))

        blk = E(nc.Block())
        S = {"blk": blk, "ld": Sem(ctx, nc, "ld"), "st": Sem(ctx, nc, "st"),
             "sc": Sem(ctx, nc, "sc"), "ve": Sem(ctx, nc, "ve"),
             "te": Sem(ctx, nc, "te")}

        @blk.gpsimd
        def _(gp):
            S["ld"].dma(gp.dma_start(
                out=w_sb[:, :, :], in_=woutT.rearrange("(ct p) d -> p ct d", p=P)))

        @blk.sync
        def _(sy):
            S["ld"].dma(sy.dma_start(
                out=cent_sb[:, :, :], in_=centT.rearrange("(kt p) e -> p kt e", p=P)))
            S["ld"].dma(sy.dma_start(out=bout_sb[:, :], in_=bout))
            S["ld"].dma(sy.dma_start(out=lng_sb[:, :], in_=lng))
            S["ld"].dma(sy.dma_start(out=lnb_sb[:, :], in_=lnb))

        @blk.vector
        def _(ve):
            ve.memset(ones_col[:, :], 1.0)
            ve.memset(ones_row[:, :], 1.0)
            inst = ve.memset(eps_ap[:, :], LN_EPS)
            S["ve"].inc(inst)

        for ci in range(nch):
            t0 = ci * nt

            @blk.sync
            def _(sy):
                S["st"].wait(sy)
                S["ve"].wait(sy)
                S["sc"].wait(sy)
                S["te"].wait(sy)
                S["ld"].dma(sy.dma_start(out=xc[:, :, :], in_=fm_ap(x, D, t0, nt)))

            @blk.gpsimd
            def _(gp, t0=t0):
                S["st"].wait(gp)
                S["ve"].wait(gp)
                S["ld"].dma(gp.dma_start(
                    out=hc[:, :, :],
                    in_=h_in[:, t0:t0 + nt].rearrange("(ct p) t -> p ct t", p=P)))

            yg = [0, 0]
            for dt in range(8):
                @blk.tensor
                def _(te, dt=dt):
                    S["ld"].wait(te)
                    S["ve"].wait_at(te, yg[dt % 2])
                    for ctj in range(12):
                        inst = te.matmul(p_y2[dt % 2][:, :], w_sb[:, ctj, dt * P:(dt + 1) * P],
                                         hc[:, ctj, :], start=(ctj == 0), stop=(ctj == 11))
                    S["te"].inc(inst)

                @blk.vector
                def _(ve, dt=dt):
                    S["te"].wait(ve)
                    inst = ve.scalar_tensor_tensor(
                        out=x2c[:, dt, :], in0=p_y2[dt % 2][:, :], scalar=bout_sb[:, dt:dt + 1],
                        in1=xc[:, dt, :], op0=ALU.add, op1=ALU.add)
                    S["ve"].inc(inst)
                yg[dt % 2] = S["ve"].n

            if stage < 1:
                @blk.sync
                def _(sy, t0=t0):
                    S["ve"].wait(sy)
                    S["st"].dma(sy.dma_start(out=fm_ap(x2_o, D, t0, nt), in_=x2c[:, :, :]))
                continue

            emit_ln(nc, S, x2c, 8, nt, ones_col, ones_row, p_sum, p_sq, p_mu, p_rstd,
                    xsq, murow, qrow, mu2, varrow, rstdrow, eps_ap, mu_b, rstd_b,
                    gates=("ve",))

            @blk.vector
            def _(ve):
                S["ve"].wait(ve)
                for kt in range(8):
                    ve.tensor_sub(xsq[:, kt, :], x2c[:, kt, :], mu_b[:, :])
                    inst = ve.tensor_mul(featc[:, kt, :], xsq[:, kt, :], rstd_b[:, :])
                S["ve"].inc(inst)

            if stage < 2:
                @blk.sync
                def _(sy, t0=t0):
                    S["ve"].wait(sy)
                    S["st"].dma(sy.dma_start(out=fm_ap(x2_o, D, t0, nt), in_=x2c[:, :, :]))
                    S["st"].dma(sy.dma_start(out=fm_ap(feat_o, D, t0, nt), in_=featc[:, :, :]))
                continue

            emit_ln(nc, S, featc, 8, nt, ones_col, ones_row, p_sum, p_sq, p_mu, p_rstd,
                    xsq, murow, qrow, mu2, varrow, rstdrow, eps_ap, mu_b, rstd_b,
                    gates=("ve",))

            @blk.vector
            def _(ve):
                S["ve"].wait(ve)
                for kt in range(8):
                    ve.tensor_sub(xsq[:, kt, :], featc[:, kt, :], mu_b[:, :])
                    ve.tensor_mul(xsq[:, kt, :], xsq[:, kt, :], rstd_b[:, :])
                    ve.tensor_scalar(
                        normc[:, kt, :], xsq[:, kt, :],
                        lng_sb[:, kt:kt + 1], lnb_sb[:, kt:kt + 1],
                        op0=ALU.mult, op1=ALU.add)
                    inst = ve.tensor_copy(normb[:, kt, :], normc[:, kt, :])
                S["ve"].inc(inst)

            @blk.tensor
            def _(te):
                S["ve"].wait(te)
                S["sc"].wait(te)
                for kt in range(8):
                    inst = te.matmul(p_sc[:, :], cent_sb[:, kt, :], normc[:, kt, :],
                                     start=(kt == 0), stop=(kt == 7))
                S["te"].inc(inst)

            @blk.scalar
            def _(sc):
                S["te"].wait(sc)
                inst = sc.copy(sc_st[:, :], p_sc[:, :])
                S["sc"].inc(inst)

            @blk.sync
            def _(sy, t0=t0):
                S["ve"].wait(sy)
                S["sc"].wait(sy)
                S["st"].dma(sy.dma_start(out=fm_ap(x2_o, D, t0, nt), in_=x2c[:, :, :]))
                S["st"].dma(sy.dma_start(out=fm_ap(feat_o, D, t0, nt), in_=featc[:, :, :]))
                S["st"].dma(sy.dma_start(out=fm_ap(norm_o, D, t0, nt), in_=normc[:, :, :]))
                S["st"].dma(sy.dma_start(out=fm_ap(normbf_o, D, t0, nt), in_=normb[:, :, :]))
                S["st"].dma(sy.dma_start(out=sc_o[:, t0:t0 + nt], in_=sc_st[:, :]))

        @blk.sync
        def _(sy):
            S["st"].wait(sy)
    return nc


def build_launch_D(niter=32):
    T = 32768
    FE = T // P
    CAP = 4096.0
    NEG = -3.0e38
    nc = bass.Bass("TRN2", target_bir_lowering=False)
    sc_in = nc.dram_tensor("sc_in", [8, T], F32, kind="ExternalInput").ap()
    triu = nc.dram_tensor("triu", [P, P], F32, kind="ExternalInput").ap()
    iota_f = nc.dram_tensor("iota_f", [P, FE], F32, kind="ExternalInput").ap()
    iota_i = nc.dram_tensor("iota_i", [P, FE], I32, kind="ExternalInput").ap()
    t2s_o = nc.dram_tensor("t2s_o", [P, FE], I32, kind="ExternalOutput").ap()
    s2t_o = nc.dram_tensor("s2t_o", [T, 1], I32, kind="ExternalOutput").ap()
    ident = nc.dram_tensor("ident", [P, P], F32, kind="ExternalInput").ap()
    bmax_d = nc.dram_tensor("bmax_d", [P, 1], F32, kind="Internal").ap()
    bmin_d = nc.dram_tensor("bmin_d", [P, 1], F32, kind="Internal").ap()
    m_out = nc.dram_tensor("m_out", [P, 8 * FE], F32, kind="ExternalOutput").ap()
    dbg_o = nc.dram_tensor("dbg_o", [P, 40], F32, kind="ExternalOutput").ap()

    with ExitStack() as ctx:
        E = ctx.enter_context
        s_sc = E(nc.sbuf_tensor("s_sc", [P, 8, FE], F32))
        s_m = E(nc.sbuf_tensor("s_m", [P, 8, FE], F32))
        s_cur = E(nc.sbuf_tensor("s_cur", [P, FE], F32))
        s_taken = E(nc.sbuf_tensor("s_taken", [P, FE], mybir.dt.uint8))
        s_tkf = E(nc.sbuf_tensor("s_tkf", [P, FE], F32))
        s_id = E(nc.sbuf_tensor("s_id", [P, P], F32))
        s_neg = E(nc.sbuf_tensor("s_neg", [P, FE], F32))
        s_bounds = E(nc.sbuf_tensor("s_bounds", [P, 2], F32))
        junk = E(nc.sbuf_tensor("junk", [P, FE], F32))
        cnt = E(nc.sbuf_tensor("cnt", [P, 1], F32))
        tot_s = E(nc.sbuf_tensor("tot_s", [1, 1], F32))
        lo = E(nc.sbuf_tensor("lo", [P, 1], F32))
        hi = E(nc.sbuf_tensor("hi", [P, 1], F32))
        mid = E(nc.sbuf_tensor("mid", [P, 1], F32))
        lo_n = E(nc.sbuf_tensor("lo_n", [P, 1], F32))
        mid_ptr = E(nc.sbuf_tensor("mid_ptr", [P, 1], F32))
        hi_n = E(nc.sbuf_tensor("hi_n", [P, 1], F32))
        geb = E(nc.sbuf_tensor("geb", [P, 1], mybir.dt.uint8))
        ones_col = E(nc.sbuf_tensor("ones_col", [P, 1], F32))
        ones_row = E(nc.sbuf_tensor("ones_row", [1, P], F32))
        rowred = E(nc.sbuf_tensor("rowred", [P, 2], F32))
        glob = E(nc.sbuf_tensor("glob", [1, 2], F32))
        brow = E(nc.sbuf_tensor("brow", [1, 2 * P], F32))
        glob2 = E(nc.sbuf_tensor("glob2", [1, 1], F32))
        dbg_sb = E(nc.sbuf_tensor("dbg_sb", [P, 40], F32))
        s_triu = E(nc.sbuf_tensor("s_triu", [P, P], F32))
        s_iota = E(nc.sbuf_tensor("s_iota", [P, FE], F32))
        s_iota_i = E(nc.sbuf_tensor("s_iota_i", [P, FE], I32))
        s_one256 = E(nc.sbuf_tensor("s_one256", [P, FE], F32))
        s_cums = E(nc.sbuf_tensor("s_cums", [P, FE], F32))
        s_rowtot = E(nc.sbuf_tensor("s_rowtot", [P, 1], F32))
        s_pre = E(nc.sbuf_tensor("s_pre", [P, 1], F32))
        s_t2s = E(nc.sbuf_tensor("s_t2s", [P, FE], F32))
        s_t2si = E(nc.sbuf_tensor("s_t2si", [P, FE], I32))
        s_tmp = E(nc.sbuf_tensor("s_tmp", [P, FE], F32))
        p_pre = E(nc.psum_tensor("p_pre", [P, 1], F32))
        p_row = E(nc.psum_tensor("p_row", [1, 256], F32))
        p_bc = E(nc.psum_tensor("p_bc", [P, 2], F32))
        p_cntb = E(nc.psum_tensor("p_cntb", [P, 1], F32))

        blk = E(nc.Block())
        S = {"blk": blk, "ld": Sem(ctx, nc, "ld"), "st": Sem(ctx, nc, "st"),
             "ve": Sem(ctx, nc, "ve"), "te": Sem(ctx, nc, "te"),
             "sc": Sem(ctx, nc, "sc"), "vv": Sem(ctx, nc, "vv")}

        @blk.sync
        def _(sy):
            S["ld"].dma(sy.dma_start(
                out=s_sc[:, :, :], in_=sc_in.rearrange("e (p f) -> p e f", p=P)))
            S["ld"].dma(sy.dma_start(out=s_id[:, :], in_=ident))
            S["ld"].dma(sy.dma_start(out=s_triu[:, :], in_=triu))
            S["ld"].dma(sy.dma_start(out=s_iota[:, :], in_=iota_f))
            S["ld"].dma(sy.dma_start(out=s_iota_i[:, :], in_=iota_i))

        @blk.vector
        def _(ve):
            ve.memset(ones_col[:, :], 1.0)
            ve.memset(ones_row[:, :], 1.0)
            ve.memset(s_taken[:, :], 0)
            ve.memset(s_tkf[:, :], 0.0)
            ve.memset(s_neg[:, :], NEG)
            S["ld"].wait(ve)
            flat = s_sc[:, :, :].rearrange("p e f -> p (e f)")
            ve.tensor_reduce(rowred[:, 0:1], flat, mybir.AxisListType.X, ALU.max)
            inst = ve.tensor_reduce(rowred[:, 1:2], flat, mybir.AxisListType.X, ALU.min)
            S["ve"].inc(inst)

        @blk.vector
        def _(ve):
            S["ve"].wait(ve)
            inst = ve.tensor_scalar_mul(rowred[:, 1:2], rowred[:, 1:2], -1.0)
            S["ve"].inc(inst)

        @blk.sync
        def _(sy):
            S["ve"].wait(sy)
            S["st"].dma(sy.dma_start(out=bmax_d, in_=rowred[:, 0:1]))
            S["st"].dma(sy.dma_start(out=bmin_d, in_=rowred[:, 1:2]))

        @blk.sync
        def _(sy):
            S["st"].wait(sy)
            S["ld"].dma(sy.dma_start(out=brow[0:1, 0:P], in_=bmax_d.rearrange("p one -> (p one)")[None, :]))
            S["ld"].dma(sy.dma_start(out=brow[0:1, P:2 * P], in_=bmin_d.rearrange("p one -> (p one)")[None, :]))

        @blk.vector
        def _(ve):
            S["ld"].wait(ve)
            ve.tensor_reduce(glob[0:1, 0:1], brow[0:1, 0:P], mybir.AxisListType.X, ALU.max)
            inst = ve.tensor_reduce(glob2[0:1, 0:1], brow[0:1, P:2 * P], mybir.AxisListType.X, ALU.max)
            S["ve"].inc(inst)

        @blk.tensor
        def _(te):
            S["ve"].wait(te)
            te.matmul(p_bc[:, 0:1], ones_row[:, :], glob[0:1, 0:1], start=True, stop=True)
            inst = te.matmul(p_bc[:, 1:2], ones_row[:, :], glob2[0:1, 0:1], start=True, stop=True)
            S["te"].inc(inst)

        @blk.vector
        def _(ve):
            S["te"].wait(ve)
            ve.tensor_scalar_add(s_bounds[:, 0:1], p_bc[:, 0:1], 1.0)
            ve.tensor_scalar(s_bounds[:, 1:2], p_bc[:, 1:2], -1.0, -1.0,
                             op0=ALU.mult, op1=ALU.add)
            inst = ve.tensor_copy(dbg_sb[:, 32:34], s_bounds[:, :])
            S["ve"].inc(inst)

        for e in range(8):
            @blk.vector
            def _(ve, e=e):
                S["ve"].wait(ve)
                ve.select(s_cur[:, :], s_taken[:, :], s_neg[:, :], s_sc[:, e, :])
                ve.tensor_copy(hi[:, :], s_bounds[:, 0:1])
                inst = ve.tensor_copy(lo[:, :], s_bounds[:, 1:2])
                S["ve"].inc(inst)

            for it in range(niter):
                @blk.vector
                def _(ve):
                    S["ve"].wait(ve)
                    S["te"].wait(ve)
                    S["vv"].inc(ve.tensor_add(mid[:, :], lo[:, :], hi[:, :]))
                    S["vv"].wait(ve)
                    S["vv"].inc(ve.tensor_scalar_mul(mid[:, :], mid[:, :], 0.5))
                    S["vv"].wait(ve)
                    inst = ve.tensor_scalar(junk[:, :], s_cur[:, :], mid[:, :], None,
                                            op0=ALU.is_gt)
                    S["ve"].inc(inst)

                @blk.tensor
                def _(te):
                    S["ve"].wait(te)
                    inst = te.matmul(p_row[:, :], ones_col[:, :], junk[:, :],
                                     start=True, stop=True)
                    S["te"].inc(inst)

                @blk.vector
                def _(ve):
                    S["te"].wait(ve)
                    inst = ve.tensor_reduce(tot_s[:, :], p_row[:, :],
                                            mybir.AxisListType.X, ALU.add)
                    S["ve"].inc(inst)

                @blk.tensor
                def _(te):
                    S["ve"].wait(te)
                    inst = te.matmul(p_cntb[:, :], ones_row[:, :], tot_s[:, :],
                                     start=True, stop=True)
                    S["te"].inc(inst)

                @blk.vector
                def _(ve):
                    S["te"].wait(ve)
                    S["vv"].inc(ve.tensor_scalar(geb[:, :], p_cntb[:, :], CAP, None,
                                                 op0=ALU.is_ge))
                    S["vv"].wait(ve)
                    ve.select(lo_n[:, :], geb[:, :], mid[:, :], lo[:, :])
                    S["vv"].inc(ve.select(hi_n[:, :], geb[:, :], hi[:, :], mid[:, :]))
                    S["vv"].wait(ve)
                    ve.tensor_copy(lo[:, :], lo_n[:, :])
                    inst = ve.tensor_copy(hi[:, :], hi_n[:, :])
                    S["ve"].inc(inst)

            @blk.vector
            def _(ve, e=e):
                S["ve"].wait(ve)
                ve.tensor_copy(dbg_sb[:, 3 * e:3 * e + 1], lo[:, :])
                ve.tensor_copy(dbg_sb[:, 3 * e + 1:3 * e + 2], hi[:, :])
                ve.tensor_copy(dbg_sb[:, 3 * e + 2:3 * e + 3], p_cntb[:, :])
                ve.tensor_scalar(s_m[:, e, :], s_cur[:, :], lo[:, :], None, op0=ALU.is_gt)
                ve.tensor_add(s_tkf[:, :], s_tkf[:, :], s_m[:, e, :])
                inst = ve.tensor_scalar(s_taken[:, :], s_tkf[:, :], 0.5, None, op0=ALU.is_gt)
                S["ve"].inc(inst)

        @blk.vector
        def _(ve):
            S["ve"].wait(ve)
            ve.memset(s_one256[:, :], 1.0)
            inst = ve.memset(s_t2s[:, :], 0.0)
            S["ve"].inc(inst)

        for e in range(8):
            @blk.vector
            def _(ve, e=e):
                S["ve"].wait(ve)
                S["te"].wait(ve)
                S["vv"].inc(ve.tensor_tensor_scan(
                    out=s_cums[:, :], data0=s_one256[:, :], data1=s_m[:, e, :],
                    initial=0.0, op0=ALU.mult, op1=ALU.add))
                S["vv"].wait(ve)
                inst = ve.tensor_copy(s_rowtot[:, :], s_cums[:, FE - 1:FE])
                S["ve"].inc(inst)

            @blk.tensor
            def _(te):
                S["ve"].wait(te)
                inst = te.matmul(p_pre[:, :], s_triu[:, :], s_rowtot[:, :],
                                 start=True, stop=True)
                S["te"].inc(inst)

            @blk.vector
            def _(ve, e=e):
                S["te"].wait(ve)
                S["vv"].inc(ve.tensor_copy(s_pre[:, :], p_pre[:, :]))
                S["vv"].wait(ve)
                ve.tensor_scalar_add(s_tmp[:, :], s_cums[:, :], float(4096 * e - 1))
                S["vv"].inc(ve.tensor_scalar_add(s_tmp[:, :], s_tmp[:, :], s_pre[:, :]))
                S["vv"].wait(ve)
                ve.tensor_mul(s_tmp[:, :], s_tmp[:, :], s_m[:, e, :])
                inst = ve.tensor_add(s_t2s[:, :], s_t2s[:, :], s_tmp[:, :])
                S["ve"].inc(inst)

        @blk.vector
        def _(ve):
            S["ve"].wait(ve)
            inst = ve.tensor_copy(s_t2si[:, :], s_t2s[:, :])
            S["ve"].inc(inst)

        @blk.sync
        def _(sy):
            S["ve"].wait(sy)
            S["st"].dma(sy.dma_start(
                out=m_out.rearrange("p (e f) -> p e f", e=8), in_=s_m[:, :, :]))
            S["st"].dma(sy.dma_start(out=dbg_o, in_=dbg_sb[:, :]))
            S["st"].dma(sy.dma_start(out=t2s_o, in_=s_t2si[:, :]))

        @blk.gpsimd
        def _(gp):
            S["ve"].wait(gp)
            from concourse.bass_core_rust import IndirectOffsetOnAxis
            for f in range(FE):
                S["st"].dma(gp.indirect_dma_start(
                    out=s2t_o,
                    out_offset=IndirectOffsetOnAxis(ap=s_t2si[:, f:f + 1], axis=0),
                    in_=s_iota_i[:, f:f + 1], in_offset=None))

        @blk.sync
        def _(sy):
            sy.wait_ge(S["st"].h, S["st"].n)
    return nc


def _pair_core(c):
    return c // 2, c % 2


def kernel(**inputs):
    raise NotImplementedError("assembled incrementally; see tests")


N_FFT = 16384
R_F = 128


def fft_host_consts():
    k = np.arange(R_F)
    aa, bb = np.meshgrid(k, k, indexing="ij")
    w128 = np.exp(-2j * np.pi * aa * bb / R_F)
    bgrid, kgrid = np.meshgrid(k, k, indexing="ij")
    tw = np.exp(-2j * np.pi * bgrid * kgrid / N_FFT)
    twc = np.conj(tw) / N_FFT
    w128c = np.conj(w128)

    def bf(x):
        return np.ascontiguousarray(x.astype(np.float32)).astype(BF)

    t4 = np.tile(tw, (1, 4))
    tc4 = np.tile(twc, (1, 4))
    return {
        "f_re": bf(w128.real), "f_im": bf(w128.imag), "f_imN": bf(-w128.imag),
        "fc_re": bf(w128c.real), "fc_im": bf(w128c.imag), "fc_imN": bf(-w128c.imag),
        "t_re": bf(t4.real), "t_im": bf(t4.imag),
        "tc_re": bf(tc4.real), "tc_im": bf(tc4.imag),
    }


def emit_fft4(S, te_c, ve_c, xre, xim, C, ps, sb, out_re, out_im, zero_pad=True):
    blk = S["blk"]
    KA = 64 if zero_pad else 128

    @blk.tensor
    def _(te):
        te_c(te)
        for c in range(4):
            sl = slice(128 * c, 128 * (c + 1))
            te.matmul(ps["pA_re"][:, sl], xre[:, c, :], C["f_re"][0:KA, :], start=True, stop=False)
            te.matmul(ps["pA_re"][:, sl], xim[:, c, :], C["f_imN"][0:KA, :], start=False, stop=True)
            te.matmul(ps["pA_im"][:, sl], xre[:, c, :], C["f_im"][0:KA, :], start=True, stop=False)
            inst = te.matmul(ps["pA_im"][:, sl], xim[:, c, :], C["f_re"][0:KA, :], start=False, stop=True)
        S["te"].inc(inst)

    @blk.vector
    def _(ve):
        S["te"].wait(ve)
        ve.tensor_copy(sb["d_re"][:, :], ps["pA_re"][:, :])
        ve.tensor_copy(sb["d_im"][:, :], ps["pA_im"][:, :])
        ve.tensor_mul(sb["e_re"][:, :], sb["d_re"][:, :], C["t_re"][:, :])
        ve.tensor_mul(sb["scr"][:, :], sb["d_im"][:, :], C["t_im"][:, :])
        ve.tensor_sub(sb["e_re"][:, :], sb["e_re"][:, :], sb["scr"][:, :])
        ve.tensor_mul(sb["e_im"][:, :], sb["d_re"][:, :], C["t_im"][:, :])
        ve.tensor_mul(sb["scr"][:, :], sb["d_im"][:, :], C["t_re"][:, :])
        inst = ve.tensor_add(sb["e_im"][:, :], sb["e_im"][:, :], sb["scr"][:, :])
        S["ve"].inc(inst)

    @blk.tensor
    def _(te):
        S["ve"].wait(te)
        te.matmul(ps["pB_re"][:, :], C["f_re"][:, :], sb["e_re"][:, :], start=True, stop=False)
        te.matmul(ps["pB_re"][:, :], C["f_imN"][:, :], sb["e_im"][:, :], start=False, stop=True)
        te.matmul(ps["pB_im"][:, :], C["f_im"][:, :], sb["e_re"][:, :], start=True, stop=False)
        inst = te.matmul(ps["pB_im"][:, :], C["f_re"][:, :], sb["e_im"][:, :], start=False, stop=True)
        S["te"].inc(inst)

    @blk.vector
    def _(ve):
        S["te"].wait(ve)
        ve.tensor_copy(out_re.rearrange("p c k -> p (c k)"), ps["pB_re"][:, :])
        inst = ve.tensor_copy(out_im.rearrange("p c k -> p (c k)"), ps["pB_im"][:, :])
        S["ve"].inc(inst)


def emit_ifft4(S, pr_re, pr_im, C, ps, sb, y_re, y_im):
    blk = S["blk"]

    @blk.tensor
    def _(te):
        S["ve"].wait(te)
        for c in range(4):
            sl = slice(128 * c, 128 * (c + 1))
            te.matmul(ps["pA_re"][:, sl], pr_re[:, c, :], C["fc_re"][:, :], start=True, stop=False)
            te.matmul(ps["pA_re"][:, sl], pr_im[:, c, :], C["fc_imN"][:, :], start=False, stop=True)
            te.matmul(ps["pA_im"][:, sl], pr_re[:, c, :], C["fc_im"][:, :], start=True, stop=False)
            inst = te.matmul(ps["pA_im"][:, sl], pr_im[:, c, :], C["fc_re"][:, :], start=False, stop=True)
        S["te"].inc(inst)

    @blk.vector
    def _(ve):
        S["te"].wait(ve)
        ve.tensor_copy(sb["d_re"][:, :], ps["pA_re"][:, :])
        ve.tensor_copy(sb["d_im"][:, :], ps["pA_im"][:, :])
        ve.tensor_mul(sb["e_re"][:, :], sb["d_re"][:, :], C["tc_re"][:, :])
        ve.tensor_mul(sb["scr"][:, :], sb["d_im"][:, :], C["tc_im"][:, :])
        ve.tensor_sub(sb["e_re"][:, :], sb["e_re"][:, :], sb["scr"][:, :])
        ve.tensor_mul(sb["e_im"][:, :], sb["d_re"][:, :], C["tc_im"][:, :])
        ve.tensor_mul(sb["scr"][:, :], sb["d_im"][:, :], C["tc_re"][:, :])
        inst = ve.tensor_add(sb["e_im"][:, :], sb["e_im"][:, :], sb["scr"][:, :])
        S["ve"].inc(inst)

    @blk.tensor
    def _(te):
        S["ve"].wait(te)
        te.matmul(ps["pB_re"][:, :], C["fc_re"][:, :], sb["e_re"][:, :], start=True, stop=False)
        te.matmul(ps["pB_re"][:, :], C["fc_imN"][:, :], sb["e_im"][:, :], start=False, stop=True)
        te.matmul(ps["pB_im"][:, :], C["fc_im"][:, :], sb["e_re"][:, :], start=True, stop=False)
        inst = te.matmul(ps["pB_im"][:, :], C["fc_re"][:, :], sb["e_im"][:, :], start=False, stop=True)
        S["te"].inc(inst)

    @blk.vector
    def _(ve):
        S["te"].wait(ve)
        ve.tensor_copy(y_re.rearrange("p c k -> p (c k)"), ps["pB_re"][0:64, :])
        inst = ve.tensor_copy(y_im.rearrange("p c k -> p (c k)"), ps["pB_im"][0:64, :])
        S["ve"].inc(inst)


def emit_cmul4(S, a_re, a_im, b_re, b_im, o_re, o_im, scr):
    blk = S["blk"]

    @blk.vector
    def _(ve):
        S["ve"].wait(ve)
        ar = a_re.rearrange("p c k -> p (c k)"); ai = a_im.rearrange("p c k -> p (c k)")
        br = b_re.rearrange("p c k -> p (c k)"); bi = b_im.rearrange("p c k -> p (c k)")
        orr = o_re.rearrange("p c k -> p (c k)"); oi = o_im.rearrange("p c k -> p (c k)")
        ve.tensor_mul(orr, ar, br)
        ve.tensor_mul(scr[:, :], ai, bi)
        ve.tensor_sub(orr, orr, scr[:, :])
        ve.tensor_mul(oi, ar, bi)
        ve.tensor_mul(scr[:, :], ai, br)
        inst = ve.tensor_add(oi, oi, scr[:, :])
        S["ve"].inc(inst)


def build_launch_E(nt=512):
    F = 4096
    nch = TPC // nt
    nc = bass.Bass("TRN2", target_bir_lowering=False)
    normG = nc.dram_tensor("normG", [8 * TPC, D], BF16, kind="ExternalInput").ap()
    s2t = nc.dram_tensor("s2t", [P, 32], I32, kind="ExternalInput").ap()
    w1e = nc.dram_tensor("w1e", [D, F], F32, kind="ExternalInput").ap()
    b1p = nc.dram_tensor("b1p", [P, 32], F32, kind="ExternalInput").ap()
    w2e = nc.dram_tensor("w2e", [F, D], F32, kind="ExternalInput").ap()
    b2p = nc.dram_tensor("b2p", [P, 8], F32, kind="ExternalInput").ap()
    stag = nc.dram_tensor("stag", [TPC, D], BF16, kind="Internal").ap()
    slab = nc.dram_tensor("slab", [D, TPC], BF16, kind="ExternalOutput").ap()

    from concourse.bass_core_rust import IndirectOffsetOnAxis
    with ExitStack() as ctx:
        E = ctx.enter_context
        w1_sb = E(nc.sbuf_tensor("w1_sb", [P, 8, F], BF16))
        w2_sb = E(nc.sbuf_tensor("w2_sb", [P, 32, D], BF16))
        b1_sb = E(nc.sbuf_tensor("b1_sb", [P, 32], F32))
        b2_sb = E(nc.sbuf_tensor("b2_sb", [P, 8], F32))
        idx_sb = E(nc.sbuf_tensor("idx_sb", [P, 32], I32))
        g_sb = E(nc.sbuf_tensor("g_sb", [P, D], BF16))
        nrm_c = E(nc.sbuf_tensor("nrm_c", [P, 8, nt], BF16))
        h_bf = E(nc.sbuf_tensor("h_bf", [P, 32, nt], BF16))
        o_bf2 = [E(nc.sbuf_tensor("o_bf0", [P, nt], BF16)),
                 E(nc.sbuf_tensor("o_bf1", [P, nt], BF16))]
        p_h2 = [E(nc.psum_tensor("p_h0", [P, nt], F32)),
                E(nc.psum_tensor("p_h1", [P, nt], F32))]
        p_o2 = [E(nc.psum_tensor("p_o0", [P, nt], F32)),
                E(nc.psum_tensor("p_o1", [P, nt], F32))]
        blk = E(nc.Block())
        S = {"blk": blk, "ld": Sem(ctx, nc, "ld"), "st": Sem(ctx, nc, "st"),
             "sc": Sem(ctx, nc, "sc"), "ve": Sem(ctx, nc, "ve"),
             "te": Sem(ctx, nc, "te"), "gi": Sem(ctx, nc, "gi"),
             "tp": Sem(ctx, nc, "tp")}

        @blk.gpsimd
        def _(gp):
            S["ld"].dma(gp.dma_start(
                out=w1_sb[:, :, :], in_=w1e.rearrange("(kt p) f -> p kt f", p=P)))
            S["ld"].dma(gp.dma_start(
                out=w2_sb[:, :, :], in_=w2e.rearrange("(ft p) d -> p ft d", p=P)))

        @blk.sync
        def _(sy):
            S["ld"].dma(sy.dma_start(out=b1_sb[:, :], in_=b1p))
            S["ld"].dma(sy.dma_start(out=b2_sb[:, :], in_=b2p))
            S["ld"].dma(sy.dma_start(out=idx_sb[:, :], in_=s2t))

        for r in range(32):
            @blk.gpsimd
            def _(gp, r=r):
                S["ld"].wait(gp)
                S["st"].wait(gp)
                S["gi"].dma(gp.indirect_dma_start(
                    out=g_sb[:, :], out_offset=None,
                    in_=normG, in_offset=IndirectOffsetOnAxis(ap=idx_sb[:, r:r + 1], axis=0)))

            @blk.sync
            def _(sy, r=r):
                S["gi"].wait(sy)
                S["st"].dma(sy.dma_start(out=stag[r * P:(r + 1) * P, :], in_=g_sb[:, :]))

        for ci in range(nch):
            t0 = ci * nt

            @blk.scalar
            def _(sc, t0=t0):
                S["gi"].wait(sc)
                S["te"].wait(sc)
                S["ve"].wait(sc)
                for dt in range(8):
                    S["tp"].dma(sc.dma_start_transpose(
                        out=nrm_c[:, dt, :], in_=stag[t0:t0 + nt, dt * P:(dt + 1) * P]))

            for ft in range(32):
                @blk.tensor
                def _(te, ft=ft):
                    S["tp"].wait(te)
                    S["sc"].wait(te, lag=1)
                    for kt in range(8):
                        inst = te.matmul(p_h2[ft % 2][:, :], w1_sb[:, kt, ft * P:(ft + 1) * P],
                                         nrm_c[:, kt, :], start=(kt == 0), stop=(kt == 7))
                    S["te"].inc(inst)

                @blk.scalar
                def _(sc, ft=ft):
                    S["te"].wait(sc)
                    inst = sc.activation(h_bf[:, ft, :], p_h2[ft % 2][:, :], AF.Gelu,
                                         bias=b1_sb[:, ft:ft + 1])
                    S["sc"].inc(inst)

            for dt in range(8):
                @blk.tensor
                def _(te, dt=dt):
                    S["sc"].wait(te)
                    S["ve"].wait(te, lag=1)
                    for ft in range(32):
                        inst = te.matmul(p_o2[dt % 2][:, :], w2_sb[:, ft, dt * P:(dt + 1) * P],
                                         h_bf[:, ft, :], start=(ft == 0), stop=(ft == 31))
                    S["te"].inc(inst)

                @blk.vector
                def _(ve, dt=dt):
                    S["te"].wait(ve)
                    inst = ve.tensor_scalar_add(o_bf2[dt % 2][:, :], p_o2[dt % 2][:, :],
                                                b2_sb[:, dt:dt + 1])
                    S["ve"].inc(inst)

                @blk.sync
                def _(sy, dt=dt, t0=t0):
                    S["ve"].wait(sy)
                    S["st"].dma(sy.dma_start(
                        out=slab[dt * P:(dt + 1) * P, t0:t0 + nt], in_=o_bf2[dt % 2][:, :]))

                @blk.vector
                def _(ve):
                    S["st"].wait(ve, lag=16)
                    inst = ve.engine_nop()
                    S["ve"].inc(inst)

        @blk.sync
        def _(sy):
            S["st"].wait(sy)
    return nc


def build_launch_G1(nt=256):
    NCH = 37
    nch = TPC // nt
    nc = bass.Bass("TRN2", target_bir_lowering=False)
    x2 = nc.dram_tensor("x2", [D, TPC], F32, kind="ExternalInput").ap()
    feat = nc.dram_tensor("feat", [D, TPC], F32, kind="ExternalInput").ap()
    ffnG = nc.dram_tensor("ffnG", [8 * TPC, D], BF16, kind="ExternalInput").ap()
    t2s = nc.dram_tensor("t2s", [P, 32], I32, kind="ExternalInput").ap()
    wcatT = nc.dram_tensor("wcatT", [D, NCH * P], F32, kind="ExternalInput").ap()
    bcat = nc.dram_tensor("bcat", [P, NCH], F32, kind="ExternalInput").ap()
    ng = nc.dram_tensor("ng", [P, 8], F32, kind="ExternalInput").ap()
    nb = nc.dram_tensor("nb", [P, 8], F32, kind="ExternalInput").ap()
    stag2 = nc.dram_tensor("stag2", [TPC, D], BF16, kind="Internal").ap()
    x3_o = nc.dram_tensor("x3_o", [D, TPC], F32, kind="ExternalOutput").ap()
    z_o = nc.dram_tensor("z_o", [D, TPC], F32, kind="ExternalOutput").ap()
    u_re_o = nc.dram_tensor("u_re_o", [512, TPC], BF16, kind="ExternalOutput").ap()
    u_im_o = nc.dram_tensor("u_im_o", [512, TPC], BF16, kind="ExternalOutput").ap()
    cp_o = nc.dram_tensor("cp_o", [D, TPC], F32, kind="ExternalOutput").ap()
    gv_o = nc.dram_tensor("gv_o", [D, TPC], F32, kind="ExternalOutput").ap()
    ts_o = nc.dram_tensor("ts_o", [1, TPC], F32, kind="ExternalOutput").ap()
    band_o = nc.dram_tensor("band_o", [4, TPC], F32, kind="ExternalOutput").ap()

    from concourse.bass_core_rust import IndirectOffsetOnAxis
    with ExitStack() as ctx:
        E = ctx.enter_context
        w_sb = E(nc.sbuf_tensor("w_sb", [P, 8, NCH * P], BF16))
        bcat_sb = E(nc.sbuf_tensor("bcat_sb", [P, NCH], F32))
        ng_sb = E(nc.sbuf_tensor("ng_sb", [P, 8], F32))
        nb_sb = E(nc.sbuf_tensor("nb_sb", [P, 8], F32))
        idx_sb = E(nc.sbuf_tensor("idx_sb", [P, 32], I32))
        g_sb = E(nc.sbuf_tensor("g_sb", [P, D], BF16))
        x2c = E(nc.sbuf_tensor("x2c", [P, 8, nt], F32))
        featc = E(nc.sbuf_tensor("featc", [P, 8, nt], F32))
        ffn_c = E(nc.sbuf_tensor("ffn_c", [P, 8, nt], BF16))
        x3c = E(nc.sbuf_tensor("x3c", [P, 8, nt], F32))
        zc = E(nc.sbuf_tensor("zc", [P, 8, nt], F32))
        xsq = E(nc.sbuf_tensor("xsq", [P, 8, nt], F32))
        xnb = E(nc.sbuf_tensor("xnb", [P, 8, nt], BF16))
        sig_cg = E(nc.sbuf_tensor("sig_cg", [P, 8, nt], F32))
        sp_st = E(nc.sbuf_tensor("sp_st", [P, 8, nt], F32))
        fg_st = E(nc.sbuf_tensor("fg_st", [P, 4, nt], F32))
        misc_st = E(nc.sbuf_tensor("misc_st", [P, nt], F32))
        cp_st = E(nc.sbuf_tensor("cp_st", [P, nt], F32))
        gv_st = E(nc.sbuf_tensor("gv_st", [P, nt], F32))
        u_st = E(nc.sbuf_tensor("u_st", [P, nt], BF16))
        ones_col = E(nc.sbuf_tensor("ones_col", [P, 1], F32))
        ones_row = E(nc.sbuf_tensor("ones_row", [1, P], F32))
        eps_ap = E(nc.sbuf_tensor("eps_ap", [1, 1], F32))
        murow = E(nc.sbuf_tensor("murow", [1, nt], F32))
        qrow = E(nc.sbuf_tensor("qrow", [1, nt], F32))
        mu2 = E(nc.sbuf_tensor("mu2", [1, nt], F32))
        varrow = E(nc.sbuf_tensor("varrow", [1, nt], F32))
        rstdrow = E(nc.sbuf_tensor("rstdrow", [1, nt], F32))
        mu_b = E(nc.sbuf_tensor("mu_b", [P, nt], F32))
        rstd_b = E(nc.sbuf_tensor("rstd_b", [P, nt], F32))
        p_sum = E(nc.psum_tensor("p_sum", [1, nt], F32))
        p_sq = E(nc.psum_tensor("p_sq", [1, nt], F32))
        p_mu = E(nc.psum_tensor("p_mu", [P, nt], F32))
        p_rstd = E(nc.psum_tensor("p_rstd", [P, nt], F32))
        p_pr2 = [E(nc.psum_tensor("p_pr0", [P, nt], F32)),
                 E(nc.psum_tensor("p_pr1", [P, nt], F32))]
        blk = E(nc.Block())
        S = {"blk": blk, "ld": Sem(ctx, nc, "ld"), "st": Sem(ctx, nc, "st"),
             "sc": Sem(ctx, nc, "sc"), "ve": Sem(ctx, nc, "ve"),
             "te": Sem(ctx, nc, "te"), "gi": Sem(ctx, nc, "gi"),
             "tp": Sem(ctx, nc, "tp")}
        pr_guard = [(), ()]
        cp_guard = [0]
        gv_guard = [0]

        @blk.gpsimd
        def _(gp):
            S["ld"].dma(gp.dma_start(
                out=w_sb[:, :, :], in_=wcatT.rearrange("(kt p) c -> p kt c", p=P)))

        @blk.sync
        def _(sy):
            S["ld"].dma(sy.dma_start(out=bcat_sb[:, :], in_=bcat))
            S["ld"].dma(sy.dma_start(out=ng_sb[:, :], in_=ng))
            S["ld"].dma(sy.dma_start(out=nb_sb[:, :], in_=nb))
            S["ld"].dma(sy.dma_start(out=idx_sb[:, :], in_=t2s))

        @blk.vector
        def _(ve):
            ve.memset(ones_col[:, :], 1.0)
            ve.memset(ones_row[:, :], 1.0)
            inst = ve.memset(eps_ap[:, :], LN_EPS)
            S["ve"].inc(inst)

        for r in range(32):
            @blk.gpsimd
            def _(gp, r=r):
                S["ld"].wait(gp)
                S["st"].wait(gp)
                S["gi"].dma(gp.indirect_dma_start(
                    out=g_sb[:, :], out_offset=None,
                    in_=ffnG, in_offset=IndirectOffsetOnAxis(ap=idx_sb[:, r:r + 1], axis=0)))

            @blk.sync
            def _(sy, r=r):
                S["gi"].wait(sy)
                S["st"].dma(sy.dma_start(out=stag2[r * P:(r + 1) * P, :], in_=g_sb[:, :]))

        for ci in range(nch):
            t0 = ci * nt

            @blk.sync
            def _(sy, t0=t0):
                S["st"].wait(sy)
                S["ve"].wait(sy)
                S["sc"].wait(sy)
                S["te"].wait(sy)
                S["ld"].dma(sy.dma_start(out=x2c[:, :, :], in_=fm_ap(x2, D, t0, nt)))
                S["ld"].dma(sy.dma_start(out=featc[:, :, :], in_=fm_ap(feat, D, t0, nt)))

            @blk.scalar
            def _(sc, t0=t0):
                S["gi"].wait(sc)
                S["ve"].wait(sc)
                for dt in range(8):
                    S["tp"].dma(sc.dma_start_transpose(
                        out=ffn_c[:, dt, :], in_=stag2[t0:t0 + nt, dt * P:(dt + 1) * P]))

            @blk.vector
            def _(ve):
                S["ld"].wait(ve)
                S["tp"].wait(ve)
                S["st"].wait(ve)
                for kt in range(8):
                    ve.tensor_add(x3c[:, kt, :], x2c[:, kt, :], featc[:, kt, :])
                    inst = ve.tensor_add(x3c[:, kt, :], x3c[:, kt, :], ffn_c[:, kt, :])
                S["ve"].inc(inst)

            emit_ln(nc, S, x3c, 8, nt, ones_col, ones_row, p_sum, p_sq, p_mu, p_rstd,
                    xsq, murow, qrow, mu2, varrow, rstdrow, eps_ap, mu_b, rstd_b,
                    gates=("ve",))

            @blk.vector
            def _(ve):
                S["ve"].wait(ve)
                for kt in range(8):
                    ve.tensor_sub(xsq[:, kt, :], x3c[:, kt, :], mu_b[:, :])
                    inst = ve.tensor_mul(zc[:, kt, :], xsq[:, kt, :], rstd_b[:, :])
                S["ve"].inc(inst)

            emit_ln(nc, S, zc, 8, nt, ones_col, ones_row, p_sum, p_sq, p_mu, p_rstd,
                    xsq, murow, qrow, mu2, varrow, rstdrow, eps_ap, mu_b, rstd_b,
                    gates=("ve",))

            @blk.vector
            def _(ve):
                S["ve"].wait(ve)
                for kt in range(8):
                    ve.tensor_sub(xsq[:, kt, :], zc[:, kt, :], mu_b[:, :])
                    ve.tensor_mul(xsq[:, kt, :], xsq[:, kt, :], rstd_b[:, :])
                    inst = ve.tensor_scalar(
                        xnb[:, kt, :], xsq[:, kt, :],
                        ng_sb[:, kt:kt + 1], nb_sb[:, kt:kt + 1],
                        op0=ALU.mult, op1=ALU.add)
                S["ve"].inc(inst)

            order = list(range(8, 16)) + list(range(0, 8)) + list(range(16, NCH))
            xnb_ready = S["ve"].n
            sc_ready = S["sc"].n
            for idx, ct in enumerate(order):
                @blk.tensor
                def _(te, ct=ct, idx=idx):
                    S["ve"].wait_at(te, xnb_ready)
                    S["sc"].wait_at(te, sc_ready)
                    for g0 in pr_guard[idx % 2]:
                        S[g0[0]].wait_at(te, g0[1])
                    for kt in range(8):
                        inst = te.matmul(p_pr2[idx % 2][:, :], w_sb[:, kt, ct * P:(ct + 1) * P],
                                         xnb[:, kt, :], start=(kt == 0), stop=(kt == 7))
                    S["te"].inc(inst)

                if 8 <= ct < 16:
                    @blk.scalar
                    def _(sc, ct=ct, idx=idx):
                        S["te"].wait(sc)
                        inst = sc.activation(sig_cg[:, ct - 8, :], p_pr2[idx % 2][:, :], AF.Sigmoid,
                                             bias=bcat_sb[:, ct:ct + 1])
                        S["sc"].inc(inst)
                    pr_guard[idx % 2] = (("sc", S["sc"].n),)
                elif ct < 8:
                    @blk.vector
                    def _(ve, ct=ct, t0=t0, idx=idx):
                        S["te"].wait(ve)
                        S["st"].wait_at(ve, cp_guard[0])
                        inst = ve.scalar_tensor_tensor(
                            out=cp_st[:, :], in0=p_pr2[idx % 2][:, :], scalar=bcat_sb[:, ct:ct + 1],
                            in1=sig_cg[:, ct, :], op0=ALU.add, op1=ALU.mult)
                        S["ve"].inc(inst)
                    pr_guard[idx % 2] = (("ve", S["ve"].n),)

                    @blk.sync
                    def _(sy, ct=ct, t0=t0):
                        S["ve"].wait(sy)
                        S["st"].dma(sy.dma_start(
                            out=cp_o[ct * P:(ct + 1) * P, t0:t0 + nt], in_=cp_st[:, :]))
                    cp_guard[0] = S["st"].n
                elif 16 <= ct < 24:
                    @blk.vector
                    def _(ve, ct=ct, idx=idx):
                        S["te"].wait(ve)
                        inst = ve.tensor_copy(sp_st[:, ct - 16, :], p_pr2[idx % 2][:, :])
                        S["ve"].inc(inst)
                    pr_guard[idx % 2] = (("ve", S["ve"].n),)
                elif 24 <= ct < 32:
                    @blk.scalar
                    def _(sc, ct=ct, idx=idx):
                        S["te"].wait(sc)
                        S["st"].wait_at(sc, gv_guard[0])
                        inst = sc.activation(gv_st[:, :], p_pr2[idx % 2][:, :], AF.Sigmoid,
                                             bias=bcat_sb[:, ct:ct + 1])
                        S["sc"].inc(inst)
                    pr_guard[idx % 2] = (("sc", S["sc"].n),)

                    @blk.sync
                    def _(sy, ct=ct, t0=t0):
                        S["sc"].wait(sy)
                        S["st"].dma(sy.dma_start(
                            out=gv_o[(ct - 24) * P:(ct - 23) * P, t0:t0 + nt], in_=gv_st[:, :]))
                    gv_guard[0] = S["st"].n
                elif 32 <= ct < 36:
                    @blk.scalar
                    def _(sc, ct=ct, idx=idx):
                        S["te"].wait(sc)
                        inst = sc.activation(fg_st[:, ct - 32, :], p_pr2[idx % 2][:, :], AF.Sigmoid,
                                             bias=bcat_sb[:, ct:ct + 1])
                        S["sc"].inc(inst)
                    pr_guard[idx % 2] = (("sc", S["sc"].n),)
                else:
                    @blk.scalar
                    def _(sc, ct=ct, idx=idx):
                        S["te"].wait(sc)
                        inst = sc.activation(misc_st[:, :], p_pr2[idx % 2][:, :], AF.Sigmoid,
                                             bias=bcat_sb[:, ct:ct + 1])
                        S["sc"].inc(inst)
                    pr_guard[idx % 2] = (("sc", S["sc"].n),)

            for j in range(4):
                @blk.vector
                def _(ve, j=j, t0=t0):
                    S["ve"].wait(ve)
                    S["sc"].wait(ve)
                    S["st"].wait(ve)
                    inst = ve.tensor_mul(u_st[:, :], sp_st[:, j, :], fg_st[:, j, :])
                    S["ve"].inc(inst)

                @blk.sync
                def _(sy, j=j, t0=t0):
                    S["ve"].wait(sy)
                    S["st"].dma(sy.dma_start(
                        out=u_re_o[j * P:(j + 1) * P, t0:t0 + nt], in_=u_st[:, :]))

                @blk.vector
                def _(ve, j=j):
                    S["st"].wait(ve)
                    inst = ve.tensor_mul(u_st[:, :], sp_st[:, 4 + j, :], fg_st[:, j, :])
                    S["ve"].inc(inst)

                @blk.sync
                def _(sy, j=j, t0=t0):
                    S["ve"].wait(sy)
                    S["st"].dma(sy.dma_start(
                        out=u_im_o[j * P:(j + 1) * P, t0:t0 + nt], in_=u_st[:, :]))

            @blk.sync
            def _(sy, t0=t0):
                S["ve"].wait(sy)
                S["sc"].wait(sy)
                S["st"].dma(sy.dma_start(out=fm_ap(x3_o, D, t0, nt), in_=x3c[:, :, :]))
                S["st"].dma(sy.dma_start(out=fm_ap(z_o, D, t0, nt), in_=zc[:, :, :]))
                S["st"].dma(sy.dma_start(out=band_o[:, t0:t0 + nt], in_=misc_st[0:4, :]))
                S["st"].dma(sy.dma_start(out=ts_o[:, t0:t0 + nt], in_=misc_st[4:5, :]))

        @blk.sync
        def _(sy):
            S["st"].wait(sy)
    return nc


def build_launch_F(nt=512):
    CH = 256
    NJ = CH // P
    nch = SEQ // nt
    PI = float(np.pi)
    nc = bass.Bass("TRN2", target_bir_lowering=False)
    u_re = nc.dram_tensor("u_re", [CH, SEQ], BF16, kind="ExternalInput").ap()
    u_im = nc.dram_tensor("u_im", [CH, SEQ], BF16, kind="ExternalInput").ap()
    tsrow = nc.dram_tensor("tsrow", [1, SEQ], F32, kind="ExternalInput").ap()
    bandr = nc.dram_tensor("bandr", [1, 2 * SEQ], F32, kind="ExternalInput").ap()
    ld_loc = nc.dram_tensor("ld_loc", [1, CH], F32, kind="ExternalInput").ap()
    fr_loc = nc.dram_tensor("fr_loc", [1, CH], F32, kind="ExternalInput").ap()
    dt_loc = nc.dram_tensor("dt_loc", [1, CH], F32, kind="ExternalInput").ap()
    fi_roll = nc.dram_tensor("fi_roll", [1, 512], F32, kind="ExternalInput").ap()
    iota1 = nc.dram_tensor("iota1", [1, SEQ], F32, kind="ExternalInput").ap()
    cst = {n: nc.dram_tensor(n, [128, 512 if n.startswith("t") else 128], BF16,
                             kind="ExternalInput").ap()
           for n in ["f_re", "f_im", "f_imN", "fc_re", "fc_im", "fc_imN",
                     "t_re", "t_im", "tc_re", "tc_im"]}
    dec_d0 = nc.dram_tensor("dec_d0", [P, 1], F32, kind="Internal").ap()
    dec_d1 = nc.dram_tensor("dec_d1", [P, 1], F32, kind="Internal").ap()
    k_re_d = nc.dram_tensor("k_re_d", [CH, SEQ], BF16, kind="Internal").ap()
    k_im_d = nc.dram_tensor("k_im_d", [CH, SEQ], BF16, kind="Internal").ap()
    y_re_o = nc.dram_tensor("y_re_o", [CH, SEQ], BF16, kind="ExternalOutput").ap()
    y_im_o = nc.dram_tensor("y_im_o", [CH, SEQ], BF16, kind="ExternalOutput").ap()

    with ExitStack() as ctx:
        E = ctx.enter_context
        C = {n: E(nc.sbuf_tensor("s_" + n, [128, 512 if n.startswith("t") else 128], BF16))
             for n in cst}
        ldr = E(nc.sbuf_tensor("ldr", [1, CH], F32))
        frr = E(nc.sbuf_tensor("frr", [1, CH], F32))
        dtr = E(nc.sbuf_tensor("dtr", [1, CH], F32))
        fir = E(nc.sbuf_tensor("fir", [1, 512], F32))
        fie = E(nc.sbuf_tensor("fie", [1, 512], F32))
        smx = E(nc.sbuf_tensor("smx", [1, 4], F32))
        decr = E(nc.sbuf_tensor("decr", [1, CH], F32))
        omr = E(nc.sbuf_tensor("omr", [1, CH], F32))
        ts_c = E(nc.sbuf_tensor("ts_c", [1, 512], F32))
        bnd_c = E(nc.sbuf_tensor("bnd_c", [1, 512], F32))
        iot_c = E(nc.sbuf_tensor("iot_c", [1, 512], F32))
        dec_col = E(nc.sbuf_tensor("dec_col", [P, NJ], F32))
        amag = E(nc.sbuf_tensor("amag", [P, SEQ], F32))
        rscan = E(nc.sbuf_tensor("rscan", [P, SEQ + 1], F32))
        ones_row = E(nc.sbuf_tensor("ones_row", [1, P], F32))
        ones_cl = E(nc.sbuf_tensor("ones_cl", [P, 1], F32))
        phs = E(nc.sbuf_tensor("phs", [P, nt], F32))
        ki = E(nc.sbuf_tensor("ki", [P, nt], I32))
        kf = E(nc.sbuf_tensor("kf", [P, nt], F32))
        cu8 = E(nc.sbuf_tensor("cu8", [P, nt], mybir.dt.uint8))
        sinv = E(nc.sbuf_tensor("sinv", [P, nt], F32))
        cosv = E(nc.sbuf_tensor("cosv", [P, nt], F32))
        ktmp = E(nc.sbuf_tensor("ktmp", [P, nt], F32))
        kre_st = E(nc.sbuf_tensor("kre_st", [P, nt], BF16))
        kim_st = E(nc.sbuf_tensor("kim_st", [P, nt], BF16))
        xre = E(nc.sbuf_tensor("xre", [64, 4, 128], BF16))
        xim = E(nc.sbuf_tensor("xim", [64, 4, 128], BF16))
        kre = E(nc.sbuf_tensor("kre", [64, 4, 128], BF16))
        kim = E(nc.sbuf_tensor("kim", [64, 4, 128], BF16))
        su_re = E(nc.sbuf_tensor("su_re", [128, 4, 128], BF16))
        su_im = E(nc.sbuf_tensor("su_im", [128, 4, 128], BF16))
        sk_re = E(nc.sbuf_tensor("sk_re", [128, 4, 128], BF16))
        sk_im = E(nc.sbuf_tensor("sk_im", [128, 4, 128], BF16))
        pr_re = E(nc.sbuf_tensor("pr_re", [128, 4, 128], BF16))
        pr_im = E(nc.sbuf_tensor("pr_im", [128, 4, 128], BF16))
        yt_re = E(nc.sbuf_tensor("yt_re", [64, 4, 128], BF16))
        yt_im = E(nc.sbuf_tensor("yt_im", [64, 4, 128], BF16))
        sb = {n: E(nc.sbuf_tensor(n, [128, 512], BF16))
              for n in ["d_re", "d_im", "e_re", "e_im", "scr"]}
        ps = {n: E(nc.psum_tensor(n, [128, 512], F32))
              for n in ["pA_re", "pA_im", "pB_re", "pB_im"]}
        p_b = E(nc.psum_tensor("p_b", [P, nt], F32))
        p_t = E(nc.psum_tensor("p_t", [P, nt], F32))
        p_ph = E(nc.psum_tensor("p_ph", [P, nt], F32))
        blk = E(nc.Block())
        S = {"blk": blk, "ld": Sem(ctx, nc, "ld"), "st": Sem(ctx, nc, "st"),
             "sc": Sem(ctx, nc, "sc"), "ve": Sem(ctx, nc, "ve"),
             "te": Sem(ctx, nc, "te"), "vv": Sem(ctx, nc, "vv")}

        @blk.sync
        def _(sy):
            for n, ap in cst.items():
                S["ld"].dma(sy.dma_start(out=C[n][:, :], in_=ap))
            for t, ap in ((ldr, ld_loc), (frr, fr_loc), (dtr, dt_loc), (fir, fi_roll)):
                S["ld"].dma(sy.dma_start(out=t[:, :], in_=ap))

        @blk.vector
        def _(ve):
            ve.memset(ones_row[:, :], 1.0)
            ve.memset(ones_cl[:, :], 1.0)
            ve.memset(rscan[:, 0:1], 1.0)
            S["ld"].wait(ve)
            inst = ve.tensor_reduce(smx[0:1, 0:1], fir[:, :], mybir.AxisListType.X, ALU.max)
            S["ve"].inc(inst)

        @blk.vector
        def _(ve):
            S["ve"].wait(ve)
            S["vv"].inc(ve.tensor_scalar_mul(smx[0:1, 1:2], smx[0:1, 0:1], -1.0))
            S["vv"].wait(ve)
            inst = ve.tensor_scalar_add(fie[:, :], fir[:, :], smx[0:1, 1:2])
            S["ve"].inc(inst)

        @blk.scalar
        def _(sc):
            S["ve"].wait(sc)
            inst = sc.activation(fie[:, :], fie[:, :], AF.Exp)
            S["sc"].inc(inst)

        @blk.vector
        def _(ve):
            S["sc"].wait(ve)
            S["vv"].inc(ve.tensor_reduce(smx[0:1, 2:3], fie[:, :], mybir.AxisListType.X, ALU.add))
            S["vv"].wait(ve)
            S["vv"].inc(ve.reciprocal(smx[0:1, 3:4], smx[0:1, 2:3]))
            S["vv"].wait(ve)
            S["vv"].inc(ve.tensor_scalar_mul(decr[:, :], fie[0:1, 0:CH], smx[0:1, 3:4]))
            S["vv"].wait(ve)
            inst = ve.tensor_scalar_mul(decr[:, :], decr[:, :], 512.0)
            S["ve"].inc(inst)

        @blk.scalar
        def _(sc):
            S["ve"].wait(sc)
            sc.activation(ldr[:, :], ldr[:, :], AF.Sigmoid)
            inst = sc.activation(dtr[:, :], dtr[:, :], AF.Sigmoid)
            S["sc"].inc(inst)

        @blk.vector
        def _(ve):
            S["sc"].wait(ve)
            ve.tensor_mul(decr[:, :], decr[:, :], ldr[:, :])
            S["vv"].inc(ve.tensor_mul(omr[:, :], frr[:, :], dtr[:, :]))
            S["vv"].wait(ve)
            inst = ve.tensor_scalar_mul(omr[:, :], omr[:, :], 0.2)
            S["ve"].inc(inst)

        @blk.sync
        def _(sy):
            S["ve"].wait(sy)
            S["st"].dma(sy.dma_start(out=dec_d0, in_=decr[0:1, 0:P]))
            S["st"].dma(sy.dma_start(out=dec_d1, in_=decr[0:1, P:2 * P]))

        @blk.sync
        def _(sy):
            S["st"].wait(sy)
            S["ld"].dma(sy.dma_start(out=dec_col[:, 0:1], in_=dec_d0))
            S["ld"].dma(sy.dma_start(out=dec_col[:, 1:2], in_=dec_d1))

        for j in range(NJ):
            @blk.vector
            def _(ve, j=j):
                S["ld"].wait(ve)
                S["te"].wait(ve)
                inst = ve.engine_nop()
                S["ve"].inc(inst)

            for ci in range(nch):
                t0 = ci * nt

                @blk.sync
                def _(sy, j=j, t0=t0):
                    S["te"].wait(sy)
                    S["ld"].dma(sy.dma_start(out=bnd_c[:, 0:nt],
                                             in_=bandr[0:1, j * SEQ + t0:j * SEQ + t0 + nt]))

                @blk.tensor
                def _(te, j=j, t0=t0):
                    S["ve"].wait(te)
                    S["ld"].wait(te)
                    inst = te.matmul(p_b[:, :], ones_row[:, :], bnd_c[0:1, 0:nt],
                                     start=True, stop=True)
                    S["te"].inc(inst)

                @blk.vector
                def _(ve, j=j, t0=t0):
                    S["te"].wait(ve)
                    inst = ve.tensor_single_scalar(amag[:, t0:t0 + nt], p_b[:, :],
                                                   dec_col[:, j:j + 1], op=ALU.mult)
                    S["ve"].inc(inst)

            @blk.vector
            def _(ve, j=j):
                S["ve"].wait(ve)
                inst = ve.tensor_tensor_scan(
                    out=rscan[:, 1:SEQ + 1], data0=amag[:, :],
                    data1=ones_cl[:, 0:1].to_broadcast([P, SEQ]),
                    initial=1.0, op0=ALU.mult, op1=ALU.mult)
                S["ve"].inc(inst)

            for ci in range(nch):
                t0 = ci * nt

                @blk.sync
                def _(sy, t0=t0):
                    S["te"].wait(sy)
                    S["ld"].dma(sy.dma_start(out=iot_c[:, 0:nt], in_=iota1[0:1, t0:t0 + nt]))
                    S["ld"].dma(sy.dma_start(out=ts_c[:, 0:nt], in_=tsrow[0:1, t0:t0 + nt]))

                @blk.tensor
                def _(te, j=j, t0=t0):
                    S["ve"].wait(te)
                    S["sc"].wait(te)
                    S["ld"].wait(te)
                    te.matmul(p_ph[:, :], omr[0:1, j * P:(j + 1) * P], iot_c[0:1, 0:nt],
                              start=True, stop=True)
                    inst = te.matmul(p_t[:, :], ones_row[:, :], ts_c[0:1, 0:nt],
                                     start=True, stop=True)
                    S["te"].inc(inst)

                @blk.vector
                def _(ve, t0=t0):
                    S["te"].wait(ve)
                    S["vv"].inc(ve.tensor_single_scalar(kf[:, :], p_ph[:, :], 1.0 / (2 * PI), op=ALU.mult))
                    S["vv"].wait(ve)
                    S["vv"].inc(ve.tensor_copy(ki[:, :], kf[:, :]))
                    S["vv"].wait(ve)
                    S["vv"].inc(ve.tensor_copy(kf[:, :], ki[:, :]))
                    S["vv"].wait(ve)
                    S["vv"].inc(ve.scalar_tensor_tensor(out=phs[:, :], in0=kf[:, :],
                                                        scalar=-2 * PI, in1=p_ph[:, :],
                                                        op0=ALU.mult, op1=ALU.add))
                    S["vv"].wait(ve)
                    S["vv"].inc(ve.tensor_scalar(cu8[:, :], phs[:, :], PI, None, op0=ALU.is_gt))
                    S["vv"].wait(ve)
                    S["vv"].inc(ve.scalar_tensor_tensor(out=phs[:, :], in0=cu8[:, :],
                                                        scalar=-2 * PI, in1=phs[:, :],
                                                        op0=ALU.mult, op1=ALU.add))
                    S["vv"].wait(ve)
                    S["vv"].inc(ve.tensor_scalar(cu8[:, :], phs[:, :], -PI, None, op0=ALU.is_lt))
                    S["vv"].wait(ve)
                    inst = ve.scalar_tensor_tensor(out=phs[:, :], in0=cu8[:, :],
                                                   scalar=2 * PI, in1=phs[:, :],
                                                   op0=ALU.mult, op1=ALU.add)
                    S["ve"].inc(inst)

                @blk.scalar
                def _(sc):
                    S["ve"].wait(sc)
                    inst = sc.activation(sinv[:, :], phs[:, :], AF.Sin)
                    S["sc"].inc(inst)

                @blk.vector
                def _(ve):
                    S["sc"].wait(ve)
                    S["vv"].inc(ve.tensor_scalar_add(phs[:, :], phs[:, :], PI / 2))
                    S["vv"].wait(ve)
                    S["vv"].inc(ve.tensor_scalar(cu8[:, :], phs[:, :], PI, None, op0=ALU.is_gt))
                    S["vv"].wait(ve)
                    inst = ve.scalar_tensor_tensor(out=phs[:, :], in0=cu8[:, :],
                                                   scalar=-2 * PI, in1=phs[:, :],
                                                   op0=ALU.mult, op1=ALU.add)
                    S["ve"].inc(inst)

                @blk.scalar
                def _(sc):
                    S["ve"].wait(sc)
                    inst = sc.activation(cosv[:, :], phs[:, :], AF.Sin)
                    S["sc"].inc(inst)

                @blk.vector
                def _(ve, t0=t0):
                    S["sc"].wait(ve)
                    S["st"].wait(ve)
                    S["vv"].inc(ve.tensor_mul(ktmp[:, :], rscan[:, t0:t0 + nt], p_t[:, :]))
                    S["vv"].wait(ve)
                    ve.tensor_mul(kre_st[:, :], ktmp[:, :], cosv[:, :])
                    inst = ve.tensor_mul(kim_st[:, :], ktmp[:, :], sinv[:, :])
                    S["ve"].inc(inst)

                @blk.sync
                def _(sy, j=j, t0=t0):
                    S["ve"].wait(sy)
                    S["st"].dma(sy.dma_start(
                        out=k_re_d[j * P:(j + 1) * P, t0:t0 + nt], in_=kre_st[:, :]))
                    S["st"].dma(sy.dma_start(
                        out=k_im_d[j * P:(j + 1) * P, t0:t0 + nt], in_=kim_st[:, :]))

        for g in range(CH // 4):
            c0 = g * 4

            @blk.gpsimd
            def _(gp, c0=c0):
                S["ve"].wait(gp)
                S["st"].wait(gp)
                S["ld"].dma(gp.dma_start(
                    out=xre[:, :, :], in_=u_re[c0:c0 + 4, :].rearrange("c (a b) -> a c b", b=128)))
                S["ld"].dma(gp.dma_start(
                    out=xim[:, :, :], in_=u_im[c0:c0 + 4, :].rearrange("c (a b) -> a c b", b=128)))
                S["ld"].dma(gp.dma_start(
                    out=kre[:, :, :], in_=k_re_d[c0:c0 + 4, :].rearrange("c (a b) -> a c b", b=128)))
                S["ld"].dma(gp.dma_start(
                    out=kim[:, :, :], in_=k_im_d[c0:c0 + 4, :].rearrange("c (a b) -> a c b", b=128)))

            te_gate = (lambda te: (S["ld"].wait(te), S["ve"].wait(te)))
            emit_fft4(S, te_gate, None, xre, xim, C, ps, sb, su_re, su_im)
            emit_fft4(S, te_gate, None, kre, kim, C, ps, sb, sk_re, sk_im)
            emit_cmul4(S, su_re, su_im, sk_re, sk_im, pr_re, pr_im, sb["scr"])
            emit_ifft4(S, pr_re, pr_im, C, ps, sb, yt_re, yt_im)

            @blk.sync
            def _(sy, c0=c0):
                S["ve"].wait(sy)
                S["st"].dma(sy.dma_start(
                    out=y_re_o[c0:c0 + 4, :].rearrange("c (a b) -> a c b", b=128),
                    in_=yt_re[:, :, :]))
                S["st"].dma(sy.dma_start(
                    out=y_im_o[c0:c0 + 4, :].rearrange("c (a b) -> a c b", b=128),
                    in_=yt_im[:, :, :]))

        @blk.sync
        def _(sy):
            S["st"].wait(sy)
    return nc


def build_launch_G2a(nt=512):
    nch = TPC // nt
    nc = bass.Bass("TRN2", target_bir_lowering=False)
    z_in = nc.dram_tensor("z_in", [D, TPC], F32, kind="ExternalInput").ap()
    spec = nc.dram_tensor("spec", [D, TPC], BF16, kind="ExternalInput").ap()
    cp = nc.dram_tensor("cp", [D, TPC], F32, kind="ExternalInput").ap()
    gv = nc.dram_tensor("gv", [D, TPC], F32, kind="ExternalInput").ap()
    wfsT = nc.dram_tensor("wfsT", [D, D], F32, kind="ExternalInput").ap()
    osc = nc.dram_tensor("osc", [1, 1], F32, kind="ExternalInput").ap()
    x4_o = nc.dram_tensor("x4_o", [D, TPC], F32, kind="ExternalOutput").ap()

    with ExitStack() as ctx:
        E = ctx.enter_context
        wfs_sb = E(nc.sbuf_tensor("wfs_sb", [P, 8, D], BF16))
        osc_sb = E(nc.sbuf_tensor("osc_sb", [1, 1], F32))
        osc_col = E(nc.sbuf_tensor("osc_col", [P, 1], F32))
        ones_row = E(nc.sbuf_tensor("ones_row", [1, P], F32))
        specc = E(nc.sbuf_tensor("specc", [P, 8, nt], BF16))
        zc = E(nc.sbuf_tensor("zc", [P, 8, nt], F32))
        cpc = E(nc.sbuf_tensor("cpc", [P, 8, nt], F32))
        gvc = E(nc.sbuf_tensor("gvc", [P, 8, nt], F32))
        x4c = E(nc.sbuf_tensor("x4c", [P, 8, nt], F32))
        scr = E(nc.sbuf_tensor("scr", [P, nt], F32))
        p_y = E(nc.psum_tensor("p_y", [P, nt], F32))
        p_o = E(nc.psum_tensor("p_o", [P, 1], F32))
        blk = E(nc.Block())
        S = {"blk": blk, "ld": Sem(ctx, nc, "ld"), "st": Sem(ctx, nc, "st"),
             "sc": Sem(ctx, nc, "sc"), "ve": Sem(ctx, nc, "ve"),
             "te": Sem(ctx, nc, "te")}

        @blk.gpsimd
        def _(gp):
            S["ld"].dma(gp.dma_start(
                out=wfs_sb[:, :, :], in_=wfsT.rearrange("(kt p) d -> p kt d", p=P)))

        @blk.sync
        def _(sy):
            S["ld"].dma(sy.dma_start(out=osc_sb[:, :], in_=osc))

        @blk.vector
        def _(ve):
            inst = ve.memset(ones_row[:, :], 1.0)
            S["ve"].inc(inst)

        @blk.tensor
        def _(te):
            S["ld"].wait(te)
            S["ve"].wait(te)
            inst = te.matmul(p_o[:, :], ones_row[:, :], osc_sb[:, :], start=True, stop=True)
            S["te"].inc(inst)

        @blk.vector
        def _(ve):
            S["te"].wait(ve)
            inst = ve.tensor_copy(osc_col[:, :], p_o[:, :])
            S["ve"].inc(inst)

        for ci in range(nch):
            t0 = ci * nt

            @blk.sync
            def _(sy, t0=t0):
                S["st"].wait(sy)
                S["ve"].wait(sy)
                S["te"].wait(sy)
                S["ld"].dma(sy.dma_start(out=zc[:, :, :], in_=fm_ap(z_in, D, t0, nt)))
                S["ld"].dma(sy.dma_start(out=cpc[:, :, :], in_=fm_ap(cp, D, t0, nt)))
                S["ld"].dma(sy.dma_start(out=gvc[:, :, :], in_=fm_ap(gv, D, t0, nt)))

            @blk.gpsimd
            def _(gp, t0=t0):
                S["ve"].wait(gp)
                S["te"].wait(gp)
                S["ld"].dma(gp.dma_start(out=specc[:, :, :], in_=fm_ap(spec, D, t0, nt)))

            for dt in range(8):
                @blk.tensor
                def _(te, dt=dt):
                    S["ld"].wait(te)
                    S["ve"].wait(te)
                    for kt in range(8):
                        inst = te.matmul(p_y[:, :], wfs_sb[:, kt, dt * P:(dt + 1) * P],
                                         specc[:, kt, :], start=(kt == 0), stop=(kt == 7))
                    S["te"].inc(inst)

                @blk.vector
                def _(ve, dt=dt):
                    S["te"].wait(ve)
                    ve.tensor_single_scalar(scr[:, :], p_y[:, :], osc_col[:, 0:1], op=ALU.mult)
                    ve.tensor_mul(scr[:, :], scr[:, :], gvc[:, dt, :])
                    ve.tensor_add(scr[:, :], scr[:, :], cpc[:, dt, :])
                    inst = ve.scalar_tensor_tensor(
                        out=x4c[:, dt, :], in0=scr[:, :], scalar=0.5,
                        in1=zc[:, dt, :], op0=ALU.mult, op1=ALU.add)
                    S["ve"].inc(inst)

            @blk.sync
            def _(sy, t0=t0):
                S["ve"].wait(sy)
                S["st"].dma(sy.dma_start(out=fm_ap(x4_o, D, t0, nt), in_=x4c[:, :, :]))

        @blk.sync
        def _(sy):
            S["st"].wait(sy)
    return nc


def build_launch_G2f(nt=256, w_dtype=F32):
    F = 4096
    nch = TPC // nt
    nc = bass.Bass("TRN2", target_bir_lowering=False)
    x3 = nc.dram_tensor("x3", [D, TPC], F32, kind="ExternalInput").ap()
    x4 = nc.dram_tensor("x4", [D, TPC], F32, kind="ExternalInput").ap()
    n2g = nc.dram_tensor("n2g", [P, 8], F32, kind="ExternalInput").ap()
    n2b = nc.dram_tensor("n2b", [P, 8], F32, kind="ExternalInput").ap()
    w1T = nc.dram_tensor("w1T", [D, F], w_dtype, kind="ExternalInput").ap()
    b1p = nc.dram_tensor("b1p", [P, 32], F32, kind="ExternalInput").ap()
    w2T = nc.dram_tensor("w2T", [F, D], w_dtype, kind="ExternalInput").ap()
    b2p = nc.dram_tensor("b2p", [P, 8], F32, kind="ExternalInput").ap()
    x5_o = nc.dram_tensor("x5_o", [D, TPC], F32, kind="ExternalOutput").ap()

    with ExitStack() as ctx:
        E = ctx.enter_context
        w1_sb = E(nc.sbuf_tensor("w1_sb", [P, 8, F], BF16))
        w2_sb = E(nc.sbuf_tensor("w2_sb", [P, 32, D], BF16))
        b1_sb = E(nc.sbuf_tensor("b1_sb", [P, 32], F32))
        b2_sb = E(nc.sbuf_tensor("b2_sb", [P, 8], F32))
        n2g_sb = E(nc.sbuf_tensor("n2g_sb", [P, 8], F32))
        n2b_sb = E(nc.sbuf_tensor("n2b_sb", [P, 8], F32))
        x3c = E(nc.sbuf_tensor("x3c", [P, 8, nt], F32))
        x4c = E(nc.sbuf_tensor("x4c", [P, 8, nt], F32))
        xsq = E(nc.sbuf_tensor("xsq", [P, 8, nt], F32))
        xnb = E(nc.sbuf_tensor("xnb", [P, 8, nt], BF16))
        h_bf = E(nc.sbuf_tensor("h_bf", [P, 32, nt], BF16))
        x5c = E(nc.sbuf_tensor("x5c", [P, 8, nt], F32))
        ones_col = E(nc.sbuf_tensor("ones_col", [P, 1], F32))
        ones_row = E(nc.sbuf_tensor("ones_row", [1, P], F32))
        eps_ap = E(nc.sbuf_tensor("eps_ap", [1, 1], F32))
        murow = E(nc.sbuf_tensor("murow", [1, nt], F32))
        qrow = E(nc.sbuf_tensor("qrow", [1, nt], F32))
        mu2 = E(nc.sbuf_tensor("mu2", [1, nt], F32))
        varrow = E(nc.sbuf_tensor("varrow", [1, nt], F32))
        rstdrow = E(nc.sbuf_tensor("rstdrow", [1, nt], F32))
        mu_b = E(nc.sbuf_tensor("mu_b", [P, nt], F32))
        rstd_b = E(nc.sbuf_tensor("rstd_b", [P, nt], F32))
        p_sum = E(nc.psum_tensor("p_sum", [1, nt], F32))
        p_sq = E(nc.psum_tensor("p_sq", [1, nt], F32))
        p_mu = E(nc.psum_tensor("p_mu", [P, nt], F32))
        p_rstd = E(nc.psum_tensor("p_rstd", [P, nt], F32))
        p_h2 = [E(nc.psum_tensor("p_h0", [P, nt], F32)),
                E(nc.psum_tensor("p_h1", [P, nt], F32))]
        p_o2 = [E(nc.psum_tensor("p_o0", [P, nt], F32)),
                E(nc.psum_tensor("p_o1", [P, nt], F32))]
        blk = E(nc.Block())
        S = {"blk": blk, "ld": Sem(ctx, nc, "ld"), "st": Sem(ctx, nc, "st"),
             "sc": Sem(ctx, nc, "sc"), "ve": Sem(ctx, nc, "ve"),
             "te": Sem(ctx, nc, "te")}

        @blk.gpsimd
        def _(gp):
            S["ld"].dma(gp.dma_start(
                out=w1_sb[:, :, :], in_=w1T.rearrange("(kt p) f -> p kt f", p=P)))
            S["ld"].dma(gp.dma_start(
                out=w2_sb[:, :, :], in_=w2T.rearrange("(ft p) d -> p ft d", p=P)))

        @blk.sync
        def _(sy):
            S["ld"].dma(sy.dma_start(out=b1_sb[:, :], in_=b1p))
            S["ld"].dma(sy.dma_start(out=b2_sb[:, :], in_=b2p))
            S["ld"].dma(sy.dma_start(out=n2g_sb[:, :], in_=n2g))
            S["ld"].dma(sy.dma_start(out=n2b_sb[:, :], in_=n2b))

        @blk.vector
        def _(ve):
            ve.memset(ones_col[:, :], 1.0)
            ve.memset(ones_row[:, :], 1.0)
            inst = ve.memset(eps_ap[:, :], LN_EPS)
            S["ve"].inc(inst)

        for ci in range(nch):
            t0 = ci * nt

            @blk.sync
            def _(sy, t0=t0):
                S["st"].wait(sy)
                S["ve"].wait(sy)
                S["sc"].wait(sy)
                S["te"].wait(sy)
                S["ld"].dma(sy.dma_start(out=x3c[:, :, :], in_=fm_ap(x3, D, t0, nt)))
                S["ld"].dma(sy.dma_start(out=x4c[:, :, :], in_=fm_ap(x4, D, t0, nt)))

            emit_ln(nc, S, x4c, 8, nt, ones_col, ones_row, p_sum, p_sq, p_mu, p_rstd,
                    xsq, murow, qrow, mu2, varrow, rstdrow, eps_ap, mu_b, rstd_b,
                    gates=("ld",))

            @blk.vector
            def _(ve):
                S["ve"].wait(ve)
                for kt in range(8):
                    ve.tensor_sub(xsq[:, kt, :], x4c[:, kt, :], mu_b[:, :])
                    ve.tensor_mul(xsq[:, kt, :], xsq[:, kt, :], rstd_b[:, :])
                    inst = ve.tensor_scalar(
                        xnb[:, kt, :], xsq[:, kt, :],
                        n2g_sb[:, kt:kt + 1], n2b_sb[:, kt:kt + 1],
                        op0=ALU.mult, op1=ALU.add)
                S["ve"].inc(inst)

            for ft in range(32):
                @blk.tensor
                def _(te, ft=ft):
                    S["ve"].wait(te)
                    S["sc"].wait(te, lag=1)
                    for kt in range(8):
                        inst = te.matmul(p_h2[ft % 2][:, :], w1_sb[:, kt, ft * P:(ft + 1) * P],
                                         xnb[:, kt, :], start=(kt == 0), stop=(kt == 7))
                    S["te"].inc(inst)

                @blk.scalar
                def _(sc, ft=ft):
                    S["te"].wait(sc)
                    inst = sc.activation(h_bf[:, ft, :], p_h2[ft % 2][:, :], AF.Gelu,
                                         bias=b1_sb[:, ft:ft + 1])
                    S["sc"].inc(inst)

            for dt in range(8):
                @blk.tensor
                def _(te, dt=dt):
                    S["sc"].wait(te)
                    S["ve"].wait(te, lag=1)
                    for ft in range(32):
                        inst = te.matmul(p_o2[dt % 2][:, :], w2_sb[:, ft, dt * P:(dt + 1) * P],
                                         h_bf[:, ft, :], start=(ft == 0), stop=(ft == 31))
                    S["te"].inc(inst)

                @blk.vector
                def _(ve, dt=dt):
                    S["te"].wait(ve)
                    ve.scalar_tensor_tensor(
                        out=x5c[:, dt, :], in0=p_o2[dt % 2][:, :], scalar=b2_sb[:, dt:dt + 1],
                        in1=x4c[:, dt, :], op0=ALU.add, op1=ALU.add)
                    inst = ve.tensor_add(x5c[:, dt, :], x5c[:, dt, :], x3c[:, dt, :])
                    S["ve"].inc(inst)

            @blk.sync
            def _(sy, t0=t0):
                S["ve"].wait(sy)
                S["st"].dma(sy.dma_start(out=fm_ap(x5_o, D, t0, nt), in_=x5c[:, :, :]))

        @blk.sync
        def _(sy):
            S["st"].wait(sy)
    return nc


def build_launch_BIN():
    F = 4096
    nc = bass.Bass("TRN2", target_bir_lowering=False)
    fw1T = nc.dram_tensor("fw1T", [D, F], F32, kind="ExternalInput").ap()
    fw2T = nc.dram_tensor("fw2T", [F, D], F32, kind="ExternalInput").ap()
    bw1T_o = nc.dram_tensor("bw1T_o", [D, F], BF16, kind="ExternalOutput").ap()
    bw2T_o = nc.dram_tensor("bw2T_o", [F, D], BF16, kind="ExternalOutput").ap()

    with ExitStack() as ctx:
        E = ctx.enter_context
        wtile = E(nc.sbuf_tensor("wtile", [P, F], F32))
        scr = E(nc.sbuf_tensor("scr", [P, F], F32))
        alpha = E(nc.sbuf_tensor("alpha", [1, F], F32))
        alpha_b = E(nc.sbuf_tensor("alpha_b", [P, F], F32))
        bw_st = E(nc.sbuf_tensor("bw_st", [P, F], BF16))
        ones_col = E(nc.sbuf_tensor("ones_col", [P, 1], F32))
        ones_row = E(nc.sbuf_tensor("ones_row", [1, P], F32))
        p_al = E(nc.psum_tensor("p_al", [1, 512], F32))
        p_alb = E(nc.psum_tensor("p_alb", [P, 512], F32))
        blk = E(nc.Block())
        S = {"blk": blk, "ld": Sem(ctx, nc, "ld"), "st": Sem(ctx, nc, "st"),
             "sc": Sem(ctx, nc, "sc"), "ve": Sem(ctx, nc, "ve"),
             "te": Sem(ctx, nc, "te")}

        @blk.vector
        def _(ve):
            ve.memset(ones_col[:, :], 1.0)
            inst = ve.memset(ones_row[:, :], 1.0)
            S["ve"].inc(inst)

        def binarize(wT_ap, bw_ap, nkt, nfree, dtot):
            for f0 in range(0, nfree, 512):
                for kt in range(nkt):
                    @blk.sync
                    def _(sy, kt=kt, f0=f0):
                        S["sc"].wait(sy)
                        S["te"].wait(sy)
                        S["ld"].dma(sy.dma_start(
                            out=wtile[:, 0:512],
                            in_=wT_ap.rearrange("(kt p) f -> p kt f", p=P)[:, kt, f0:f0 + 512]))

                    @blk.scalar
                    def _(sc, kt=kt):
                        S["ld"].wait(sc)
                        inst = sc.activation(scr[:, 0:512], wtile[:, 0:512], AF.Abs)
                        S["sc"].inc(inst)

                    @blk.tensor
                    def _(te, kt=kt, f0=f0):
                        S["sc"].wait(te)
                        inst = te.matmul(p_al[:, 0:512], ones_col[:, :], scr[:, 0:512],
                                         start=(kt == 0), stop=(kt == nkt - 1))
                        S["te"].inc(inst)

                @blk.scalar
                def _(sc, f0=f0):
                    S["te"].wait(sc)
                    inst = sc.mul(alpha[:, f0:f0 + 512], p_al[:, 0:512], 1.0 / dtot)
                    S["sc"].inc(inst)

            for f0 in range(0, nfree, 512):
                @blk.tensor
                def _(te, f0=f0):
                    S["sc"].wait(te)
                    S["ve"].wait(te)
                    inst = te.matmul(p_alb[:, 0:512], ones_row[:, :], alpha[0:1, f0:f0 + 512],
                                     start=True, stop=True)
                    S["te"].inc(inst)

                @blk.vector
                def _(ve, f0=f0):
                    S["te"].wait(ve)
                    inst = ve.tensor_copy(alpha_b[:, f0:f0 + 512], p_alb[:, 0:512])
                    S["ve"].inc(inst)

            for kt in range(nkt):
                @blk.sync
                def _(sy, kt=kt):
                    S["ve"].wait(sy)
                    S["sc"].wait(sy)
                    S["st"].wait(sy)
                    S["ld"].dma(sy.dma_start(
                        out=wtile[:, 0:nfree],
                        in_=wT_ap.rearrange("(kt p) f -> p kt f", p=P)[:, kt, :]))

                @blk.scalar
                def _(sc, kt=kt):
                    S["ld"].wait(sc)
                    inst = sc.activation(scr[:, 0:nfree], wtile[:, 0:nfree], AF.Sign)
                    S["sc"].inc(inst)

                @blk.vector
                def _(ve, kt=kt):
                    S["sc"].wait(ve)
                    inst = ve.tensor_mul(bw_st[:, 0:nfree], scr[:, 0:nfree],
                                         alpha_b[:, 0:nfree])
                    S["ve"].inc(inst)

                @blk.sync
                def _(sy, kt=kt):
                    S["ve"].wait(sy)
                    S["st"].dma(sy.dma_start(
                        out=bw_ap.rearrange("(kt p) f -> p kt f", p=P)[:, kt, :],
                        in_=bw_st[:, 0:nfree]))

        binarize(fw1T, bw1T_o, 8, 4096, D)
        binarize(fw2T, bw2T_o, 32, D, 4096)

        @blk.sync
        def _(sy):
            S["st"].wait(sy)
    return nc


def _t(a):
    return np.ascontiguousarray(a.T)


def _pp(v, ntile):
    return np.ascontiguousarray(v.reshape(ntile, 128).T)


def kernel(**ins):
    cores = list(range(8))
    x = ins["L0_attn_x"]
    x_fm = [_t(x[b]) for b in range(4)]

    ncA = build_launch_A()
    whgT = _t(ins["L0_attn_W_hg"])
    bhgp = _pp(ins["L0_attn_b_hg"], 24)
    im = []
    for c in cores:
        b, h = c // 2, c % 2
        im.append({"x": np.ascontiguousarray(x_fm[b][:, h * TPC:(h + 1) * TPC]),
                   "whgT": whgT, "bhg": bhgp})
    rA = run_bass_kernel_spmd(ncA, im, core_ids=cores).results

    a_full = [np.concatenate([rA[2 * b]["a_out"], rA[2 * b + 1]["a_out"]], 1) for b in range(4)]
    b_full = [np.concatenate([rA[2 * b]["b_out"], rA[2 * b + 1]["b_out"]], 1) for b in range(4)]

    ncB = build_launch_B()
    h0 = ins["L0_attn_h0"]
    im = []
    for c in cores:
        b, h = c // 2, c % 2
        chs = slice(h * 768, (h + 1) * 768)
        im.append({"a_in": np.ascontiguousarray(a_full[b][chs]),
                   "b_in": np.ascontiguousarray(b_full[b][chs]),
                   "h0": np.ascontiguousarray(h0[chs].reshape(6, 128).T)})
    rB = run_bass_kernel_spmd(ncB, im, core_ids=cores).results
    h_full = [np.concatenate([rB[2 * b]["h_out"], rB[2 * b + 1]["h_out"]], 0) for b in range(4)]

    ncC = build_launch_C()
    woutT = _t(ins["L0_attn_W_out"])
    im = []
    for c in cores:
        b, h = c // 2, c % 2
        ts = slice(h * TPC, (h + 1) * TPC)
        im.append({"h_in": np.ascontiguousarray(h_full[b][:, ts]),
                   "x": np.ascontiguousarray(x_fm[b][:, ts]),
                   "woutT": woutT,
                   "bout": _pp(ins["L0_attn_b_out"], 8),
                   "lng": _pp(ins["L0_ffn_ln_g"], 8),
                   "lnb": _pp(ins["L0_ffn_ln_b"], 8),
                   "centT": _t(ins["L0_ffn_centroids"])})
    rC = run_bass_kernel_spmd(ncC, im, core_ids=cores).results

    scores = np.concatenate([rC[c]["sc_o"] for c in cores], axis=1)

    ncD = build_launch_D()
    pgrid, fgrid = np.meshgrid(np.arange(128), np.arange(256), indexing="ij")
    iota = (pgrid * 256 + fgrid)
    im = [{"sc_in": np.ascontiguousarray(scores),
           "triu": np.ascontiguousarray(np.triu(np.ones((128, 128), np.float32), 1)),
           "iota_f": iota.astype(np.float32),
           "iota_i": iota.astype(np.int32),
           "ident": np.eye(128, dtype=np.float32)}] * 8
    rD = run_bass_kernel_spmd(ncD, im, core_ids=cores).results
    t2s_flat = rD[0]["t2s_o"].reshape(-1)
    s2t_flat = rD[0]["s2t_o"].reshape(-1)

    normG = np.concatenate([_t(rC[c]["normbf_o"]) for c in cores], axis=0)
    ncE = build_launch_E()
    w1 = ins["L0_ffn_w1"]; b1 = ins["L0_ffn_b1"]
    w2 = ins["L0_ffn_w2"]; b2 = ins["L0_ffn_b2"]
    im = []
    for c in cores:
        sl = s2t_flat[c * TPC:(c + 1) * TPC]
        im.append({"normG": normG,
                   "s2t": np.ascontiguousarray(sl.reshape(32, 128).T.astype(np.int32)),
                   "w1e": np.ascontiguousarray(w1[c]),
                   "b1p": _pp(b1[c], 32),
                   "w2e": np.ascontiguousarray(w2[c]),
                   "b2p": _pp(b2[c], 8)})
    rE = run_bass_kernel_spmd(ncE, im, core_ids=cores).results

    ffnG = np.concatenate([_t(rE[c]["slab"]) for c in cores], axis=0)
    ncG1 = build_launch_G1()
    wcatT = np.concatenate([
        _t(ins["L1_attn_Wconcept"]), _t(ins["L1_attn_Wcg"]), _t(ins["L1_attn_Wsp"]),
        _t(ins["L1_attn_Wgate"]), _t(ins["L1_attn_Wfg"]), _t(ins["L1_attn_Wband"]),
        _t(ins["L1_attn_Wtg"]), np.zeros((1024, 123), np.float32)], axis=1)
    bcatv = np.concatenate([
        ins["L1_attn_bconcept"], ins["L1_attn_bcg"], np.zeros(1024, np.float32),
        ins["L1_attn_bgate"], ins["L1_attn_bfg"], ins["L1_attn_bband"],
        ins["L1_attn_btg"], np.zeros(123, np.float32)])
    bcatp = _pp(bcatv, 37)
    im = []
    for c in cores:
        tl = t2s_flat[c * TPC:(c + 1) * TPC]
        im.append({"x2": np.ascontiguousarray(rC[c]["x2_o"]),
                   "feat": np.ascontiguousarray(rC[c]["feat_o"]),
                   "ffnG": ffnG,
                   "t2s": np.ascontiguousarray(tl.reshape(32, 128).T.astype(np.int32)),
                   "wcatT": np.ascontiguousarray(wcatT),
                   "bcat": bcatp,
                   "ng": _pp(ins["L1_attn_norm_g"], 8),
                   "nb": _pp(ins["L1_attn_norm_b"], 8)})
    rG1 = run_bass_kernel_spmd(ncG1, im, core_ids=cores).results

    u_re_full = [np.concatenate([rG1[2 * b]["u_re_o"], rG1[2 * b + 1]["u_re_o"]], 1) for b in range(4)]
    u_im_full = [np.concatenate([rG1[2 * b]["u_im_o"], rG1[2 * b + 1]["u_im_o"]], 1) for b in range(4)]
    ts_full = [np.concatenate([rG1[2 * b]["ts_o"], rG1[2 * b + 1]["ts_o"]], 1) for b in range(4)]
    band_full = [np.concatenate([rG1[2 * b]["band_o"], rG1[2 * b + 1]["band_o"]], 1) for b in range(4)]
    ncF = build_launch_F()
    consts = fft_host_consts()
    ld_ = ins["L1_attn_log_decay"]; fi_ = ins["L1_attn_freq_importance"]
    fr_ = ins["L1_attn_frequencies"]; dt_ = ins["L1_attn_dt"]
    iota1 = np.arange(1, SEQ + 1, dtype=np.float32).reshape(1, SEQ)
    im = []
    for c in cores:
        b, fh = c // 2, c % 2
        fs = slice(256 * fh, 256 * (fh + 1))
        d = {"u_re": np.ascontiguousarray(u_re_full[b][fs]),
             "u_im": np.ascontiguousarray(u_im_full[b][fs]),
             "tsrow": np.ascontiguousarray(ts_full[b]),
             "bandr": np.ascontiguousarray(band_full[b][2 * fh:2 * fh + 2].reshape(1, -1)),
             "ld_loc": np.ascontiguousarray(ld_[fs].reshape(1, 256)),
             "fr_loc": np.ascontiguousarray(fr_[fs].reshape(1, 256)),
             "dt_loc": np.ascontiguousarray(dt_[fs].reshape(1, 256)),
             "fi_roll": np.ascontiguousarray(np.roll(fi_, -256 * fh).reshape(1, 512)),
             "iota1": iota1}
        d.update(consts)
        im.append(d)
    rF = run_bass_kernel_spmd(ncF, im, core_ids=cores).results

    y_re_full = [np.concatenate([rF[2 * b]["y_re_o"], rF[2 * b + 1]["y_re_o"]], 0) for b in range(4)]
    y_im_full = [np.concatenate([rF[2 * b]["y_im_o"], rF[2 * b + 1]["y_im_o"]], 0) for b in range(4)]
    ncG2a = build_launch_G2a()
    im = []
    for c in cores:
        b, h = c // 2, c % 2
        ts = slice(h * TPC, (h + 1) * TPC)
        spec = np.concatenate([y_re_full[b][:, ts], y_im_full[b][:, ts]], axis=0)
        im.append({"z_in": np.ascontiguousarray(rG1[c]["z_o"]),
                   "spec": np.ascontiguousarray(spec),
                   "cp": np.ascontiguousarray(rG1[c]["cp_o"]),
                   "gv": np.ascontiguousarray(rG1[c]["gv_o"]),
                   "wfsT": _t(ins["L1_attn_Wfs"]),
                   "osc": np.ascontiguousarray(ins["L1_attn_output_scale"].reshape(1, 1))})
    rG2a = run_bass_kernel_spmd(ncG2a, im, core_ids=cores).results

    ncG2f = build_launch_G2f()
    im = []
    for c in cores:
        im.append({"x3": np.ascontiguousarray(rG1[c]["x3_o"]),
                   "x4": np.ascontiguousarray(rG2a[c]["x4_o"]),
                   "n2g": _pp(ins["L1_attn_norm2_g"], 8),
                   "n2b": _pp(ins["L1_attn_norm2_b"], 8),
                   "w1T": _t(ins["L1_attn_W1"]),
                   "b1p": _pp(ins["L1_attn_b1"], 32),
                   "w2T": _t(ins["L1_attn_W2"]),
                   "b2p": _pp(ins["L1_attn_b2"], 8)})
    rG2f = run_bass_kernel_spmd(ncG2f, im, core_ids=cores).results

    ncBIN = build_launch_BIN()
    im = [{"fw1T": _t(ins["L1_ffn_w1"]), "fw2T": _t(ins["L1_ffn_w2"])}] * 8
    rBIN = run_bass_kernel_spmd(ncBIN, im, core_ids=cores).results
    bw1T = rBIN[0]["bw1T_o"]
    bw2T = rBIN[0]["bw2T_o"]

    ncFFN = build_launch_G2f(w_dtype=BF16)
    zeros_fm = np.zeros((D, TPC), np.float32)
    ones8 = np.ones((P, 8), np.float32)
    zeros8 = np.zeros((P, 8), np.float32)
    im = []
    for c in cores:
        im.append({"x3": zeros_fm,
                   "x4": np.ascontiguousarray(rG2f[c]["x5_o"]),
                   "n2g": ones8, "n2b": zeros8,
                   "w1T": bw1T,
                   "b1p": _pp(ins["L1_ffn_b1"], 32),
                   "w2T": bw2T,
                   "b2p": _pp(ins["L1_ffn_b2"], 8)})
    rFFN = run_bass_kernel_spmd(ncFFN, im, core_ids=cores).results

    out = np.empty((4, SEQ, D), np.float32)
    for c in cores:
        b, h = c // 2, c % 2
        out[b, h * TPC:(h + 1) * TPC, :] = np.asarray(rFFN[c]["x5_o"], np.float32).T
    return out
